# version 18
# speedup vs baseline: 1.0386x; 1.0012x over previous
import numpy as np
import ml_dtypes
from contextlib import ExitStack

import concourse.bass as bass
import concourse.mybir as mybir
from concourse.alu_op_type import AluOpType
from concourse.bass_utils import run_bass_kernel_spmd

F32 = mybir.dt.float32
BF16 = mybir.dt.bfloat16
AF = mybir.ActivationFunctionType
OP = AluOpType
bfdt = ml_dtypes.bfloat16

D = 1024
NCH = 8
H, HD, LR, NE = 16, 64, 64, 8
SUPPV = 15.5
CL30 = float(np.exp(np.float32(-30.0)))
EPS_LN = 1e-5
EPS_RMS = 1e-6
EPS_WKV = 1e-9


DEBUG_TAPS = False


class _FakeInst:
    __slots__ = ("ti",)

    def __init__(self):
        self.ti = None

    def then_inc(self, sem, v):
        self.ti = (sem, v)
        return self


class _EngRecorder:
    def __init__(self, lst):
        self._lst = lst

    def __getattr__(self, name):
        lst = self._lst

        def call(*a, **k):
            fi = _FakeInst()
            lst.append((name, a, k, fi))
            return fi
        return call


class _BlockRecorder:

    def __init__(self):
        self.q = {e: [] for e in ("sync", "scalar", "vector", "tensor", "gpsimd")}

    def _mk(self, eng):
        def add(fn):
            fn(_EngRecorder(self.q[eng]))
        return add

    def __getattr__(self, name):
        if name in ("sync", "scalar", "vector", "tensor", "gpsimd"):
            return self._mk(name)
        raise AttributeError(name)


def _replay(engine, lst):
    for name, a, k, fi in lst:
        r = getattr(engine, name)(*a, **k)
        if fi.ti is not None:
            r.then_inc(*fi.ti)


def build_nc(nseq, tseq):
    ntok = nseq * tseq
    FT = min(512, tseq)
    BT = min(256, tseq)
    BTc = BT + 1
    HT = FT // 4
    NG = HT // 8
    NMB = NG // 8
    assert NG % 8 == 0

    nc = bass.Bass("TRN2", target_bir_lowering=False, debug=False)

    dram = {}
    def din(name, shape, dt=F32):
        dram[name] = nc.dram_tensor(name, shape, dt, kind="ExternalInput").ap()
        return dram[name]

    x_in = din("x_fm", [D, ntok])
    for w in ["W_k", "W_v", "W_r", "W_o"]:
        din("at_" + w, [D, D])
    for v in ["tmk", "tmv", "tmr", "w", "u"]:
        din("at_" + v, [128, NCH])
    for v in ["g1", "g2", "bq", "bo"]:
        din("f_" + v, [128, NCH])
    din("f_Wq", [D, D]); din("f_Wdown", [D, LR]); din("f_Wup", [LR, 2 * D])
    din("f_Wo", [D, D]); din("f_Wg", [D, NE]); din("f_We", [NE * D, D])
    din("f_bdown", [LR, 1]); din("f_bup", [128, 2 * NCH]); din("f_bg", [NE, 1])
    din("c_ones", [128, 128], BF16)
    din("c_id64", [128, 64], BF16)
    din("c_esel", [NE, NE * 128], BF16)
    din("c_mask", [128, 512], BF16)
    out_fm = nc.dram_tensor("out_fm", [D, ntok], F32, kind="ExternalOutput").ap()
    x1_dram = nc.dram_tensor("x1_scr", [D, ntok], F32, kind="Internal").ap()
    tap_o2 = (nc.dram_tensor("tap_o2", [128, 4096], BF16, kind="ExternalOutput").ap()
              if DEBUG_TAPS else None)
    tap_xp = (nc.dram_tensor("tap_xp", [128, 4096], F32, kind="ExternalOutput").ap()
              if DEBUG_TAPS else None)
    tap_hq = (nc.dram_tensor("tap_hq", [128, 4096], BF16, kind="ExternalOutput").ap()
              if DEBUG_TAPS else None)
    tap_kv = (nc.dram_tensor("tap_kv", [128, 8192], BF16, kind="ExternalOutput").ap()
              if DEBUG_TAPS else None)
    tap_sm = (nc.dram_tensor("tap_sm", [128, 4096], BF16, kind="ExternalOutput").ap()
              if DEBUG_TAPS else None)
    tap_qt = (nc.dram_tensor("tap_qt", [128, 3 * 2048], BF16, kind="ExternalOutput").ap()
              if DEBUG_TAPS else None)
    x2_dram = nc.dram_tensor("x2_scr", [D, ntok], F32, kind="Internal").ap()
    we_dram = nc.dram_tensor("we_scr", [NE, 128, NCH * D], BF16, kind="Internal").ap()

    cnt = {}
    tot = {}
    def bump(s, by):
        tot[s] = tot.get(s, 0) + by
        return tot[s]
    def snap(name, s):
        cnt[name] = tot.get(s, 0)
    def at(name):
        return cnt.get(name, 0)

    ctx = ExitStack()
    with ctx:
        sem = {}
        for s in ["qld", "qsc", "qsv", "qst", "qx", "qwa", "qwb", "pe", "ve", "ac", "gp",
                  "qsc1", "qsv1", "qsd0", "qsd1", "qtt0", "qtt1", "qx1", "qx2"]:
            sem[s] = ctx.enter_context(nc.semaphore(s))

        def inc(eng, s):
            eng.drain()
            eng.sem_inc(sem[s], 1)
            bump(s, 1)

        vecs = {}
        for name in ["tmk", "tmv", "tmr", "w", "u", "g1", "g2", "bq", "bo"]:
            vecs[name] = ctx.enter_context(nc.sbuf_tensor("v_" + name, [128, NCH], F32))
        v_bdown = ctx.enter_context(nc.sbuf_tensor("v_bdown", [LR, 1], F32))
        v_bup = ctx.enter_context(nc.sbuf_tensor("v_bup", [128, 2 * NCH], F32))
        v_bg = ctx.enter_context(nc.sbuf_tensor("v_bg", [NE, 1], F32))
        ones_b = ctx.enter_context(nc.sbuf_tensor("ones_b", [128, 128], BF16))
        cb = ctx.enter_context(nc.sbuf_tensor("cbias", [128, 4], F32))
        id64_b = ctx.enter_context(nc.sbuf_tensor("id64_b", [128, 64], BF16))
        mask_b = ctx.enter_context(nc.sbuf_tensor("mask_b", [128, 512], BF16))
        esel_b = ctx.enter_context(nc.sbuf_tensor("esel_b", [NE, NE * 128], BF16))

        WPOOL = ctx.enter_context(nc.sbuf_tensor("wpool", [128, 32768], BF16))
        SPOOL = ctx.enter_context(nc.sbuf_tensor("spool", [128, 6144], F32))
        FPOOL = ctx.enter_context(nc.sbuf_tensor("fpool", [128, 14464], F32))
        BPOOL = ctx.enter_context(nc.sbuf_tensor("bpool", [128, 24576], BF16))
        PS = [ctx.enter_context(nc.psum_tensor("ps%d" % i, [128, 512], F32))
              for i in range(8)]

        block = _BlockRecorder()

        def cap(pool, base, nch, width):
            return pool[:, base:base + nch * width].rearrange(
                "p (c t) -> p c t", c=nch, t=width)

        WQ_O, WO_O, WD_O, WU_O, WG_O, WE_O = 0, 8192, 16384, 16896, 18944, 19456

        BXB, BSQ, BKV, BQF = 0, 4096, 8192, 16384
        BO2 = BQF
        BXEA = BKV
        BXEB = BKV + 4096
        BQT, BKVT = 20480, 22528
        BGC = BQF
        SMALLB = ctx.enter_context(nc.sbuf_tensor("smallb", [128, 4096], BF16))

        AXK, AXV, AXR, ABY, AXB2, ASQ2 = 0, 2048, 4096, 6144, 8192, 10240

        FXB, FT1, FLN1 = 0, 4096, 8192
        FXP = FT1
        FXH = FLN1
        FAX, FALN, FAP, FAQ = 0, 2048, 4104, 6160
        FAK, FAV, FAR, FSC2 = 8216, 10264, 12312, 14360

        SS_SUM, SS_SQ, SS_MU, SS_RSTD, SS_C1, SS_T, SS_R2 = (
            0, 512, 1024, 1536, 2048, 2560, 3072)
        SLS, SRS, SGRB = 3584, 4608, 5632
        AS_SUM, AS_SQ, AS_MU, AS_RSTD, AS_T = 0, 256, 512, 768, 1024
        ASNV = 1280
        ASC2 = 3328
        ALNC = 5376

        def prep():
            def _vc(vector):
                vector.memset(cb[:, 0:1], EPS_LN)
                vector.memset(cb[:, 1:2], EPS_RMS)
                vector.memset(cb[:, 2:3], 1.0)
                vector.memset(cb[:, 3:4], EPS_WKV)
                inc(vector, "ve"); snap("p_cb", "ve")
            block.vector(_vc)
            def _sy(sync):
                for name in ["tmk", "tmv", "tmr", "w", "u"]:
                    sync.dma_start(out=vecs[name][:], in_=dram["at_" + name]
                                   ).then_inc(sem["qld"], 16); bump("qld", 16)
                for name in ["g1", "g2", "bq", "bo"]:
                    sync.dma_start(out=vecs[name][:], in_=dram["f_" + name]
                                   ).then_inc(sem["qld"], 16); bump("qld", 16)
                sync.dma_start(out=v_bdown[:], in_=dram["f_bdown"]).then_inc(sem["qld"], 16); bump("qld", 16)
                sync.dma_start(out=v_bup[:], in_=dram["f_bup"]).then_inc(sem["qld"], 16); bump("qld", 16)
                sync.dma_start(out=v_bg[:], in_=dram["f_bg"]).then_inc(sem["qld"], 16); bump("qld", 16)
                sync.dma_start(out=ones_b[:], in_=dram["c_ones"]).then_inc(sem["qld"], 16); bump("qld", 16)
                sync.dma_start(out=id64_b[:], in_=dram["c_id64"]).then_inc(sem["qld"], 16); bump("qld", 16)
                sync.dma_start(out=mask_b[:], in_=dram["c_mask"]).then_inc(sem["qld"], 16); bump("qld", 16)
                sync.dma_start(out=esel_b[:], in_=dram["c_esel"]).then_inc(sem["qld"], 16); bump("qld", 16)
                snap("p_vecs", "qld")
            block.sync(_sy)

            for e in range(NE):
                def _sy(sync, e=e):
                    if e > 0:
                        sync.wait_ge(sem["ve"], at("p_wecast%d" % (e - 1)))
                    for c in range(NCH):
                        sync.dma_start(out=FPOOL[:, c * D:(c + 1) * D],
                                       in_=dram["f_We"][e * D + c * 128:e * D + (c + 1) * 128, :]
                                       ).then_inc(sem["qld"], 16); bump("qld", 16)
                    snap("p_weld%d" % e, "qld")
                block.sync(_sy)
                def _ve(vector, e=e):
                    vector.wait_ge(sem["qld"], at("p_weld%d" % e))
                    if e > 0:
                        vector.wait_ge(sem["qst"], at("p_west%d" % (e - 1)))
                    for c in range(NCH):
                        vector.tensor_scalar(BPOOL[:, c * D:(c + 1) * D],
                                             FPOOL[:, c * D:(c + 1) * D],
                                             vecs["g2"][:, c:c + 1], None, OP.mult)
                    inc(vector, "ve"); snap("p_wecast%d" % e, "ve")
                block.vector(_ve)
                def _sy2(sync, e=e):
                    sync.wait_ge(sem["ve"], at("p_wecast%d" % e))
                    sync.dma_start(out=we_dram[e], in_=BPOOL[:, 0:NCH * D]
                                   ).then_inc(sem["qst"], 16); bump("qst", 16)
                    snap("p_west%d" % e, "qst")
                block.sync(_sy2)

            def wprep(dname, dst_off, ncols, scale, nrow, tag):
                def _sy(sync):
                    sync.wait_ge(sem["ve"], tot.get("ve", 0))
                    nch = (nrow + 127) // 128
                    for c in range(nch):
                        r0, r1 = c * 128, min((c + 1) * 128, nrow)
                        sync.dma_start(out=FPOOL[0:r1 - r0, c * ncols:(c + 1) * ncols],
                                       in_=dram[dname][r0:r1, :]
                                       ).then_inc(sem["qld"], 16); bump("qld", 16)
                    snap("p_%sld" % tag, "qld")
                block.sync(_sy)
                def _ve(vector):
                    vector.wait_ge(sem["qld"], at("p_%sld" % tag))
                    nch = (nrow + 127) // 128
                    for c in range(nch):
                        r0, r1 = c * 128, min((c + 1) * 128, nrow)
                        dst = WPOOL[0:r1 - r0, dst_off + c * ncols: dst_off + (c + 1) * ncols]
                        src = FPOOL[0:r1 - r0, c * ncols:(c + 1) * ncols]
                        if scale is None:
                            vector.tensor_copy(dst, src)
                        else:
                            vector.tensor_scalar(dst, src, scale[:, c:c + 1], None, OP.mult)
                    inc(vector, "ve"); snap("p_%sc" % tag, "ve")
                block.vector(_ve)

            wprep("f_Wq", WQ_O, D, vecs["g1"], D, "wq")
            wprep("f_Wo", WO_O, D, None, D, "wo")
            wprep("f_Wdown", WD_O, LR, vecs["g1"], D, "wd")
            wprep("f_Wup", WU_O, 2 * D, None, LR, "wu")
            wprep("f_Wg", WG_O, NE, vecs["g2"], D, "wg")

        dtctr = [0]

        def ffn_phase(src_dram, dst_dram, tag):
            TL = FT
            NT = ntok // TL
            mlist = []

            def frontend(j, hoisted, apply_now=True):
                P = tag + str(j) + ":"
                t0 = j * TL
                XSQR = BQT if hoisted else BSQ

                def _ld(sync, t0=t0, j=j, tag=tag):
                    if j == 0:
                        sync.wait_ge(sem["ve"], tot.get("ve", 0))
                        sync.wait_ge(sem["qst"], tot.get("qst", 0))
                    else:
                        sync.wait_ge(sem["ve"], at(tag + str(j - 1) + ":xh"))
                    qs = "qx" if j % 2 == 0 else "qx1"
                    for c in range(NCH):
                        sync.dma_start(out=FPOOL[:, FXB + c * TL: FXB + (c + 1) * TL],
                                       in_=src_dram[c * 128:(c + 1) * 128, t0:t0 + TL]
                                       ).then_inc(sem[qs], 16); bump(qs, 16)
                    snap(P + "x", qs)
                block.sync(_ld)

                def _v1(vector, j=j):
                    vector.wait_ge(sem["qx" if j % 2 == 0 else "qx1"], at(P + "x"))
                    if hoisted:
                        vector.wait_ge(sem["pe"], at("g_lastvsm0"))
                        vector.wait_ge(sem["pe"], at("g_lastvsm1"))
                    vector.tensor_copy(BPOOL[:, BXB:BXB + NCH * TL],
                                       FPOOL[:, FXB:FXB + NCH * TL])
                    inc(vector, "ve"); snap(P + "xb", "ve")
                block.vector(_v1)
                def _a1(scalar, j=j, XSQR=XSQR):
                    scalar.wait_ge(sem["qx" if j % 2 == 0 else "qx1"], at(P + "x"))
                    if hoisted:
                        scalar.wait_ge(sem["pe"], at("g_lastvsm0"))
                        scalar.wait_ge(sem["pe"], at("g_lastvsm1"))
                    scalar.activation(BPOOL[:, XSQR:XSQR + NCH * TL],
                                      FPOOL[:, FXB:FXB + NCH * TL], AF.Square)
                    inc(scalar, "ac"); snap(P + "xsq", "ac")
                block.scalar(_a1)
                def _t1(tensor, j=j, tag=tag, XSQR=XSQR):
                    tensor.wait_ge(sem["ve"], at(P + "xb"))
                    tensor.wait_ge(sem["ac"], at(P + "xsq"))
                    if j > 0 and not hoisted:
                        tensor.wait_ge(sem["ve"], at(tag + str(j - 1) + ":out01"))
                    if hoisted:
                        tensor.wait_ge(sem["ve"], at(tag + str(j - 1) + ":xpe%d" % (NCH - 1)))
                    xb = cap(BPOOL, BXB, NCH, TL)
                    xq = cap(BPOOL, XSQR, NCH, TL)
                    for c in range(NCH):
                        tensor.matmul(PS[0][:, 0:TL], ones_b[:], xb[:, c],
                                      start=(c == 0), stop=(c == NCH - 1))
                    for c in range(NCH):
                        tensor.matmul(PS[1][:, 0:TL], ones_b[:], xq[:, c],
                                      start=(c == 0), stop=(c == NCH - 1))
                    inc(tensor, "pe"); snap(P + "st", "pe")
                block.tensor(_t1)
                def _v2(vector):
                    vector.wait_ge(sem["pe"], at(P + "st"))
                    vector.tensor_scalar(SPOOL[:, SS_MU:SS_MU + TL], PS[0][:, 0:TL],
                                         1.0 / D, None, OP.mult)
                    vector.tensor_scalar(SPOOL[:, SS_T:SS_T + TL], PS[1][:, 0:TL],
                                         1.0 / D, None, OP.mult)
                    vector.drain()
                    vector.tensor_tensor(SPOOL[:, SS_SQ:SS_SQ + TL],
                                         SPOOL[:, SS_MU:SS_MU + TL],
                                         SPOOL[:, SS_MU:SS_MU + TL], OP.mult)
                    vector.drain()
                    vector.tensor_tensor(SPOOL[:, SS_SUM:SS_SUM + TL],
                                         SPOOL[:, SS_T:SS_T + TL],
                                         SPOOL[:, SS_SQ:SS_SQ + TL], OP.subtract)
                    inc(vector, "ve"); snap(P + "var", "ve")
                block.vector(_v2)
                def _a2(scalar):
                    scalar.wait_ge(sem["ve"], at(P + "var"))
                    scalar.activation(SPOOL[:, SS_T:SS_T + TL],
                                      SPOOL[:, SS_SUM:SS_SUM + TL], AF.Ln, bias=cb[:, 0:1])
                    scalar.drain()
                    scalar.activation(SPOOL[:, SS_RSTD:SS_RSTD + TL],
                                      SPOOL[:, SS_T:SS_T + TL], AF.Exp, scale=-0.5)
                    inc(scalar, "ac"); snap(P + "rstd", "ac")
                block.scalar(_a2)
                def _v3(vector):
                    vector.wait_ge(sem["ac"], at(P + "rstd"))
                    vector.tensor_tensor(SPOOL[:, SS_T:SS_T + TL],
                                         SPOOL[:, SS_SUM:SS_SUM + TL],
                                         SPOOL[:, SS_RSTD:SS_RSTD + TL], OP.mult)
                    vector.drain()
                    vector.tensor_tensor(SPOOL[:, SS_T:SS_T + TL],
                                         SPOOL[:, SS_T:SS_T + TL],
                                         SPOOL[:, SS_RSTD:SS_RSTD + TL], OP.mult)
                    inc(vector, "ve"); snap(P + "tt", "ve")
                block.vector(_v3)
                def _a3(scalar):
                    scalar.wait_ge(sem["ve"], at(P + "tt"))
                    scalar.activation(SPOOL[:, SS_T:SS_T + TL],
                                      SPOOL[:, SS_T:SS_T + TL], AF.Ln, bias=cb[:, 1:2])
                    scalar.drain()
                    scalar.activation(SPOOL[:, SS_C1:SS_C1 + TL],
                                      SPOOL[:, SS_T:SS_T + TL], AF.Exp, scale=-0.5)
                    inc(scalar, "ac"); snap(P + "r1", "ac")
                block.scalar(_a3)
                def _v4a(vector):
                    vector.wait_ge(sem["ac"], at(P + "r1"))
                    vector.tensor_tensor(SPOOL[:, SS_C1:SS_C1 + TL],
                                         SPOOL[:, SS_C1:SS_C1 + TL],
                                         SPOOL[:, SS_RSTD:SS_RSTD + TL], OP.mult)
                    inc(vector, "ve"); snap(P + "c1", "ve")
                block.vector(_v4a)
                if not apply_now:
                    return
                emit_lnapply(j)

            def emit_lnapply(j):
                P = tag + str(j) + ":"
                def _v4(vector, j=j, tag=tag):
                    vector.wait_ge(sem["ve"], at(P + "c1"))
                    for c in range(NCH):
                        vector.tensor_tensor(PS[c][:, 0:TL],
                                             FPOOL[:, FXB + c * TL:FXB + (c + 1) * TL],
                                             SPOOL[:, SS_MU:SS_MU + TL], OP.subtract)
                    vector.drain()
                    for c in range(NCH):
                        vector.tensor_tensor(FPOOL[:, FLN1 + c * TL:FLN1 + (c + 1) * TL],
                                             PS[c][:, 0:TL],
                                             SPOOL[:, SS_RSTD:SS_RSTD + TL], OP.mult)
                        vector.tensor_tensor(BPOOL[:, BXB + c * TL:BXB + (c + 1) * TL],
                                             PS[c][:, 0:TL],
                                             SPOOL[:, SS_C1:SS_C1 + TL], OP.mult)
                    inc(vector, "ve"); snap(P + "hq", "ve")
                block.vector(_v4)

            frontend(0, False)
            for j in range(NT):
                t0 = j * TL
                P = tag + str(j) + ":"
                QROT = [0, 1, 5, 6, 7, 0, 1, 5]
                for hp in range(NCH):
                    def _tq(tensor, hp=hp):
                        tensor.wait_ge(sem["ve"], at(P + "hq"))
                        if hp >= 5:
                            tensor.wait_ge(sem["ac"], at(P + "qe%d" % (hp - 5)))
                        hq = cap(BPOOL, BXB, NCH, TL)
                        for c in range(NCH):
                            tensor.matmul(PS[QROT[hp]][:, 0:TL],
                                          WPOOL[:, WQ_O + c * D + hp * 128:
                                                WQ_O + c * D + (hp + 1) * 128],
                                          hq[:, c], start=(c == 0), stop=(c == NCH - 1))
                        inc(tensor, "pe"); snap(P + "q%d" % hp, "pe")
                    block.tensor(_tq)
                    def _aq(scalar, hp=hp):
                        scalar.wait_ge(sem["pe"], at(P + "q%d" % hp))
                        scalar.activation(BPOOL[:, BSQ + hp * TL: BSQ + (hp + 1) * TL],
                                          PS[QROT[hp]][:, 0:TL], AF.Identity,
                                          bias=vecs["bq"][:, hp:hp + 1])
                        inc(scalar, "ac"); snap(P + "qe%d" % hp, "ac")
                    block.scalar(_aq)

                def _td(tensor):
                    tensor.wait_ge(sem["ac"], at(P + "qe%d" % (NCH - 1)))
                    hq = cap(BPOOL, BXB, NCH, TL)
                    for c in range(NCH):
                        tensor.matmul(PS[2][0:LR, 0:TL],
                                      WPOOL[:, WD_O + c * LR: WD_O + (c + 1) * LR],
                                      hq[:, c], start=(c == 0), stop=(c == NCH - 1))
                    inc(tensor, "pe"); snap(P + "dv", "pe")
                block.tensor(_td)
                def _ad(scalar):
                    scalar.wait_ge(sem["pe"], at(P + "dv"))
                    scalar.activation(SMALLB[0:LR, 2560:2560 + TL],
                                      PS[2][0:LR, 0:TL], AF.Identity,
                                      bias=v_bdown[:, 0:1])
                    inc(scalar, "ac"); snap(P + "dve", "ac")
                block.scalar(_ad)

                def _dst0(par):
                    return BPOOL[0:64, BQT:BQT + H * HT].rearrange(
                        "p (G q j t) -> p q j G t", G=NG, q=2, j=8, t=8)[:, par]

                def _qs0(par):
                    return [BPOOL[par * 64:par * 64 + 64,
                                  BSQ + jj * TL:BSQ + jj * TL + HT]
                            for jj in range(8)]

                def _q0(sync, P=P):
                    sync.wait_ge(sem["ac"], at(P + "qe%d" % (NCH - 1)))
                    sync.wait_ge(sem["pe"], at("g_lastvsm0"))
                    if DEBUG_TAPS:
                        sync.wait_ge(sem["qst"], tot.get("qst", 0))
                    for jj in range(8):
                        sync.dma_start(out=_dst0(0)[:, jj], in_=_qs0(0)[jj]
                                       ).then_inc(sem["qsc"], 16)
                        bump("qsc", 16)
                block.sync(_q0)
                snap(P + "h0dt_scalar", "qsc")

                def _q1(gpsimd, P=P):
                    gpsimd.wait_ge(sem["ac"], at(P + "qe%d" % (NCH - 1)))
                    gpsimd.wait_ge(sem["pe"], at("g_lastvsm0"))
                    if DEBUG_TAPS:
                        gpsimd.wait_ge(sem["qst"], tot.get("qst", 0))
                    for jj in range(8):
                        gpsimd.dma_start(out=_dst0(1)[:, jj], in_=_qs0(1)[jj]
                                         ).then_inc(sem["qsd0"], 16)
                        bump("qsd0", 16)
                block.gpsimd(_q1)
                snap(P + "h0dt_gpsimd", "qsd0")

                for g in range(2 * NCH):
                    def _tk(tensor, g=g):
                        tensor.wait_ge(sem["ac"], at(P + "dve"))
                        if g >= 6:
                            tensor.wait_ge(sem["ac"], at(P + "kve%d" % (g - 6)))
                        tensor.matmul(PS[2 + g % 6][:, 0:TL],
                                      WPOOL[0:LR, WU_O + g * 128: WU_O + (g + 1) * 128],
                                      SMALLB[0:LR, 2560:2560 + TL],
                                      start=True, stop=True)
                        inc(tensor, "pe"); snap(P + "kv%d" % g, "pe")
                    block.tensor(_tk)
                    def _ak(scalar, g=g):
                        scalar.wait_ge(sem["pe"], at(P + "kv%d" % g))
                        scalar.activation(BPOOL[:, BKV + g * TL: BKV + (g + 1) * TL],
                                          PS[2 + g % 6][:, 0:TL], AF.Identity,
                                          bias=v_bup[:, g:g + 1])
                        inc(scalar, "ac"); snap(P + "kve%d" % g, "ac")
                        if g % 2 == 1:
                            if g == 1:
                                scalar.wait_ge(sem["pe"], at("g_lastvsm0"))
                                if DEBUG_TAPS:
                                    scalar.wait_ge(sem["qst"], tot.get("qst", 0))
                            dvx = BPOOL[0:128, BKVT:BKVT + H * HT].rearrange(
                                "p (G q j t) -> p q j G t", G=NG, q=2, j=8, t=8)
                            scalar.dma_start(out=dvx[:, g % 2, g // 2],
                                             in_=BPOOL[0:128, BKV + g * TL:
                                                       BKV + g * TL + HT]
                                             ).then_inc(sem["qsv"], 16)
                            bump("qsv", 16)
                    block.scalar(_ak)

                def _kvd(sync, P=P):
                    sync.wait_ge(sem["pe"], at("g_lastvsm0"))
                    if DEBUG_TAPS:
                        sync.wait_ge(sem["qst"], tot.get("qst", 0))
                    dv0 = BPOOL[0:128, BKVT:BKVT + H * HT].rearrange(
                        "p (G q j t) -> p q j G t", G=NG, q=2, j=8, t=8)
                    for g in range(0, 2 * NCH, 2):
                        sync.wait_ge(sem["ac"], at(P + "kve%d" % g))
                        sync.dma_start(out=dv0[:, g % 2, g // 2],
                                       in_=BPOOL[0:128, BKV + g * TL:
                                                 BKV + g * TL + HT]
                                       ).then_inc(sem["qsv"], 16)
                        bump("qsv", 16)
                block.sync(_kvd)
                snap(P + "h0dt_sync", "qsv")

                if DEBUG_TAPS and tag == "A" and j == 0:
                    def _tapkv(sync):
                        sync.wait_ge(sem["ac"], at(P + "kve%d" % (2 * NCH - 1)))
                        sync.dma_start(out=tap_kv, in_=BPOOL[:, BKV:BKV + 8192]
                                       ).then_inc(sem["qst"], 16); bump("qst", 16)
                    block.sync(_tapkv)
                QTB = [BQT, BXB]
                KVB = [BKVT, BXB + 2048]
                rs_names = {"scalar": ["qsc", "qsc1"], "gpsimd": ["qsd0", "qsd1"],
                            "sync": ["qsv", "qsv1"]}

                def emit_dt(s):
                    ht0 = s * HT
                    buf = dtctr[0] % 2
                    dtctr[0] += 1
                    PH = P + "h%d" % s
                    qt_o, kv_o = QTB[buf], KVB[buf]

                    def dstap(base, par):
                        return BPOOL[0:64, base:base + H * HT].rearrange(
                            "p (G q j t) -> p q j G t", G=NG, q=2, j=8, t=8)[:, par]

                    def qsrc(par):
                        return [BPOOL[par * 64:par * 64 + 64,
                                      BSQ + jj * TL + ht0:BSQ + jj * TL + ht0 + HT]
                                for jj in range(8)]

                    def kvsrc(par):
                        return [BPOOL[0:128, BKV + (2 * jj + par) * TL + ht0:
                                      BKV + (2 * jj + par) * TL + ht0 + HT]
                                for jj in range(8)]

                    def emit_ring(eng_name, dst, srcs, issue=None):
                        sname = rs_names[eng_name][buf]
                        def _r(eng, dst=dst, srcs=srcs, sname=sname):
                            eng.wait_ge(sem["ac"], at(P + "qe%d" % (NCH - 1)))
                            if buf == 1:
                                eng.wait_ge(sem["ac"], at(P + "kve%d" % (2 * NCH - 1)))
                            eng.wait_ge(sem["pe"], at("g_lastvsm%d" % buf))
                            if DEBUG_TAPS:
                                eng.wait_ge(sem["qst"], tot.get("qst", 0))
                            for jj in range(8):
                                eng.dma_start(out=dst[:, jj], in_=srcs[jj]
                                              ).then_inc(sem[sname], 16)
                                bump(sname, 16)
                        getattr(block, issue or eng_name)(_r)
                        snap(PH + "dt_" + eng_name, sname)

                    if s != 0:
                        emit_ring("scalar", dstap(qt_o, 0), qsrc(0), issue="sync")
                        emit_ring("gpsimd", dstap(qt_o, 1), qsrc(1))
                        sname = rs_names["sync"][buf]

                        def _kv(sync, kv_o=kv_o, buf=buf, sname=sname):
                            sync.wait_ge(sem["ac"], at(P + "kve%d" % (2 * NCH - 1)))
                            sync.wait_ge(sem["pe"], at("g_lastvsm%d" % buf))
                            dv = BPOOL[0:128, kv_o:kv_o + H * HT].rearrange(
                                "p (G q j t) -> p q j G t", G=NG, q=2, j=8, t=8)
                            ss = kvsrc(0)
                            for jj in range(8):
                                sync.dma_start(out=dv[:, 0, jj], in_=ss[jj]
                                               ).then_inc(sem[sname], 16)
                                bump(sname, 16)
                        block.sync(_kv)

                        def _kv2(scalar, kv_o=kv_o, buf=buf, sname=sname):
                            scalar.wait_ge(sem["ac"], at(P + "kve%d" % (2 * NCH - 1)))
                            scalar.wait_ge(sem["pe"], at("g_lastvsm%d" % buf))
                            dv = BPOOL[0:128, kv_o:kv_o + H * HT].rearrange(
                                "p (G q j t) -> p q j G t", G=NG, q=2, j=8, t=8)
                            ss = kvsrc(1)
                            for jj in range(8):
                                scalar.dma_start(out=dv[:, 1, jj], in_=ss[jj]
                                                 ).then_inc(sem[sname], 16)
                                bump(sname, 16)
                        block.scalar(_kv2)
                        snap(PH + "dt_sync", sname)
                    return buf

                dt_buf = {}
                dt_buf[0] = emit_dt(0)
                if FT // HT > 1:
                    dt_buf[1] = emit_dt(1)
                EMB = FPOOL[:, 12288:12800].bitcast(BF16)
                bundles = []
                NHALF = FT // HT
                for half in range(NHALF):
                    PH = P + "h%d" % half
                    for mb in range(NMB):
                        m = half * NMB + mb
                        g0 = mb * 8
                        PM = PH + "m%d" % mb
                        mlist.append(PM)
                        prev1 = mlist[-2] if len(mlist) >= 2 else None
                        prev2 = mlist[-3] if len(mlist) >= 3 else None
                        p = mb % 2
                        E2 = 0 if p == 0 else 3072
                        VSo = 2048 if p == 0 else 2560
                        QB = (PS[2], PS[3]) if p == 0 else (PS[0], PS[1])
                        em_full = SMALLB[:, 1024:2048] if p == 0 else EMB
                        srs_full = (SPOOL[:, SRS:SRS + 1024] if p == 0
                                    else FPOOL[:, 12800:13824])
                        buf = dt_buf[half]

                        def _ta(tensor, g0=g0, mb=mb, PM=PM, PH=PH, buf=buf,
                                QB=QB, prev1=prev1, prev2=prev2):
                            if mb == 0:
                                for en in ("scalar", "gpsimd", "sync"):
                                    tensor.wait_ge(sem[rs_names[en][buf]],
                                                   at(PH + "dt_" + en))
                            if prev2:
                                tensor.wait_ge(sem["ac"], at(prev2 + "e2"))
                            qt_o, kv_o = QTB[buf], KVB[buf]
                            for gi in range(8):
                                g = g0 + gi
                                bank = QB[0] if gi < 4 else QB[1]
                                tensor.matmul(bank[:, (gi % 4) * 128:(gi % 4 + 1) * 128],
                                              BPOOL[0:64, kv_o + g * 128:kv_o + (g + 1) * 128],
                                              BPOOL[0:64, qt_o + g * 128:qt_o + (g + 1) * 128],
                                              start=True, stop=True)
                            inc(tensor, "pe"); snap(PM + "att", "pe")
                            if prev1:
                                tensor.wait_ge(sem["ve"], at(prev1 + "vse"))
                            for gi in range(8):
                                g = g0 + gi
                                tensor.matmul(PS[4][:, gi * 64:(gi + 1) * 64],
                                              BPOOL[64:128, kv_o + g * 128:kv_o + (g + 1) * 128],
                                              id64_b[64:128, :],
                                              start=True, stop=True)
                            inc(tensor, "pe"); snap(PM + "vsm", "pe")
                            snap("g_lastvsm%d" % buf, "pe")

                        def _ae(scalar, PM=PM, QB=QB, E2=E2, prev2=prev2):
                            scalar.wait_ge(sem["pe"], at(PM + "att"))
                            if prev2:
                                scalar.wait_ge(sem["pe"], at(prev2 + "av"))
                            scalar.activation(SMALLB[:, E2:E2 + 512], QB[0][:, 0:512],
                                              AF.Exp, scale=0.125)
                            scalar.activation(SMALLB[:, E2 + 512:E2 + 1024],
                                              QB[1][:, 0:512], AF.Exp, scale=0.125)
                            inc(scalar, "ac"); snap(PM + "e2", "ac")

                        def _av(vector, PM=PM, VSo=VSo, prev2=prev2):
                            vector.wait_ge(sem["pe"], at(PM + "vsm"))
                            if prev2:
                                vector.wait_ge(sem["pe"], at(prev2 + "av"))
                            vector.tensor_copy(SMALLB[:, VSo:VSo + 512],
                                               PS[4][:, 0:512])
                            inc(vector, "ve"); snap(PM + "vse", "ve")

                        def _gm(gpsimd, PM=PM, E2=E2, em_full=em_full, prev2=prev2):
                            gpsimd.wait_ge(sem["ac"], at(PM + "e2"))
                            if prev2:
                                gpsimd.wait_ge(sem["pe"], at(prev2 + "s"))
                                gpsimd.wait_ge(sem["ve"], at(prev2 + "e2n"))
                            mb_ = mask_b[:, :].rearrange("p (a t) -> p a t", a=4, t=128
                                                         ).unsqueeze(1).broadcast_to(
                                                             [128, 2, 4, 128])
                            e2v = SMALLB[:, E2:E2 + 1024].rearrange(
                                "p (b a t) -> p b a t", b=2, a=4, t=128)
                            emv = em_full.rearrange(
                                "p (b a t) -> p b a t", b=2, a=4, t=128)
                            gpsimd.tensor_tensor(emv, e2v, mb_, OP.mult)
                            gpsimd.drain()
                            gpsimd.sem_inc(sem["gp"], 1)
                            bump("gp", 1)
                            snap(PM + "em", "gp")

                        def _ts(tensor, PM=PM, em_full=em_full, prev1=prev1):
                            tensor.wait_ge(sem["gp"], at(PM + "em"))
                            if prev1:
                                tensor.wait_ge(sem["ac"], at(prev1 + "rs"))
                            tensor.matmul(PS[5][:, 0:512], ones_b[:],
                                          em_full[:, 0:512], start=True, stop=True)
                            tensor.matmul(PS[7][:, 0:512], ones_b[:],
                                          em_full[:, 512:1024], start=True, stop=True)
                            inc(tensor, "pe"); snap(PM + "s", "pe")

                        def _ar(scalar, PM=PM, srs_full=srs_full, prev2=prev2):
                            scalar.wait_ge(sem["pe"], at(PM + "s"))
                            if prev2:
                                scalar.wait_ge(sem["ve"], at(prev2 + "e2n"))
                            scalar.activation(SPOOL[:, SLS:SLS + 512], PS[5][:, 0:512], AF.Ln)
                            scalar.activation(SPOOL[:, SLS + 512:SLS + 1024],
                                              PS[7][:, 0:512], AF.Ln)
                            scalar.drain()
                            scalar.activation(srs_full,
                                              SPOOL[:, SLS:SLS + 1024], AF.Exp, scale=-1.0)
                            inc(scalar, "ac"); snap(PM + "rs", "ac")

                        def _vn(vector, PM=PM, E2=E2, em_full=em_full,
                                srs_full=srs_full):
                            vector.wait_ge(sem["ac"], at(PM + "rs"))
                            vector.wait_ge(sem["gp"], at(PM + "em"))
                            vector.tensor_tensor(SMALLB[:, E2:E2 + 1024],
                                                 em_full, srs_full, OP.mult)
                            inc(vector, "ve"); snap(PM + "e2n", "ve")

                        def _tv(tensor, g0=g0, PM=PM, E2=E2, VSo=VSo, prev1=prev1):
                            tensor.wait_ge(sem["ve"], at(PM + "e2n"))
                            if prev1:
                                tensor.wait_ge(sem["ve"], at(prev1 + "o2"))
                            for gi in range(8):
                                rhs_e = SMALLB[:, E2 + gi * 128:E2 + gi * 128 + 64]
                                rhs_o = SMALLB[:, E2 + gi * 128 + 64:E2 + (gi + 1) * 128]
                                vs = SMALLB[:, VSo + gi * 64:VSo + (gi + 1) * 64]
                                tensor.matmul(PS[6][0:64, gi * 64:(gi + 1) * 64],
                                              vs, rhs_e, start=True, stop=True)
                                tensor.matmul(PS[6][64:128, gi * 64:(gi + 1) * 64],
                                              vs, rhs_o, start=True, stop=True,
                                              tile_position=(0, 64))
                            inc(tensor, "pe"); snap(PM + "av", "pe")

                        def _ao(vector, g0=g0, half=half, PM=PM):
                            vector.wait_ge(sem["pe"], at(PM + "av"))
                            g_glob = half * NG + g0
                            dst = BPOOL[:, BO2:BO2 + 8 * TL].rearrange(
                                "p (j G t) -> p G j t", j=8, G=TL // 8, t=8)[
                                :, g_glob:g_glob + 8]
                            srcv = PS[6][:, 0:512].rearrange(
                                "p (G j t) -> p G j t", G=8, j=8, t=8)
                            vector.tensor_copy(dst, srcv)
                            inc(vector, "ve"); snap(PM + "o2", "ve")

                        bundles.append({"ta": _ta, "ae": _ae, "gm": _gm,
                                        "ts": _ts, "ar": _ar, "vn": _vn,
                                        "tv": _tv, "ao": _ao, "av": _av})
                        block.tensor(_ta)
                        if m >= 1:
                            block.tensor(bundles[m - 1]["ts"])
                        if m >= 2:
                            block.tensor(bundles[m - 2]["tv"])
                        block.scalar(_ae)
                        if m >= 1:
                            block.scalar(bundles[m - 1]["ar"])
                        block.gpsimd(_gm)
                        block.vector(_av)
                        if m >= 2:
                            block.vector(bundles[m - 2]["ao"])
                        if m >= 1:
                            block.vector(bundles[m - 1]["vn"])
                        if mb == 1 and half + 2 < NHALF:
                            dt_buf[half + 2] = emit_dt(half + 2)
                M = NHALF * NMB
                block.tensor(bundles[M - 1]["ts"])
                block.scalar(bundles[M - 1]["ar"])
                block.tensor(bundles[M - 2]["tv"])
                block.vector(bundles[M - 2]["ao"])
                block.vector(bundles[M - 1]["vn"])
                block.tensor(bundles[M - 1]["tv"])
                block.vector(bundles[M - 1]["ao"])
                snap(P + "h%do2last" % (NHALF - 1), "ve")

                for fc in range(NCH):
                    def _tw(tensor, fc=fc):
                        tensor.wait_ge(sem["ve"], at(P + "h%do2last" % (FT // HT - 1)))
                        if fc >= 2:
                            tensor.wait_ge(sem["ve"], at(P + "xpe%d" % (fc - 2)))
                        for jj in range(8):
                            tensor.matmul(PS[fc % 2][:, 0:TL],
                                          WPOOL[:, WO_O + jj * D + fc * 128:
                                                WO_O + jj * D + (fc + 1) * 128],
                                          BPOOL[:, BO2 + jj * TL:BO2 + (jj + 1) * TL],
                                          start=(jj == 0), stop=(jj == 7))
                        inc(tensor, "pe"); snap(P + "wo%d" % fc, "pe")
                    block.tensor(_tw)
                    def _vw(vector, fc=fc, j=j):
                        vector.wait_ge(sem["pe"], at(P + "wo%d" % fc))
                        if fc == 0 and j > 0:
                            vector.wait_ge(sem["qst"],
                                           at(tag + str(j - 1) + ":stored"))
                        vector.scalar_tensor_tensor(
                            FPOOL[:, FXP + fc * TL: FXP + (fc + 1) * TL],
                            PS[fc % 2][:, 0:TL], vecs["bo"][:, fc:fc + 1],
                            FPOOL[:, FLN1 + fc * TL: FLN1 + (fc + 1) * TL],
                            OP.add, OP.add)
                        inc(vector, "ve"); snap(P + "xpe%d" % fc, "ve")
                    block.vector(_vw)

                if DEBUG_TAPS and tag == "A" and j == 0:
                    def _tapxp(sync):
                        sync.wait_ge(sem["ve"], at(P + "xpe%d" % (NCH - 1)))
                        sync.dma_start(out=tap_xp, in_=FPOOL[:, FXP:FXP + 4096]
                                       ).then_inc(sem["qst"], 16); bump("qst", 16)
                    block.sync(_tapxp)
                def _a4(scalar):
                    scalar.wait_ge(sem["ve"], at(P + "xpe%d" % (NCH - 1)))
                    scalar.activation(BPOOL[:, BSQ:BSQ + NCH * TL],
                                      FPOOL[:, FXP:FXP + NCH * TL], AF.Square)
                    inc(scalar, "ac"); snap(P + "sq2", "ac")
                block.scalar(_a4)
                def _vxh(vector):
                    vector.wait_ge(sem["ve"], at(P + "xpe%d" % (NCH - 1)))
                    vector.scalar_tensor_tensor(
                        cap(FPOOL, FXH, NCH, TL), cap(FPOOL, FXP, NCH, TL), 0.5,
                        cap(FPOOL, FXB, NCH, TL), OP.mult, OP.add)
                    inc(vector, "ve"); snap(P + "xh", "ve")
                block.vector(_vxh)
                def _t1b(tensor):
                    tensor.wait_ge(sem["ac"], at(P + "sq2"))
                    xq = cap(BPOOL, BSQ, NCH, TL)
                    for c in range(NCH):
                        tensor.matmul(PS[2][:, 0:TL], ones_b[:], xq[:, c],
                                      start=(c == 0), stop=(c == NCH - 1))
                    inc(tensor, "pe"); snap(P + "st2", "pe")
                block.tensor(_t1b)
                def _v5(vector):
                    vector.wait_ge(sem["pe"], at(P + "st2"))
                    vector.tensor_scalar(SPOOL[:, SS_T:SS_T + TL], PS[2][:, 0:TL],
                                         1.0 / D, None, OP.mult)
                    inc(vector, "ve"); snap(P + "m2", "ve")
                block.vector(_v5)
                def _a5(scalar):
                    scalar.wait_ge(sem["ve"], at(P + "m2"))
                    scalar.activation(SPOOL[:, SS_T:SS_T + TL],
                                      SPOOL[:, SS_T:SS_T + TL], AF.Ln, bias=cb[:, 1:2])
                    scalar.drain()
                    scalar.activation(SPOOL[:, SS_R2:SS_R2 + TL],
                                      SPOOL[:, SS_T:SS_T + TL], AF.Exp, scale=-0.5)
                    inc(scalar, "ac"); snap(P + "r2", "ac")
                block.scalar(_a5)
                def _v6(vector):
                    vector.wait_ge(sem["ac"], at(P + "r2"))
                    r2b = SPOOL[:, SS_R2:SS_R2 + TL].unsqueeze(1).broadcast_to([128, NCH, TL])
                    vector.tensor_tensor(cap(BPOOL, BSQ, NCH, TL),
                                         cap(FPOOL, FXP, NCH, TL), r2b, OP.mult)
                    inc(vector, "ve"); snap(P + "h2", "ve")
                block.vector(_v6)

                def _tg(tensor):
                    tensor.wait_ge(sem["ve"], at(P + "h2"))
                    h2 = cap(BPOOL, BSQ, NCH, TL)
                    for c in range(NCH):
                        tensor.matmul(PS[3][0:NE, 0:TL],
                                      WPOOL[:, WG_O + c * NE: WG_O + (c + 1) * NE],
                                      h2[:, c], start=(c == 0), stop=(c == NCH - 1))
                    inc(tensor, "pe"); snap(P + "gz", "pe")
                block.tensor(_tg)
                def _ag(scalar):
                    scalar.wait_ge(sem["pe"], at(P + "gz"))
                    scalar.activation(SMALLB[0:NE, 3072:3072 + TL],
                                      PS[3][0:NE, 0:TL], AF.Exp, bias=v_bg[:, 0:1])
                    inc(scalar, "ac"); snap(P + "eg", "ac")
                block.scalar(_ag)
                def _tgs(tensor):
                    tensor.wait_ge(sem["ac"], at(P + "eg"))
                    tensor.matmul(PS[3][:, 0:TL], ones_b[0:NE, :],
                                  SMALLB[0:NE, 3072:3072 + TL], start=True, stop=True)
                    inc(tensor, "pe"); snap(P + "gs", "pe")
                block.tensor(_tgs)
                def _agr(scalar):
                    scalar.wait_ge(sem["pe"], at(P + "gs"))
                    scalar.activation(SPOOL[:, SGRB:SGRB + TL], PS[3][:, 0:TL], AF.Ln)
                    scalar.drain()
                    scalar.activation(SPOOL[:, SGRB:SGRB + TL],
                                      SPOOL[:, SGRB:SGRB + TL], AF.Exp, scale=-1.0)
                    inc(scalar, "ac"); snap(P + "gr", "ac")
                block.scalar(_agr)
                def _vgn(vector):
                    vector.wait_ge(sem["ac"], at(P + "gr"))
                    vector.tensor_tensor(SMALLB[0:NE, 3584:3584 + TL],
                                         SMALLB[0:NE, 3072:3072 + TL],
                                         SPOOL[0:NE, SGRB:SGRB + TL], OP.mult)
                    inc(vector, "ve"); snap(P + "gn", "ve")
                block.vector(_vgn)
                for e in range(NE):
                    def _tgb(tensor, e=e):
                        tensor.wait_ge(sem["ve"], at(P + "gn"))
                        if e > 0:
                            tensor.wait_ge(sem["ac"], at(P + "gbe%d" % (e - 1)))
                        tensor.matmul(PS[4][:, 0:TL],
                                      esel_b[:, e * 128:(e + 1) * 128],
                                      SMALLB[0:NE, 3584:3584 + TL],
                                      start=True, stop=True)
                        inc(tensor, "pe"); snap(P + "gb%d" % e, "pe")
                    block.tensor(_tgb)
                    def _agb(scalar, e=e):
                        scalar.wait_ge(sem["pe"], at(P + "gb%d" % e))
                        if DEBUG_TAPS:
                            scalar.wait_ge(sem["qst"], tot.get("qst", 0))
                        scalar.activation(BPOOL[:, BGC + e * TL: BGC + (e + 1) * TL],
                                          PS[4][:, 0:TL], AF.Copy)
                        inc(scalar, "ac"); snap(P + "gbe%d" % e, "ac")
                    block.scalar(_agb)

                if j + 1 < NT:
                    frontend(j + 1, True, apply_now=False)

                steps = [(e, hh) for e in range(NE) for hh in range(2)]
                for si, (e, hh) in enumerate(steps):
                    buf = si % 2
                    def _swe(sync, e=e, hh=hh, si=si, buf=buf):
                        if si >= 2:
                            sync.wait_ge(sem["pe"], at(P + "ex%d" % (si - 2)))
                        else:
                            sync.wait_ge(sem["qst"], at("p_west%d" % (NE - 1)))
                        qs = "qwa" if buf == 0 else "qwb"
                        src = we_dram[e].rearrange("p (c f) -> p c f", c=NCH, f=D)[
                            :, :, hh * 512:(hh + 1) * 512]
                        sync.dma_start(out=WPOOL[:, WE_O + buf * 4096:
                                                 WE_O + buf * 4096 + 4096],
                                       in_=src).then_inc(sem[qs], 16); bump(qs, 16)
                        snap(P + "we%d" % si, qs)
                    block.sync(_swe)
                    if hh == 0:
                        def _vxe(vector, e=e):
                            vector.wait_ge(sem["ac"], at(P + "gbe%d" % e))
                            if DEBUG_TAPS:
                                vector.wait_ge(sem["qst"], tot.get("qst", 0))
                            if e >= 2:
                                vector.wait_ge(sem["pe"], at(P + "ex%d" % (2 * e - 3)))
                            gb = BPOOL[:, BGC + e * TL: BGC + (e + 1) * TL
                                       ].unsqueeze(1).broadcast_to([128, NCH, TL])
                            xeo = BXEA if e % 2 == 0 else BXEB
                            vector.tensor_tensor(cap(BPOOL, xeo, NCH, TL),
                                                 cap(BPOOL, BSQ, NCH, TL), gb, OP.mult)
                            inc(vector, "ve"); snap(P + "xe%d" % e, "ve")
                        block.vector(_vxe)
                    def _tex(tensor, e=e, hh=hh, si=si, buf=buf):
                        tensor.wait_ge(sem["ve"], at(P + "xe%d" % e))
                        tensor.wait_ge(sem["qwa" if buf == 0 else "qwb"],
                                       at(P + "we%d" % si))
                        if si == 0:
                            tensor.wait_ge(sem["ve"], at(P + "xh"))
                            tensor.wait_ge(sem["ac"], at(P + "gbe%d" % (NE - 1)))
                            if j + 1 < NT:
                                tensor.wait_ge(sem["ve"],
                                               at(tag + str(j + 1) + ":var"))
                        xeo = BXEA if e % 2 == 0 else BXEB
                        xe = cap(BPOOL, xeo, NCH, TL)
                        for fi in range(4):
                            fc = hh * 4 + fi
                            for c in range(NCH):
                                tensor.matmul(
                                    PS[fc][:, 0:TL],
                                    WPOOL[:, WE_O + buf * 4096 + c * 512 + fi * 128:
                                          WE_O + buf * 4096 + c * 512 + (fi + 1) * 128],
                                    xe[:, c],
                                    start=(e == 0 and c == 0),
                                    stop=(e == NE - 1 and c == NCH - 1))
                        inc(tensor, "pe"); snap(P + "ex%d" % si, "pe")
                    block.tensor(_tex)

                def _vout(vector):
                    vector.wait_ge(sem["pe"], at(P + "ex%d" % (2 * NE - 1)))
                    for fc in range(2):
                        vector.scalar_tensor_tensor(
                            FPOOL[:, FT1 + fc * TL: FT1 + (fc + 1) * TL],
                            PS[fc][:, 0:TL], 0.5,
                            FPOOL[:, FXH + fc * TL: FXH + (fc + 1) * TL],
                            OP.mult, OP.add)
                    inc(vector, "ve"); snap(P + "out01", "ve")
                    for fc in range(2, NCH):
                        vector.scalar_tensor_tensor(
                            FPOOL[:, FT1 + fc * TL: FT1 + (fc + 1) * TL],
                            PS[fc][:, 0:TL], 0.5,
                            FPOOL[:, FXH + fc * TL: FXH + (fc + 1) * TL],
                            OP.mult, OP.add)
                    inc(vector, "ve"); snap(P + "out", "ve")
                block.vector(_vout)
                def _sst(sync, t0=t0):
                    sync.wait_ge(sem["ve"], at(P + "out"))
                    for c in range(0, NCH, 2):
                        sync.dma_start(out=dst_dram[c * 128:(c + 1) * 128, t0:t0 + TL],
                                       in_=FPOOL[:, FT1 + c * TL: FT1 + (c + 1) * TL]
                                       ).then_inc(sem["qst"], 16); bump("qst", 16)
                block.sync(_sst)
                def _sst2(scalar, t0=t0):
                    scalar.wait_ge(sem["ve"], at(P + "out"))
                    for c in range(1, NCH, 2):
                        scalar.dma_start(out=dst_dram[c * 128:(c + 1) * 128, t0:t0 + TL],
                                         in_=FPOOL[:, FT1 + c * TL: FT1 + (c + 1) * TL]
                                         ).then_inc(sem["qst"], 16); bump("qst", 16)
                    snap(P + "stored", "qst")
                block.scalar(_sst2)
                if j + 1 < NT:
                    emit_lnapply(j + 1)


        def attn_phase(src_dram, dst_dram):
            TL, TLc = BT, BTc
            NBT = ntok // TL
            for wi, wname in enumerate(["W_k", "W_v", "W_r", "W_o"]):
                def _sw(sync, wi=wi, wname=wname):
                    if wi == 0:
                        sync.wait_ge(sem["pe"], tot.get("pe", 0))
                        sync.wait_ge(sem["ve"], tot.get("ve", 0))
                        sync.wait_ge(sem["qst"], tot.get("qst", 0))
                    else:
                        sync.wait_ge(sem["ve"], at("aw%d" % (wi - 1)))
                    for c in range(NCH):
                        sync.dma_start(out=FPOOL[:, c * D:(c + 1) * D],
                                       in_=dram["at_" + wname][c * 128:(c + 1) * 128, :]
                                       ).then_inc(sem["qld"], 16); bump("qld", 16)
                    snap("awl%d" % wi, "qld")
                block.sync(_sw)
                def _vw(vector, wi=wi):
                    vector.wait_ge(sem["qld"], at("awl%d" % wi))
                    for c in range(NCH):
                        vector.tensor_copy(WPOOL[:, wi * 8192 + c * D: wi * 8192 + (c + 1) * D],
                                           FPOOL[:, c * D:(c + 1) * D])
                    inc(vector, "ve"); snap("aw%d" % wi, "ve")
                block.vector(_vw)

            BITX = BPOOL[:, 12288:16384].bitcast(F32)
            BITMD = BPOOL[:, 16384:20480].bitcast(F32)
            BITX2 = BPOOL[:, 20480:24576].bitcast(F32)

            def axv(j):
                k = j % 3
                if k == 0:
                    return cap(FPOOL, FAX, NCH, TL)
                b = BITX if k == 1 else BITX2
                return b.rearrange("p (c t) -> p c t", c=NCH, t=TL)

            def bnd(base, c):
                return FPOOL[:, base + c * TLc: base + (c + 1) * TLc]
            pall = FPOOL[:, FAP:FAP + NCH * TLc].rearrange(
                "p (c t) -> p c t", c=NCH, t=TLc)
            qall = FPOOL[:, FAQ:FAQ + NCH * TLc].rearrange(
                "p (c t) -> p c t", c=NCH, t=TLc)
            lall = FPOOL[:, FALN:FALN + NCH * TLc].rearrange(
                "p (c t) -> p c t", c=NCH, t=TLc)
            mdv = BITMD.rearrange("p (c t) -> p c t", c=NCH, t=TL)

            def frontend1(j):
                P = "B%d:" % j
                t0 = j * TL

                def _ld(sync, t0=t0, j=j):
                    if j == 0:
                        sync.wait_ge(sem["ve"], tot.get("ve", 0))
                        sync.wait_ge(sem["qst"], tot.get("qst", 0))
                    elif j >= 3:
                        sync.wait_ge(sem["ve"], at("B%d:x2el" % (j - 3)))
                    dst = axv(j)
                    qs = ["qx", "qx1", "qx2"][j % 3]
                    for c in range(NCH):
                        sync.dma_start(out=dst[:, c],
                                       in_=src_dram[c * 128:(c + 1) * 128, t0:t0 + TL]
                                       ).then_inc(sem[qs], 16); bump(qs, 16)
                    snap(P + "x", qs)
                block.sync(_ld)

                def _ac(scalar, j=j):
                    scalar.wait_ge(sem[["qx", "qx1", "qx2"][j % 3]], at(P + "x"))
                    if j > 0:
                        scalar.wait_ge(sem["pe"], at("B%d:st" % (j - 1)))
                    xv = axv(j)
                    scalar.activation(cap(BPOOL, AXB2, NCH, TL), xv, AF.Identity)
                    scalar.activation(cap(BPOOL, ASQ2, NCH, TL), xv, AF.Square)
                    inc(scalar, "ac"); snap(P + "cast", "ac")
                block.scalar(_ac)

                def _t1(tensor, j=j):
                    tensor.wait_ge(sem["ac"], at(P + "cast"))
                    if j > 0:
                        tensor.wait_ge(sem["ve"], at("B%d:var" % (j - 1)))
                    xb = cap(BPOOL, AXB2, NCH, TL)
                    xq = cap(BPOOL, ASQ2, NCH, TL)
                    for c in range(NCH):
                        tensor.matmul(PS[0][:, 0:TL], ones_b[:], xb[:, c],
                                      start=(c == 0), stop=(c == NCH - 1))
                    for c in range(NCH):
                        tensor.matmul(PS[1][:, 0:TL], ones_b[:], xq[:, c],
                                      start=(c == 0), stop=(c == NCH - 1))
                    inc(tensor, "pe"); snap(P + "st", "pe")
                block.tensor(_t1)

                def _v2(vector):
                    vector.wait_ge(sem["pe"], at(P + "st"))
                    vector.tensor_scalar(SPOOL[:, AS_MU:AS_MU + TL], PS[0][:, 0:TL],
                                         1.0 / D, None, OP.mult)
                    vector.tensor_scalar(SPOOL[:, AS_T:AS_T + TL], PS[1][:, 0:TL],
                                         1.0 / D, None, OP.mult)
                    vector.drain()
                    vector.tensor_tensor(SPOOL[:, AS_SQ:AS_SQ + TL],
                                         SPOOL[:, AS_MU:AS_MU + TL],
                                         SPOOL[:, AS_MU:AS_MU + TL], OP.mult)
                    vector.drain()
                    vector.tensor_tensor(SPOOL[:, AS_SUM:AS_SUM + TL],
                                         SPOOL[:, AS_T:AS_T + TL],
                                         SPOOL[:, AS_SQ:AS_SQ + TL], OP.subtract)
                    inc(vector, "ve"); snap(P + "var", "ve")
                block.vector(_v2)

                def _a2(scalar, j=j):
                    scalar.wait_ge(sem["ve"], at(P + "var"))
                    if j > 0:
                        scalar.wait_ge(sem["ve"], at("B%d:ln2" % (j - 1)))
                    scalar.activation(SPOOL[:, AS_T:AS_T + TL],
                                      SPOOL[:, AS_SUM:AS_SUM + TL], AF.Ln, bias=cb[:, 0:1])
                    scalar.drain()
                    scalar.activation(SPOOL[:, AS_RSTD:AS_RSTD + TL],
                                      SPOOL[:, AS_T:AS_T + TL], AF.Exp, scale=-0.5)
                    inc(scalar, "ac"); snap(P + "rstd", "ac")
                block.scalar(_a2)

            def frontend2(j):
                P = "B%d:" % j
                seq_start = (j * TL) % tseq == 0

                def _v3(vector, seq_start=seq_start, j=j):
                    vector.wait_ge(sem["ac"], at(P + "rstd"))
                    if j > 0:
                        vector.wait_ge(sem["gp"], at("B%d:mix" % (j - 1)))
                    for c in range(NCH):
                        if seq_start:
                            vector.memset(bnd(FALN, c)[:, 0:1], 0.0)
                        else:
                            vector.tensor_copy(bnd(FALN, c)[:, 0:1],
                                               SPOOL[:, ALNC + c:ALNC + c + 1])
                    vector.drain()
                    mu_b = SPOOL[:, AS_MU:AS_MU + TL].unsqueeze(1).broadcast_to([128, NCH, TL])
                    rstd_b = SPOOL[:, AS_RSTD:AS_RSTD + TL].unsqueeze(1).broadcast_to([128, NCH, TL])
                    vector.tensor_tensor(lall[:, :, 1:TLc], axv(j), mu_b, OP.subtract)
                    vector.drain()
                    vector.tensor_tensor(lall[:, :, 1:TLc], lall[:, :, 1:TLc],
                                         rstd_b, OP.mult)
                    vector.drain()
                    for c in range(NCH):
                        vector.tensor_copy(SPOOL[:, ALNC + c:ALNC + c + 1],
                                           bnd(FALN, c)[:, TL:TL + 1])
                    inc(vector, "ve"); snap(P + "ln2", "ve")
                block.vector(_v3)
                def _g3(gpsimd, j=j):
                    gpsimd.wait_ge(sem["ve"], at(P + "ln2"))
                    if j > 0:
                        gpsimd.wait_ge(sem["pe"], at("B%d:kvpL" % (j - 1)))
                    for c in range(NCH):
                        cur = bnd(FALN, c)[:, 1:TLc]
                        prev = bnd(FALN, c)[:, 0:TL]
                        d_ = SPOOL[:, 5384:5384 + TL]
                        tmp = SPOOL[:, 5640:5640 + TL]
                        gpsimd.tensor_tensor(d_, cur, prev, OP.subtract)
                        gpsimd.drain()
                        for vec, bo_ in [(vecs["tmk"], AXK), (vecs["tmv"], AXV),
                                         (vecs["tmr"], AXR)]:
                            vb = vec[:, c:c + 1].broadcast_to([128, TL])
                            gpsimd.tensor_tensor(tmp, d_, vb, OP.mult)
                            gpsimd.drain()
                            gpsimd.tensor_tensor(
                                BPOOL[:, bo_ + c * TL: bo_ + (c + 1) * TL],
                                tmp, prev, OP.add)
                            gpsimd.drain()
                    gpsimd.sem_inc(sem["gp"], 1); bump("gp", 1)
                    snap(P + "mix", "gp")
                block.gpsimd(_g3)

            frontend1(0)
            frontend2(0)
            slot_last = {}

            def kvr_block(j, kinds):
                P = "B%d:" % j
                kb = (2, 3) if j % 2 == 0 else (4, 5)
                banks = [kb[0], kb[1], 6, 7]
                for wi in kinds:
                    bo_ = [AXK, AXV, AXR][wi]
                    for fc in range(NCH):
                        gidx = wi * NCH + fc
                        bank = banks[fc % 4]
                        off = (fc // 4) * 256
                        skey = (bank, off)
                        def _tk(tensor, wi=wi, bo_=bo_, fc=fc, bank=bank, off=off,
                                prev=slot_last.get(skey)):
                            tensor.wait_ge(sem["gp"], at(P + "mix"))
                            if prev is not None:
                                tensor.wait_ge(sem[prev[0]], at(prev[1]))
                            xm = cap(BPOOL, bo_, NCH, TL)
                            for c in range(NCH):
                                tensor.matmul(PS[bank][:, off:off + TL],
                                              WPOOL[:, wi * 8192 + c * D + fc * 128:
                                                    wi * 8192 + c * D + (fc + 1) * 128],
                                              xm[:, c], start=(c == 0), stop=(c == NCH - 1))
                            inc(tensor, "pe"); snap(P + "kvp%d" % gidx, "pe")
                        block.tensor(_tk)
                        fdst = [FAK, FAV, FAR][wi]
                        def _ak(scalar, wi=wi, fc=fc, fdst=fdst, gidx=gidx,
                                bank=bank, off=off, j=j):
                            scalar.wait_ge(sem["pe"], at(P + "kvp%d" % gidx))
                            if gidx == 0 and j > 0:
                                scalar.wait_ge(sem["ve"], at("B%d:y" % (j - 1)))
                            dst = FPOOL[:, fdst + fc * TL: fdst + (fc + 1) * TL]
                            scalar.activation(dst, PS[bank][:, off:off + TL], AF.Copy)
                            inc(scalar, "ac"); snap(P + "kve%d" % gidx, "ac")
                        block.scalar(_ak)
                        slot_last[skey] = ("ac", P + "kve%d" % gidx)
                snap(P + "kvpL", "pe")

            def wo_block(j):
                P = "B%d:" % j
                t0 = j * TL
                kb = (2, 3) if j % 2 == 0 else (4, 5)
                for fc in range(NCH):
                    bank = kb[fc % 2]
                    skey = (bank, "wo")
                    def _to(tensor, fc=fc, bank=bank, prevk=slot_last.get((bank, 0)),
                            prevk2=slot_last.get((bank, 256)), P=P):
                        tensor.wait_ge(sem["ve"], at(P + "y"))
                        if fc >= 2:
                            tensor.wait_ge(sem["ve"], at(P + "x2e%d" % (fc - 2)))
                        if prevk is not None:
                            tensor.wait_ge(sem[prevk[0]], at(prevk[1]))
                        if prevk2 is not None:
                            tensor.wait_ge(sem[prevk2[0]], at(prevk2[1]))
                        ym = cap(BPOOL, ABY, NCH, TL)
                        for c in range(NCH):
                            tensor.matmul(PS[bank][:, 0:TL],
                                          WPOOL[:, 3 * 8192 + c * D + fc * 128:
                                                3 * 8192 + c * D + (fc + 1) * 128],
                                          ym[:, c], start=(c == 0), stop=(c == NCH - 1))
                        inc(tensor, "pe"); snap(P + "yo%d" % fc, "pe")
                    block.tensor(_to)
                    def _vo(vector, fc=fc, j=j, bank=bank, P=P):
                        vector.wait_ge(sem["pe"], at(P + "yo%d" % fc))
                        xv = axv(j)
                        vector.scalar_tensor_tensor(
                            SPOOL[:, ASC2 + fc * TL: ASC2 + (fc + 1) * TL],
                            PS[bank][:, 0:TL], 0.0, xv[:, fc],
                            OP.add, OP.add)
                        inc(vector, "ve"); snap(P + "x2e%d" % fc, "ve")
                    block.vector(_vo)
                    slot_last[(bank, 0)] = ("ve", P + "x2e%d" % fc)
                snap(P + "x2el", "ve")
                def _sst(sync, t0=t0, P=P):
                    sync.wait_ge(sem["ve"], at(P + "x2el"))
                    for c in range(NCH):
                        sync.dma_start(out=dst_dram[c * 128:(c + 1) * 128, t0:t0 + TL],
                                       in_=SPOOL[:, ASC2 + c * TL: ASC2 + (c + 1) * TL]
                                       ).then_inc(sem["qst"], 16); bump("qst", 16)
                    snap(P + "stored", "qst")
                block.sync(_sst)

            for j in range(NBT):
                P = "B%d:" % j
                t0 = j * TL

                if j + 1 < NBT:
                    frontend1(j + 1)

                kvr_block(j, [0])
                if j > 0:
                    wo_block(j - 1)
                kvr_block(j, [1, 2])

                def _asig(scalar, P=P):
                    scalar.wait_ge(sem["ac"], at(P + "kve%d" % (3 * NCH - 1)))
                    scalar.activation(FPOOL[:, FAR:FAR + NCH * TL],
                                      FPOOL[:, FAR:FAR + NCH * TL], AF.Exp, scale=-1.0)
                    scalar.drain()
                    scalar.activation(FPOOL[:, FAR:FAR + NCH * TL],
                                      FPOOL[:, FAR:FAR + NCH * TL], AF.Ln, bias=cb[:, 2:3])
                    scalar.drain()
                    scalar.activation(FPOOL[:, FAR:FAR + NCH * TL],
                                      FPOOL[:, FAR:FAR + NCH * TL], AF.Exp, scale=-1.0)
                    inc(scalar, "ac"); snap(P + "sig", "ac")
                block.scalar(_asig)

                seq_start = (t0 % tseq) == 0
                def _vscan(vector, seq_start=seq_start, j=j, P=P):
                    vector.wait_ge(sem["ac"], at(P + "kve%d" % (2 * NCH - 1)))
                    if j > 0:
                        vector.wait_ge(sem["qst"], at("B%d:stored" % (j - 1)))
                    for c in range(NCH):
                        if seq_start:
                            vector.memset(bnd(FAP, c)[:, 0:1], -1e30)
                            vector.memset(bnd(FAQ, c)[:, 0:1], 0.0)
                        else:
                            vector.tensor_copy(bnd(FAP, c)[:, 0:1],
                                               bnd(FAP, c)[:, TL:TL + 1])
                            vector.tensor_copy(bnd(FAQ, c)[:, 0:1],
                                               bnd(FAQ, c)[:, TL:TL + 1])
                    vector.drain()
                    for c in range(NCH):
                        wb = vecs["w"][:, c:c + 1].broadcast_to([128, TL])
                        vector.tensor_tensor_scan(
                            bnd(FAP, c)[:, 1:TLc], wb,
                            FPOOL[:, FAK + c * TL: FAK + (c + 1) * TL],
                            bnd(FAP, c)[:, 0:1], OP.add, OP.max)
                    vector.drain()
                    for c in range(NCH):
                        vector.scalar_tensor_tensor(
                            mdv[:, c], FPOOL[:, FAK + c * TL: FAK + (c + 1) * TL],
                            vecs["u"][:, c:c + 1], pall[:, c, 0:TL],
                            OP.add, OP.subtract)
                    vector.drain()
                    vector.tensor_tensor(cap(SPOOL, ASC2, NCH, TL),
                                         cap(FPOOL, FAK, NCH, TL),
                                         pall[:, :, 1:TLc], OP.subtract)
                    inc(vector, "ve"); snap(P + "snarg", "ve")
                block.vector(_vscan)

                if j + 1 < NBT:
                    frontend2(j + 1)

                def _asn(scalar, P=P):
                    scalar.wait_ge(sem["ve"], at(P + "snarg"))
                    scalar.activation(SPOOL[:, ASNV:ASNV + NCH * TL],
                                      SPOOL[:, ASC2:ASC2 + NCH * TL], AF.Exp)
                    inc(scalar, "ac"); snap(P + "sn", "ac")
                block.scalar(_asn)
                def _vsnv(vector, P=P):
                    vector.wait_ge(sem["ac"], at(P + "sn"))
                    vector.tensor_tensor(cap(FPOOL, FAK, NCH, TL),
                                         cap(SPOOL, ASNV, NCH, TL),
                                         cap(FPOOL, FAV, NCH, TL), OP.mult)
                    vector.drain()
                    vector.tensor_tensor(cap(SPOOL, ASC2, NCH, TL),
                                         pall[:, :, 0:TL], pall[:, :, 1:TLc],
                                         OP.subtract)
                    inc(vector, "ve"); snap(P + "d1", "ve")
                block.vector(_vsnv)
                def _asp(scalar, P=P):
                    scalar.wait_ge(sem["ve"], at(P + "d1"))
                    for c in range(NCH):
                        scalar.activation(SPOOL[:, ASNV + c * TL: ASNV + (c + 1) * TL],
                                          SPOOL[:, ASC2 + c * TL: ASC2 + (c + 1) * TL],
                                          AF.Exp, bias=vecs["w"][:, c:c + 1])
                    inc(scalar, "ac"); snap(P + "sp", "ac")
                block.scalar(_asp)
                def _vq(vector, P=P):
                    vector.wait_ge(sem["ac"], at(P + "sp"))
                    for c in range(NCH):
                        vector.tensor_tensor_scan(
                            bnd(FAQ, c)[:, 1:TLc],
                            SPOOL[:, ASNV + c * TL: ASNV + (c + 1) * TL],
                            FPOOL[:, FAK + c * TL: FAK + (c + 1) * TL],
                            bnd(FAQ, c)[:, 0:1], OP.mult, OP.add)
                    vector.drain()
                    vector.tensor_scalar(cap(SPOOL, ASC2, NCH, TL), mdv,
                                         -1.0, 0.0, OP.mult, OP.min)
                    inc(vector, "ve"); snap(P + "qs", "ve")
                block.vector(_vq)
                def _aA(scalar, P=P):
                    scalar.wait_ge(sem["ve"], at(P + "qs"))
                    scalar.activation(FPOOL[:, FAK:FAK + NCH * TL],
                                      SPOOL[:, ASC2:ASC2 + NCH * TL], AF.Exp)
                    inc(scalar, "ac"); snap(P + "Araw", "ac")
                block.scalar(_aA)
                def _vA(vector, P=P):
                    vector.wait_ge(sem["ac"], at(P + "Araw"))
                    vector.tensor_scalar(cap(SPOOL, ASC2, NCH, TL),
                                         cap(FPOOL, FAK, NCH, TL),
                                         CL30, None, OP.max)
                    vector.drain()
                    vector.tensor_scalar(cap(FPOOL, FAK, NCH, TL), mdv,
                                         0.0, None, OP.min)
                    inc(vector, "ve"); snap(P + "Barg", "ve")
                block.vector(_vA)
                def _aB(scalar, P=P):
                    scalar.wait_ge(sem["ve"], at(P + "Barg"))
                    scalar.activation(SPOOL[:, ASNV:ASNV + NCH * TL],
                                      FPOOL[:, FAK:FAK + NCH * TL], AF.Exp)
                    inc(scalar, "ac"); snap(P + "Braw", "ac")
                block.scalar(_aB)
                def _vfin(vector, P=P):
                    vector.wait_ge(sem["ac"], at(P + "Braw"))
                    vector.tensor_scalar(cap(FPOOL, FAK, NCH, TL),
                                         cap(SPOOL, ASNV, NCH, TL), CL30, None, OP.max)
                    vector.drain()
                    vector.tensor_tensor(cap(SPOOL, ASNV, NCH, TL),
                                         cap(SPOOL, ASC2, NCH, TL),
                                         cap(FPOOL, FAK, NCH, TL), OP.add)
                    vector.drain()
                    vector.tensor_tensor(mdv, cap(FPOOL, FAK, NCH, TL),
                                         cap(FPOOL, FAV, NCH, TL), OP.mult)
                    vector.drain()
                    vector.tensor_tensor(cap(FPOOL, FAV, NCH, TL),
                                         cap(SPOOL, ASC2, NCH, TL),
                                         qall[:, :, 0:TL], OP.mult)
                    vector.drain()
                    vector.tensor_tensor(cap(SPOOL, ASC2, NCH, TL),
                                         cap(FPOOL, FAV, NCH, TL), mdv, OP.add)
                    inc(vector, "ve"); snap(P + "num", "ve")
                block.vector(_vfin)
                def _aden(scalar, P=P):
                    scalar.wait_ge(sem["ve"], at(P + "num"))
                    scalar.activation(FPOOL[:, FAK:FAK + NCH * TL],
                                      SPOOL[:, ASNV:ASNV + NCH * TL], AF.Ln,
                                      bias=cb[:, 3:4])
                    scalar.drain()
                    scalar.activation(SPOOL[:, ASNV:ASNV + NCH * TL],
                                      FPOOL[:, FAK:FAK + NCH * TL], AF.Exp, scale=-1.0)
                    inc(scalar, "ac"); snap(P + "rden", "ac")
                block.scalar(_aden)
                def _vy(vector, P=P):
                    vector.wait_ge(sem["ac"], at(P + "rden"))
                    vector.wait_ge(sem["ac"], at(P + "sig"))
                    vector.tensor_tensor(cap(FPOOL, FAK, NCH, TL),
                                         cap(SPOOL, ASC2, NCH, TL),
                                         cap(SPOOL, ASNV, NCH, TL), OP.mult)
                    vector.drain()
                    vector.tensor_tensor(cap(BPOOL, ABY, NCH, TL),
                                         cap(FPOOL, FAK, NCH, TL),
                                         cap(FPOOL, FAR, NCH, TL), OP.mult)
                    inc(vector, "ve"); snap(P + "y", "ve")
                block.vector(_vy)
            wo_block(NBT - 1)

        prep()
        ffn_phase(x_in, x1_dram, "A")
        attn_phase(x1_dram, x2_dram)
        def _reload_gate(sync):
            sync.wait_ge(sem["pe"], tot.get("pe", 0))
            sync.wait_ge(sem["ve"], tot.get("ve", 0))
        block.sync(_reload_gate)
        def reprep():
            def wprep2(dname, dst_off, ncols, scale, nrow, tag):
                def _sy(sync):
                    sync.wait_ge(sem["ve"], tot.get("ve", 0))
                    nch = (nrow + 127) // 128
                    for c in range(nch):
                        r0, r1 = c * 128, min((c + 1) * 128, nrow)
                        sync.dma_start(out=FPOOL[0:r1 - r0, c * ncols:(c + 1) * ncols],
                                       in_=dram[dname][r0:r1, :]
                                       ).then_inc(sem["qld"], 16); bump("qld", 16)
                    snap("r_%sld" % tag, "qld")
                block.sync(_sy)
                def _ve(vector):
                    vector.wait_ge(sem["qld"], at("r_%sld" % tag))
                    nch = (nrow + 127) // 128
                    for c in range(nch):
                        r0, r1 = c * 128, min((c + 1) * 128, nrow)
                        dst = WPOOL[0:r1 - r0, dst_off + c * ncols: dst_off + (c + 1) * ncols]
                        src = FPOOL[0:r1 - r0, c * ncols:(c + 1) * ncols]
                        if scale is None:
                            vector.tensor_copy(dst, src)
                        else:
                            vector.tensor_scalar(dst, src, scale[:, c:c + 1], None, OP.mult)
                    inc(vector, "ve"); snap("r_%sc" % tag, "ve")
                block.vector(_ve)
            wprep2("f_Wq", WQ_O, D, vecs["g1"], D, "wq")
            wprep2("f_Wo", WO_O, D, None, D, "wo")
            wprep2("f_Wdown", WD_O, LR, vecs["g1"], D, "wd")
            wprep2("f_Wup", WU_O, 2 * D, None, LR, "wu")
            wprep2("f_Wg", WG_O, NE, vecs["g2"], D, "wg")
        reprep()
        ffn_phase(x2_dram, out_fm, "C")

        def _fin(sync):
            sync.wait_ge(sem["qst"], tot.get("qst", 0))
        block.sync(_fin)

        with nc.Block() as RB:
            RB.sync(lambda e: _replay(e, block.q["sync"]))
            RB.scalar(lambda e: _replay(e, block.q["scalar"]))
            RB.vector(lambda e: _replay(e, block.q["vector"]))
            RB.tensor(lambda e: _replay(e, block.q["tensor"]))
            RB.gpsimd(lambda e: _replay(e, block.q["gpsimd"]))

    return nc


def _vec_layout(v):
    v = np.asarray(v, np.float32)
    n = v.shape[0]
    return np.ascontiguousarray(v.reshape(n // 128, 128).T)


def _make_consts(HT):
    ones = np.ones((128, 128), bfdt)
    id64 = np.concatenate([np.eye(64, dtype=np.float32)] * 2, axis=0).astype(bfdt)
    esel = np.zeros((NE, NE * 128), np.float32)
    for e in range(NE):
        esel[e, e * 128:(e + 1) * 128] = 1.0
    m1 = np.where(np.arange(128)[:, None] % 8 == np.arange(128)[None, :] % 8,
                  np.float32(1.0), np.float32(CL30))
    mask = np.concatenate([m1] * 4, axis=1)
    return ones, id64, esel.astype(bfdt), mask.astype(bfdt)


_NC_CACHE = {}


def kernel(**inputs):
    x = np.asarray(inputs["L0_attn_x"], np.float32)
    B, T, _ = x.shape
    ncore = 8
    nseq = B // ncore
    key = (nseq, T)
    if key not in _NC_CACHE:
        _NC_CACHE[key] = build_nc(nseq, T)
    nc = _NC_CACHE[key]

    HT = min(512, T) // 4
    ones, id64, esel, mask = _make_consts(HT)

    shared = {
        "at_W_k": np.ascontiguousarray(np.asarray(inputs["L0_attn_W_k"], np.float32).T),
        "at_W_v": np.ascontiguousarray(np.asarray(inputs["L0_attn_W_v"], np.float32).T),
        "at_W_r": np.ascontiguousarray(np.asarray(inputs["L0_attn_W_r"], np.float32).T),
        "at_W_o": np.ascontiguousarray(np.asarray(inputs["L0_attn_W_o"], np.float32).T),
        "at_tmk": _vec_layout(inputs["L0_attn_time_mix_k"]),
        "at_tmv": _vec_layout(inputs["L0_attn_time_mix_v"]),
        "at_tmr": _vec_layout(inputs["L0_attn_time_mix_r"]),
        "at_w": _vec_layout(inputs["L0_attn_time_decay"]),
        "at_u": _vec_layout(inputs["L0_attn_time_first"]),
        "f_g1": _vec_layout(inputs["L0_ffn_gamma1"]),
        "f_g2": _vec_layout(inputs["L0_ffn_gamma2"]),
        "f_bq": _vec_layout(inputs["L0_ffn_bq"]),
        "f_bo": _vec_layout(inputs["L0_ffn_bo"]),
        "f_Wq": np.asarray(inputs["L0_ffn_Wq"], np.float32),
        "f_Wdown": np.asarray(inputs["L0_ffn_Wdown"], np.float32),
        "f_Wup": np.asarray(inputs["L0_ffn_Wup"], np.float32),
        "f_Wo": np.asarray(inputs["L0_ffn_Wo"], np.float32),
        "f_Wg": np.asarray(inputs["L0_ffn_Wg"], np.float32),
        "f_We": np.ascontiguousarray(
            np.asarray(inputs["L0_ffn_We"], np.float32).reshape(NE * D, D)),
        "f_bdown": np.asarray(inputs["L0_ffn_bdown"], np.float32).reshape(LR, 1),
        "f_bup": _vec_layout(inputs["L0_ffn_bup"]),
        "f_bg": np.asarray(inputs["L0_ffn_bg"], np.float32).reshape(NE, 1),
        "c_ones": ones, "c_id64": id64, "c_esel": esel, "c_mask": mask,
    }
    in_maps = []
    for i in range(ncore):
        xm = np.ascontiguousarray(
            x[i * nseq:(i + 1) * nseq].reshape(nseq * T, D).T)
        m = dict(shared)
        m["x_fm"] = xm
        in_maps.append(m)

    res = run_bass_kernel_spmd(nc, in_maps, core_ids=list(range(ncore)))
    outs = []
    for i in range(ncore):
        o = np.asarray(res.results[i]["out_fm"])
        outs.append(o.T.reshape(nseq, T, D))
    return np.concatenate(outs, axis=0).astype(np.float32)


# version 19 (final)
# speedup vs baseline: 1.0491x; 1.0100x over previous
import numpy as np
import ml_dtypes
from contextlib import ExitStack

import concourse.bass as bass
import concourse.mybir as mybir
from concourse.alu_op_type import AluOpType
from concourse.bass_utils import run_bass_kernel_spmd

F32 = mybir.dt.float32
BF16 = mybir.dt.bfloat16
AF = mybir.ActivationFunctionType
OP = AluOpType
bfdt = ml_dtypes.bfloat16

D = 1024
NCH = 8
H, HD, LR, NE = 16, 64, 64, 8
SUPPV = 15.5
CL30 = float(np.exp(np.float32(-30.0)))
EPS_LN = 1e-5
EPS_RMS = 1e-6
EPS_WKV = 1e-9


DEBUG_TAPS = False


class _FakeInst:
    __slots__ = ("ti",)

    def __init__(self):
        self.ti = None

    def then_inc(self, sem, v):
        self.ti = (sem, v)
        return self


class _EngRecorder:
    def __init__(self, lst):
        self._lst = lst

    def __getattr__(self, name):
        lst = self._lst

        def call(*a, **k):
            fi = _FakeInst()
            lst.append((name, a, k, fi))
            return fi
        return call


class _BlockRecorder:

    def __init__(self):
        self.q = {e: [] for e in ("sync", "scalar", "vector", "tensor", "gpsimd")}

    def _mk(self, eng):
        def add(fn):
            fn(_EngRecorder(self.q[eng]))
        return add

    def __getattr__(self, name):
        if name in ("sync", "scalar", "vector", "tensor", "gpsimd"):
            return self._mk(name)
        raise AttributeError(name)


def _replay(engine, lst):
    for name, a, k, fi in lst:
        r = getattr(engine, name)(*a, **k)
        if fi.ti is not None:
            r.then_inc(*fi.ti)


def build_nc(nseq, tseq):
    ntok = nseq * tseq
    FT = min(512, tseq)
    BT = min(256, tseq)
    BTc = BT + 1
    HT = FT // 4
    NG = HT // 8
    NMB = NG // 8
    assert NG % 8 == 0

    nc = bass.Bass("TRN2", target_bir_lowering=False, debug=False)

    dram = {}
    def din(name, shape, dt=F32):
        dram[name] = nc.dram_tensor(name, shape, dt, kind="ExternalInput").ap()
        return dram[name]

    x_in = din("x_fm", [D, ntok])
    for w in ["W_k", "W_v", "W_r", "W_o"]:
        din("at_" + w, [D, D])
    for v in ["tmk", "tmv", "tmr", "w", "u"]:
        din("at_" + v, [128, NCH])
    for v in ["g1", "g2", "bq", "bo"]:
        din("f_" + v, [128, NCH])
    din("f_Wq", [D, D]); din("f_Wdown", [D, LR]); din("f_Wup", [LR, 2 * D])
    din("f_Wo", [D, D]); din("f_Wg", [D, NE]); din("f_We", [NE * D, D])
    din("f_bdown", [LR, 1]); din("f_bup", [128, 2 * NCH]); din("f_bg", [NE, 1])
    din("c_ones", [128, 128], BF16)
    din("c_id64", [128, 64], BF16)
    din("c_esel", [NE, NE * 128], BF16)
    din("c_mask", [128, 512], BF16)
    out_fm = nc.dram_tensor("out_fm", [D, ntok], F32, kind="ExternalOutput").ap()
    x1_dram = nc.dram_tensor("x1_scr", [D, ntok], F32, kind="Internal").ap()
    tap_o2 = (nc.dram_tensor("tap_o2", [128, 4096], BF16, kind="ExternalOutput").ap()
              if DEBUG_TAPS else None)
    tap_xp = (nc.dram_tensor("tap_xp", [128, 4096], F32, kind="ExternalOutput").ap()
              if DEBUG_TAPS else None)
    tap_hq = (nc.dram_tensor("tap_hq", [128, 4096], BF16, kind="ExternalOutput").ap()
              if DEBUG_TAPS else None)
    tap_kv = (nc.dram_tensor("tap_kv", [128, 8192], BF16, kind="ExternalOutput").ap()
              if DEBUG_TAPS else None)
    tap_sm = (nc.dram_tensor("tap_sm", [128, 4096], BF16, kind="ExternalOutput").ap()
              if DEBUG_TAPS else None)
    tap_qt = (nc.dram_tensor("tap_qt", [128, 3 * 2048], BF16, kind="ExternalOutput").ap()
              if DEBUG_TAPS else None)
    x2_dram = nc.dram_tensor("x2_scr", [D, ntok], F32, kind="Internal").ap()
    we_dram = nc.dram_tensor("we_scr", [NE, 128, NCH * D], BF16, kind="Internal").ap()

    cnt = {}
    tot = {}
    def bump(s, by):
        tot[s] = tot.get(s, 0) + by
        return tot[s]
    def snap(name, s):
        cnt[name] = tot.get(s, 0)
    def at(name):
        return cnt.get(name, 0)

    ctx = ExitStack()
    with ctx:
        sem = {}
        for s in ["qld", "qsc", "qsv", "qst", "qx", "qwa", "qwb", "pe", "ve", "ac", "gp",
                  "qsc1", "qsv1", "qsd0", "qsd1", "qtt0", "qtt1", "qx1", "qx2"]:
            sem[s] = ctx.enter_context(nc.semaphore(s))

        def inc(eng, s):
            eng.drain()
            eng.sem_inc(sem[s], 1)
            bump(s, 1)

        vecs = {}
        for name in ["tmk", "tmv", "tmr", "w", "u", "g1", "g2", "bq", "bo"]:
            vecs[name] = ctx.enter_context(nc.sbuf_tensor("v_" + name, [128, NCH], F32))
        v_bdown = ctx.enter_context(nc.sbuf_tensor("v_bdown", [LR, 1], F32))
        v_bup = ctx.enter_context(nc.sbuf_tensor("v_bup", [128, 2 * NCH], F32))
        v_bg = ctx.enter_context(nc.sbuf_tensor("v_bg", [NE, 1], F32))
        ones_b = ctx.enter_context(nc.sbuf_tensor("ones_b", [128, 128], BF16))
        cb = ctx.enter_context(nc.sbuf_tensor("cbias", [128, 4], F32))
        id64_b = ctx.enter_context(nc.sbuf_tensor("id64_b", [128, 64], BF16))
        mask_b = ctx.enter_context(nc.sbuf_tensor("mask_b", [128, 512], BF16))
        esel_b = ctx.enter_context(nc.sbuf_tensor("esel_b", [NE, NE * 128], BF16))

        WPOOL = ctx.enter_context(nc.sbuf_tensor("wpool", [128, 32768], BF16))
        SPOOL = ctx.enter_context(nc.sbuf_tensor("spool", [128, 6144], F32))
        FPOOL = ctx.enter_context(nc.sbuf_tensor("fpool", [128, 14464], F32))
        BPOOL = ctx.enter_context(nc.sbuf_tensor("bpool", [128, 24576], BF16))
        PS = [ctx.enter_context(nc.psum_tensor("ps%d" % i, [128, 512], F32))
              for i in range(8)]

        block = _BlockRecorder()

        def cap(pool, base, nch, width):
            return pool[:, base:base + nch * width].rearrange(
                "p (c t) -> p c t", c=nch, t=width)

        WQ_O, WO_O, WD_O, WU_O, WG_O, WE_O = 0, 8192, 16384, 16896, 18944, 19456

        BXB, BSQ, BKV, BQF = 0, 4096, 8192, 16384
        BO2 = BQF
        BXEA = BKV
        BXEB = BKV + 4096
        BQT, BKVT = 20480, 22528
        BGC = BQF
        SMALLB = ctx.enter_context(nc.sbuf_tensor("smallb", [128, 4096], BF16))

        AXK, AXV, AXR, ABY, AXB2, ASQ2 = 0, 2048, 4096, 6144, 8192, 10240

        FXB, FT1, FLN1 = 0, 4096, 8192
        FXP = FT1
        FXH = FLN1
        FAX, FALN, FAP, FAQ = 0, 2048, 4104, 6160
        FAK, FAV, FAR, FSC2 = 8216, 10264, 12312, 14360

        SS_SUM, SS_SQ, SS_MU, SS_RSTD, SS_C1, SS_T, SS_R2 = (
            0, 512, 1024, 1536, 2048, 2560, 3072)
        SLS, SRS, SGRB = 3584, 4608, 5632
        AS_SUM, AS_SQ, AS_MU, AS_RSTD, AS_T = 0, 256, 512, 768, 1024
        ASNV = 1280
        ASC2 = 3328
        ALNC = 5376

        def prep():
            def _vc(vector):
                vector.memset(cb[:, 0:1], EPS_LN)
                vector.memset(cb[:, 1:2], EPS_RMS)
                vector.memset(cb[:, 2:3], 1.0)
                vector.memset(cb[:, 3:4], EPS_WKV)
                inc(vector, "ve"); snap("p_cb", "ve")
            block.vector(_vc)
            def _sy(sync):
                for name in ["tmk", "tmv", "tmr", "w", "u"]:
                    sync.dma_start(out=vecs[name][:], in_=dram["at_" + name]
                                   ).then_inc(sem["qld"], 16); bump("qld", 16)
                for name in ["g1", "g2", "bq", "bo"]:
                    sync.dma_start(out=vecs[name][:], in_=dram["f_" + name]
                                   ).then_inc(sem["qld"], 16); bump("qld", 16)
                sync.dma_start(out=v_bdown[:], in_=dram["f_bdown"]).then_inc(sem["qld"], 16); bump("qld", 16)
                sync.dma_start(out=v_bup[:], in_=dram["f_bup"]).then_inc(sem["qld"], 16); bump("qld", 16)
                sync.dma_start(out=v_bg[:], in_=dram["f_bg"]).then_inc(sem["qld"], 16); bump("qld", 16)
                sync.dma_start(out=ones_b[:], in_=dram["c_ones"]).then_inc(sem["qld"], 16); bump("qld", 16)
                sync.dma_start(out=id64_b[:], in_=dram["c_id64"]).then_inc(sem["qld"], 16); bump("qld", 16)
                sync.dma_start(out=mask_b[:], in_=dram["c_mask"]).then_inc(sem["qld"], 16); bump("qld", 16)
                sync.dma_start(out=esel_b[:], in_=dram["c_esel"]).then_inc(sem["qld"], 16); bump("qld", 16)
                snap("p_vecs", "qld")
            block.sync(_sy)

            for e in range(NE):
                def _sy(sync, e=e):
                    if e > 0:
                        sync.wait_ge(sem["ve"], at("p_wecast%d" % (e - 1)))
                    for c in range(NCH):
                        sync.dma_start(out=FPOOL[:, c * D:(c + 1) * D],
                                       in_=dram["f_We"][e * D + c * 128:e * D + (c + 1) * 128, :]
                                       ).then_inc(sem["qld"], 16); bump("qld", 16)
                    snap("p_weld%d" % e, "qld")
                block.sync(_sy)
                def _ve(vector, e=e):
                    vector.wait_ge(sem["qld"], at("p_weld%d" % e))
                    if e > 0:
                        vector.wait_ge(sem["qst"], at("p_west%d" % (e - 1)))
                    for c in range(NCH):
                        vector.tensor_scalar(BPOOL[:, c * D:(c + 1) * D],
                                             FPOOL[:, c * D:(c + 1) * D],
                                             vecs["g2"][:, c:c + 1], None, OP.mult)
                    inc(vector, "ve"); snap("p_wecast%d" % e, "ve")
                block.vector(_ve)
                def _sy2(sync, e=e):
                    sync.wait_ge(sem["ve"], at("p_wecast%d" % e))
                    sync.dma_start(out=we_dram[e], in_=BPOOL[:, 0:NCH * D]
                                   ).then_inc(sem["qst"], 16); bump("qst", 16)
                    snap("p_west%d" % e, "qst")
                block.sync(_sy2)

            def wprep(dname, dst_off, ncols, scale, nrow, tag):
                def _sy(sync):
                    sync.wait_ge(sem["ve"], tot.get("ve", 0))
                    nch = (nrow + 127) // 128
                    for c in range(nch):
                        r0, r1 = c * 128, min((c + 1) * 128, nrow)
                        sync.dma_start(out=FPOOL[0:r1 - r0, c * ncols:(c + 1) * ncols],
                                       in_=dram[dname][r0:r1, :]
                                       ).then_inc(sem["qld"], 16); bump("qld", 16)
                    snap("p_%sld" % tag, "qld")
                block.sync(_sy)
                def _ve(vector):
                    vector.wait_ge(sem["qld"], at("p_%sld" % tag))
                    nch = (nrow + 127) // 128
                    for c in range(nch):
                        r0, r1 = c * 128, min((c + 1) * 128, nrow)
                        dst = WPOOL[0:r1 - r0, dst_off + c * ncols: dst_off + (c + 1) * ncols]
                        src = FPOOL[0:r1 - r0, c * ncols:(c + 1) * ncols]
                        if scale is None:
                            vector.tensor_copy(dst, src)
                        else:
                            vector.tensor_scalar(dst, src, scale[:, c:c + 1], None, OP.mult)
                    inc(vector, "ve"); snap("p_%sc" % tag, "ve")
                block.vector(_ve)

            wprep("f_Wq", WQ_O, D, vecs["g1"], D, "wq")
            wprep("f_Wo", WO_O, D, None, D, "wo")
            wprep("f_Wdown", WD_O, LR, vecs["g1"], D, "wd")
            wprep("f_Wup", WU_O, 2 * D, None, LR, "wu")
            wprep("f_Wg", WG_O, NE, vecs["g2"], D, "wg")

        dtctr = [0]

        def ffn_phase(src_dram, dst_dram, tag):
            TL = FT
            NT = ntok // TL
            mlist = []

            def frontend(j, hoisted, apply_now=True):
                P = tag + str(j) + ":"
                t0 = j * TL
                XSQR = BQT if hoisted else BSQ

                def _ld(sync, t0=t0, j=j, tag=tag):
                    if j == 0:
                        sync.wait_ge(sem["ve"], tot.get("ve", 0))
                        sync.wait_ge(sem["qst"], tot.get("qst", 0))
                    else:
                        sync.wait_ge(sem["ve"], at(tag + str(j - 1) + ":xh"))
                    qs = "qx" if j % 2 == 0 else "qx1"
                    for c in range(NCH):
                        sync.dma_start(out=FPOOL[:, FXB + c * TL: FXB + (c + 1) * TL],
                                       in_=src_dram[c * 128:(c + 1) * 128, t0:t0 + TL]
                                       ).then_inc(sem[qs], 16); bump(qs, 16)
                    snap(P + "x", qs)
                block.sync(_ld)

                def _v1(vector, j=j):
                    vector.wait_ge(sem["qx" if j % 2 == 0 else "qx1"], at(P + "x"))
                    if hoisted:
                        vector.wait_ge(sem["pe"], at("g_lastvsm0"))
                        vector.wait_ge(sem["pe"], at("g_lastvsm1"))
                    vector.tensor_copy(BPOOL[:, BXB:BXB + NCH * TL],
                                       FPOOL[:, FXB:FXB + NCH * TL])
                    inc(vector, "ve"); snap(P + "xb", "ve")
                block.vector(_v1)
                def _a1(scalar, j=j, XSQR=XSQR):
                    scalar.wait_ge(sem["qx" if j % 2 == 0 else "qx1"], at(P + "x"))
                    if hoisted:
                        scalar.wait_ge(sem["pe"], at("g_lastvsm0"))
                        scalar.wait_ge(sem["pe"], at("g_lastvsm1"))
                    scalar.activation(BPOOL[:, XSQR:XSQR + NCH * TL],
                                      FPOOL[:, FXB:FXB + NCH * TL], AF.Square)
                    inc(scalar, "ac"); snap(P + "xsq", "ac")
                block.scalar(_a1)
                def _t1(tensor, j=j, tag=tag, XSQR=XSQR):
                    tensor.wait_ge(sem["ve"], at(P + "xb"))
                    tensor.wait_ge(sem["ac"], at(P + "xsq"))
                    if j > 0 and not hoisted:
                        tensor.wait_ge(sem["ve"], at(tag + str(j - 1) + ":out01"))
                    if hoisted:
                        tensor.wait_ge(sem["ve"], at(tag + str(j - 1) + ":xpe%d" % (NCH - 1)))
                    xb = cap(BPOOL, BXB, NCH, TL)
                    xq = cap(BPOOL, XSQR, NCH, TL)
                    for c in range(NCH):
                        tensor.matmul(PS[0][:, 0:TL], ones_b[:], xb[:, c],
                                      start=(c == 0), stop=(c == NCH - 1))
                    for c in range(NCH):
                        tensor.matmul(PS[1][:, 0:TL], ones_b[:], xq[:, c],
                                      start=(c == 0), stop=(c == NCH - 1))
                    inc(tensor, "pe"); snap(P + "st", "pe")
                block.tensor(_t1)
                def _v2(vector):
                    vector.wait_ge(sem["pe"], at(P + "st"))
                    vector.tensor_scalar(SPOOL[:, SS_MU:SS_MU + TL], PS[0][:, 0:TL],
                                         1.0 / D, None, OP.mult)
                    vector.tensor_scalar(SPOOL[:, SS_T:SS_T + TL], PS[1][:, 0:TL],
                                         1.0 / D, None, OP.mult)
                    vector.drain()
                    vector.tensor_tensor(SPOOL[:, SS_SQ:SS_SQ + TL],
                                         SPOOL[:, SS_MU:SS_MU + TL],
                                         SPOOL[:, SS_MU:SS_MU + TL], OP.mult)
                    vector.drain()
                    vector.tensor_tensor(SPOOL[:, SS_SUM:SS_SUM + TL],
                                         SPOOL[:, SS_T:SS_T + TL],
                                         SPOOL[:, SS_SQ:SS_SQ + TL], OP.subtract)
                    inc(vector, "ve"); snap(P + "var", "ve")
                block.vector(_v2)
                def _a2(scalar):
                    scalar.wait_ge(sem["ve"], at(P + "var"))
                    scalar.activation(SPOOL[:, SS_T:SS_T + TL],
                                      SPOOL[:, SS_SUM:SS_SUM + TL], AF.Ln, bias=cb[:, 0:1])
                    scalar.drain()
                    scalar.activation(SPOOL[:, SS_RSTD:SS_RSTD + TL],
                                      SPOOL[:, SS_T:SS_T + TL], AF.Exp, scale=-0.5)
                    inc(scalar, "ac"); snap(P + "rstd", "ac")
                block.scalar(_a2)
                def _v3(vector):
                    vector.wait_ge(sem["ac"], at(P + "rstd"))
                    vector.tensor_tensor(SPOOL[:, SS_T:SS_T + TL],
                                         SPOOL[:, SS_SUM:SS_SUM + TL],
                                         SPOOL[:, SS_RSTD:SS_RSTD + TL], OP.mult)
                    vector.drain()
                    vector.tensor_tensor(SPOOL[:, SS_T:SS_T + TL],
                                         SPOOL[:, SS_T:SS_T + TL],
                                         SPOOL[:, SS_RSTD:SS_RSTD + TL], OP.mult)
                    inc(vector, "ve"); snap(P + "tt", "ve")
                block.vector(_v3)
                def _a3(scalar):
                    scalar.wait_ge(sem["ve"], at(P + "tt"))
                    scalar.activation(SPOOL[:, SS_T:SS_T + TL],
                                      SPOOL[:, SS_T:SS_T + TL], AF.Ln, bias=cb[:, 1:2])
                    scalar.drain()
                    scalar.activation(SPOOL[:, SS_C1:SS_C1 + TL],
                                      SPOOL[:, SS_T:SS_T + TL], AF.Exp, scale=-0.5)
                    inc(scalar, "ac"); snap(P + "r1", "ac")
                block.scalar(_a3)
                def _v4a(vector):
                    vector.wait_ge(sem["ac"], at(P + "r1"))
                    vector.tensor_tensor(SPOOL[:, SS_C1:SS_C1 + TL],
                                         SPOOL[:, SS_C1:SS_C1 + TL],
                                         SPOOL[:, SS_RSTD:SS_RSTD + TL], OP.mult)
                    inc(vector, "ve"); snap(P + "c1", "ve")
                block.vector(_v4a)
                if not apply_now:
                    return
                emit_lnapply(j)

            def emit_lnapply(j):
                P = tag + str(j) + ":"
                def _v4(vector, j=j, tag=tag):
                    vector.wait_ge(sem["ve"], at(P + "c1"))
                    for c in range(NCH):
                        vector.tensor_tensor(PS[c][:, 0:TL],
                                             FPOOL[:, FXB + c * TL:FXB + (c + 1) * TL],
                                             SPOOL[:, SS_MU:SS_MU + TL], OP.subtract)
                    vector.drain()
                    for c in range(NCH):
                        vector.tensor_tensor(FPOOL[:, FLN1 + c * TL:FLN1 + (c + 1) * TL],
                                             PS[c][:, 0:TL],
                                             SPOOL[:, SS_RSTD:SS_RSTD + TL], OP.mult)
                        vector.tensor_tensor(BPOOL[:, BXB + c * TL:BXB + (c + 1) * TL],
                                             PS[c][:, 0:TL],
                                             SPOOL[:, SS_C1:SS_C1 + TL], OP.mult)
                    inc(vector, "ve"); snap(P + "hq", "ve")
                block.vector(_v4)

            frontend(0, False)
            for j in range(NT):
                t0 = j * TL
                P = tag + str(j) + ":"
                QROT = [0, 1, 5, 6, 7, 0, 1, 5]
                for hp in range(NCH):
                    def _tq(tensor, hp=hp):
                        tensor.wait_ge(sem["ve"], at(P + "hq"))
                        if hp >= 5:
                            tensor.wait_ge(sem["ac"], at(P + "qe%d" % (hp - 5)))
                        hq = cap(BPOOL, BXB, NCH, TL)
                        for c in range(NCH):
                            tensor.matmul(PS[QROT[hp]][:, 0:TL],
                                          WPOOL[:, WQ_O + c * D + hp * 128:
                                                WQ_O + c * D + (hp + 1) * 128],
                                          hq[:, c], start=(c == 0), stop=(c == NCH - 1))
                        inc(tensor, "pe"); snap(P + "q%d" % hp, "pe")
                    block.tensor(_tq)
                    def _aq(scalar, hp=hp):
                        scalar.wait_ge(sem["pe"], at(P + "q%d" % hp))
                        scalar.activation(BPOOL[:, BSQ + hp * TL: BSQ + (hp + 1) * TL],
                                          PS[QROT[hp]][:, 0:TL], AF.Identity,
                                          bias=vecs["bq"][:, hp:hp + 1])
                        inc(scalar, "ac"); snap(P + "qe%d" % hp, "ac")
                    block.scalar(_aq)

                def _td(tensor):
                    hq = cap(BPOOL, BXB, NCH, TL)
                    for c in range(NCH):
                        tensor.matmul(PS[2][0:LR, 0:TL],
                                      WPOOL[:, WD_O + c * LR: WD_O + (c + 1) * LR],
                                      hq[:, c], start=(c == 0), stop=(c == NCH - 1))
                    inc(tensor, "pe"); snap(P + "dv", "pe")
                block.tensor(_td)
                def _ad(scalar):
                    scalar.wait_ge(sem["pe"], at(P + "dv"))
                    scalar.activation(SMALLB[0:LR, 2560:2560 + TL],
                                      PS[2][0:LR, 0:TL], AF.Identity,
                                      bias=v_bdown[:, 0:1])
                    inc(scalar, "ac"); snap(P + "dve", "ac")
                block.scalar(_ad)

                def _dst0(par):
                    return BPOOL[0:64, BQT:BQT + H * HT].rearrange(
                        "p (G q j t) -> p q j G t", G=NG, q=2, j=8, t=8)[:, par]

                def _qs0(par):
                    return [BPOOL[par * 64:par * 64 + 64,
                                  BSQ + jj * TL:BSQ + jj * TL + HT]
                            for jj in range(8)]

                def _q0(sync, P=P):
                    sync.wait_ge(sem["ac"], at(P + "qe%d" % (NCH - 1)))
                    sync.wait_ge(sem["pe"], at("g_lastvsm0"))
                    if DEBUG_TAPS:
                        sync.wait_ge(sem["qst"], tot.get("qst", 0))
                    for jj in range(8):
                        sync.dma_start(out=_dst0(0)[:, jj], in_=_qs0(0)[jj]
                                       ).then_inc(sem["qsc"], 16)
                        bump("qsc", 16)
                block.sync(_q0)
                snap(P + "h0dt_scalar", "qsc")

                def _q1(gpsimd, P=P):
                    gpsimd.wait_ge(sem["ac"], at(P + "qe%d" % (NCH - 1)))
                    gpsimd.wait_ge(sem["pe"], at("g_lastvsm0"))
                    if DEBUG_TAPS:
                        gpsimd.wait_ge(sem["qst"], tot.get("qst", 0))
                    for jj in range(8):
                        gpsimd.dma_start(out=_dst0(1)[:, jj], in_=_qs0(1)[jj]
                                         ).then_inc(sem["qsd0"], 16)
                        bump("qsd0", 16)
                block.gpsimd(_q1)
                snap(P + "h0dt_gpsimd", "qsd0")

                for g in range(2 * NCH):
                    def _tk(tensor, g=g):
                        tensor.wait_ge(sem["ac"], at(P + "dve"))
                        if g >= 6:
                            tensor.wait_ge(sem["ac"], at(P + "kve%d" % (g - 6)))
                        tensor.matmul(PS[2 + g % 6][:, 0:TL],
                                      WPOOL[0:LR, WU_O + g * 128: WU_O + (g + 1) * 128],
                                      SMALLB[0:LR, 2560:2560 + TL],
                                      start=True, stop=True)
                        inc(tensor, "pe"); snap(P + "kv%d" % g, "pe")
                    block.tensor(_tk)
                    def _ak(scalar, g=g):
                        scalar.wait_ge(sem["pe"], at(P + "kv%d" % g))
                        scalar.activation(BPOOL[:, BKV + g * TL: BKV + (g + 1) * TL],
                                          PS[2 + g % 6][:, 0:TL], AF.Identity,
                                          bias=v_bup[:, g:g + 1])
                        inc(scalar, "ac"); snap(P + "kve%d" % g, "ac")
                        if g % 2 == 1:
                            if g == 1:
                                scalar.wait_ge(sem["pe"], at("g_lastvsm0"))
                                if DEBUG_TAPS:
                                    scalar.wait_ge(sem["qst"], tot.get("qst", 0))
                            dvx = BPOOL[0:128, BKVT:BKVT + H * HT].rearrange(
                                "p (G q j t) -> p q j G t", G=NG, q=2, j=8, t=8)
                            scalar.dma_start(out=dvx[:, g % 2, g // 2],
                                             in_=BPOOL[0:128, BKV + g * TL:
                                                       BKV + g * TL + HT]
                                             ).then_inc(sem["qsv"], 16)
                            bump("qsv", 16)
                    block.scalar(_ak)

                def _kvd(sync, P=P):
                    sync.wait_ge(sem["pe"], at("g_lastvsm0"))
                    if DEBUG_TAPS:
                        sync.wait_ge(sem["qst"], tot.get("qst", 0))
                    dv0 = BPOOL[0:128, BKVT:BKVT + H * HT].rearrange(
                        "p (G q j t) -> p q j G t", G=NG, q=2, j=8, t=8)
                    for g in range(0, 2 * NCH, 2):
                        sync.wait_ge(sem["ac"], at(P + "kve%d" % g))
                        sync.dma_start(out=dv0[:, g % 2, g // 2],
                                       in_=BPOOL[0:128, BKV + g * TL:
                                                 BKV + g * TL + HT]
                                       ).then_inc(sem["qsv"], 16)
                        bump("qsv", 16)
                block.sync(_kvd)
                snap(P + "h0dt_sync", "qsv")

                if DEBUG_TAPS and tag == "A" and j == 0:
                    def _tapkv(sync):
                        sync.wait_ge(sem["ac"], at(P + "kve%d" % (2 * NCH - 1)))
                        sync.dma_start(out=tap_kv, in_=BPOOL[:, BKV:BKV + 8192]
                                       ).then_inc(sem["qst"], 16); bump("qst", 16)
                    block.sync(_tapkv)
                QTB = [BQT, BXB]
                KVB = [BKVT, BXB + 2048]
                rs_names = {"scalar": ["qsc", "qsc1"], "gpsimd": ["qsd0", "qsd1"],
                            "sync": ["qsv", "qsv1"]}

                def emit_dt(s):
                    ht0 = s * HT
                    buf = dtctr[0] % 2
                    dtctr[0] += 1
                    PH = P + "h%d" % s
                    qt_o, kv_o = QTB[buf], KVB[buf]

                    def dstap(base, par):
                        return BPOOL[0:64, base:base + H * HT].rearrange(
                            "p (G q j t) -> p q j G t", G=NG, q=2, j=8, t=8)[:, par]

                    def qsrc(par):
                        return [BPOOL[par * 64:par * 64 + 64,
                                      BSQ + jj * TL + ht0:BSQ + jj * TL + ht0 + HT]
                                for jj in range(8)]

                    def kvsrc(par):
                        return [BPOOL[0:128, BKV + (2 * jj + par) * TL + ht0:
                                      BKV + (2 * jj + par) * TL + ht0 + HT]
                                for jj in range(8)]

                    def emit_ring(eng_name, dst, srcs, issue=None):
                        sname = rs_names[eng_name][buf]
                        def _r(eng, dst=dst, srcs=srcs, sname=sname):
                            eng.wait_ge(sem["ac"], at(P + "qe%d" % (NCH - 1)))
                            if buf == 1:
                                eng.wait_ge(sem["ac"], at(P + "kve%d" % (2 * NCH - 1)))
                            eng.wait_ge(sem["pe"], at("g_lastvsm%d" % buf))
                            if DEBUG_TAPS:
                                eng.wait_ge(sem["qst"], tot.get("qst", 0))
                            for jj in range(8):
                                eng.dma_start(out=dst[:, jj], in_=srcs[jj]
                                              ).then_inc(sem[sname], 16)
                                bump(sname, 16)
                        getattr(block, issue or eng_name)(_r)
                        snap(PH + "dt_" + eng_name, sname)

                    if s != 0:
                        emit_ring("scalar", dstap(qt_o, 0), qsrc(0), issue="sync")
                        emit_ring("gpsimd", dstap(qt_o, 1), qsrc(1))
                        sname = rs_names["sync"][buf]

                        def _kv(sync, kv_o=kv_o, buf=buf, sname=sname):
                            sync.wait_ge(sem["ac"], at(P + "kve%d" % (2 * NCH - 1)))
                            sync.wait_ge(sem["pe"], at("g_lastvsm%d" % buf))
                            dv = BPOOL[0:128, kv_o:kv_o + H * HT].rearrange(
                                "p (G q j t) -> p q j G t", G=NG, q=2, j=8, t=8)
                            ss = kvsrc(0)
                            for jj in range(8):
                                sync.dma_start(out=dv[:, 0, jj], in_=ss[jj]
                                               ).then_inc(sem[sname], 16)
                                bump(sname, 16)
                        block.sync(_kv)

                        def _kv2(scalar, kv_o=kv_o, buf=buf, sname=sname):
                            scalar.wait_ge(sem["ac"], at(P + "kve%d" % (2 * NCH - 1)))
                            scalar.wait_ge(sem["pe"], at("g_lastvsm%d" % buf))
                            dv = BPOOL[0:128, kv_o:kv_o + H * HT].rearrange(
                                "p (G q j t) -> p q j G t", G=NG, q=2, j=8, t=8)
                            ss = kvsrc(1)
                            for jj in range(8):
                                scalar.dma_start(out=dv[:, 1, jj], in_=ss[jj]
                                                 ).then_inc(sem[sname], 16)
                                bump(sname, 16)
                        block.scalar(_kv2)
                        snap(PH + "dt_sync", sname)
                    return buf

                dt_buf = {}
                dt_buf[0] = emit_dt(0)
                if FT // HT > 1:
                    dt_buf[1] = emit_dt(1)
                EMB = FPOOL[:, 12288:12800].bitcast(BF16)
                bundles = []
                NHALF = FT // HT
                for half in range(NHALF):
                    PH = P + "h%d" % half
                    for mb in range(NMB):
                        m = half * NMB + mb
                        g0 = mb * 8
                        PM = PH + "m%d" % mb
                        mlist.append(PM)
                        prev1 = mlist[-2] if len(mlist) >= 2 else None
                        prev2 = mlist[-3] if len(mlist) >= 3 else None
                        p = mb % 2
                        E2 = 0 if p == 0 else 3072
                        VSo = 2048 if p == 0 else 2560
                        QB = (PS[2], PS[3]) if p == 0 else (PS[0], PS[1])
                        em_full = SMALLB[:, 1024:2048] if p == 0 else EMB
                        srs_full = (SPOOL[:, SRS:SRS + 1024] if p == 0
                                    else FPOOL[:, 12800:13824])
                        buf = dt_buf[half]

                        def _ta(tensor, g0=g0, mb=mb, PM=PM, PH=PH, buf=buf,
                                QB=QB, prev1=prev1, prev2=prev2):
                            if mb == 0:
                                for en in ("scalar", "gpsimd", "sync"):
                                    tensor.wait_ge(sem[rs_names[en][buf]],
                                                   at(PH + "dt_" + en))
                            if prev2:
                                tensor.wait_ge(sem["ac"], at(prev2 + "e2"))
                            qt_o, kv_o = QTB[buf], KVB[buf]
                            for gi in range(8):
                                g = g0 + gi
                                bank = QB[0] if gi < 4 else QB[1]
                                tensor.matmul(bank[:, (gi % 4) * 128:(gi % 4 + 1) * 128],
                                              BPOOL[0:64, kv_o + g * 128:kv_o + (g + 1) * 128],
                                              BPOOL[0:64, qt_o + g * 128:qt_o + (g + 1) * 128],
                                              start=True, stop=True)
                            inc(tensor, "pe"); snap(PM + "att", "pe")
                            if prev1:
                                tensor.wait_ge(sem["ve"], at(prev1 + "vse"))
                            for gi in range(8):
                                g = g0 + gi
                                tensor.matmul(PS[4][:, gi * 64:(gi + 1) * 64],
                                              BPOOL[64:128, kv_o + g * 128:kv_o + (g + 1) * 128],
                                              id64_b[64:128, :],
                                              start=True, stop=True)
                            inc(tensor, "pe"); snap(PM + "vsm", "pe")
                            snap("g_lastvsm%d" % buf, "pe")

                        def _ae(scalar, PM=PM, QB=QB, E2=E2, prev2=prev2):
                            scalar.wait_ge(sem["pe"], at(PM + "att"))
                            if prev2:
                                scalar.wait_ge(sem["pe"], at(prev2 + "av"))
                            scalar.activation(SMALLB[:, E2:E2 + 512], QB[0][:, 0:512],
                                              AF.Exp, scale=0.125)
                            scalar.activation(SMALLB[:, E2 + 512:E2 + 1024],
                                              QB[1][:, 0:512], AF.Exp, scale=0.125)
                            inc(scalar, "ac"); snap(PM + "e2", "ac")

                        def _av(vector, PM=PM, VSo=VSo, prev2=prev2):
                            vector.wait_ge(sem["pe"], at(PM + "vsm"))
                            if prev2:
                                vector.wait_ge(sem["pe"], at(prev2 + "av"))
                            vector.tensor_copy(SMALLB[:, VSo:VSo + 512],
                                               PS[4][:, 0:512])
                            inc(vector, "ve"); snap(PM + "vse", "ve")

                        def _gm(gpsimd, PM=PM, E2=E2, em_full=em_full, prev2=prev2):
                            gpsimd.wait_ge(sem["ac"], at(PM + "e2"))
                            if prev2:
                                gpsimd.wait_ge(sem["pe"], at(prev2 + "s"))
                                gpsimd.wait_ge(sem["ve"], at(prev2 + "e2n"))
                            mb_ = mask_b[:, :].rearrange("p (a t) -> p a t", a=4, t=128
                                                         ).unsqueeze(1).broadcast_to(
                                                             [128, 2, 4, 128])
                            e2v = SMALLB[:, E2:E2 + 1024].rearrange(
                                "p (b a t) -> p b a t", b=2, a=4, t=128)
                            emv = em_full.rearrange(
                                "p (b a t) -> p b a t", b=2, a=4, t=128)
                            gpsimd.tensor_tensor(emv, e2v, mb_, OP.mult)
                            gpsimd.drain()
                            gpsimd.sem_inc(sem["gp"], 1)
                            bump("gp", 1)
                            snap(PM + "em", "gp")

                        def _ts(tensor, PM=PM, em_full=em_full, prev1=prev1):
                            tensor.wait_ge(sem["gp"], at(PM + "em"))
                            if prev1:
                                tensor.wait_ge(sem["ac"], at(prev1 + "rs"))
                            tensor.matmul(PS[5][:, 0:512], ones_b[:],
                                          em_full[:, 0:512], start=True, stop=True)
                            tensor.matmul(PS[7][:, 0:512], ones_b[:],
                                          em_full[:, 512:1024], start=True, stop=True)
                            inc(tensor, "pe"); snap(PM + "s", "pe")

                        def _ar(scalar, PM=PM, srs_full=srs_full, prev2=prev2):
                            scalar.wait_ge(sem["pe"], at(PM + "s"))
                            if prev2:
                                scalar.wait_ge(sem["ve"], at(prev2 + "e2n"))
                            scalar.activation(SPOOL[:, SLS:SLS + 512], PS[5][:, 0:512], AF.Ln)
                            scalar.activation(SPOOL[:, SLS + 512:SLS + 1024],
                                              PS[7][:, 0:512], AF.Ln)
                            scalar.drain()
                            scalar.activation(srs_full,
                                              SPOOL[:, SLS:SLS + 1024], AF.Exp, scale=-1.0)
                            inc(scalar, "ac"); snap(PM + "rs", "ac")

                        def _vn(vector, PM=PM, E2=E2, em_full=em_full,
                                srs_full=srs_full):
                            vector.wait_ge(sem["ac"], at(PM + "rs"))
                            vector.wait_ge(sem["gp"], at(PM + "em"))
                            vector.tensor_tensor(SMALLB[:, E2:E2 + 1024],
                                                 em_full, srs_full, OP.mult)
                            inc(vector, "ve"); snap(PM + "e2n", "ve")

                        def _tv(tensor, g0=g0, PM=PM, E2=E2, VSo=VSo, prev1=prev1):
                            tensor.wait_ge(sem["ve"], at(PM + "e2n"))
                            if prev1:
                                tensor.wait_ge(sem["ve"], at(prev1 + "o2"))
                            for gi in range(8):
                                rhs_e = SMALLB[:, E2 + gi * 128:E2 + gi * 128 + 64]
                                rhs_o = SMALLB[:, E2 + gi * 128 + 64:E2 + (gi + 1) * 128]
                                vs = SMALLB[:, VSo + gi * 64:VSo + (gi + 1) * 64]
                                tensor.matmul(PS[6][0:64, gi * 64:(gi + 1) * 64],
                                              vs, rhs_e, start=True, stop=True)
                                tensor.matmul(PS[6][64:128, gi * 64:(gi + 1) * 64],
                                              vs, rhs_o, start=True, stop=True,
                                              tile_position=(0, 64))
                            inc(tensor, "pe"); snap(PM + "av", "pe")

                        def _ao(vector, g0=g0, half=half, PM=PM):
                            vector.wait_ge(sem["pe"], at(PM + "av"))
                            g_glob = half * NG + g0
                            dst = BPOOL[:, BO2:BO2 + 8 * TL].rearrange(
                                "p (j G t) -> p G j t", j=8, G=TL // 8, t=8)[
                                :, g_glob:g_glob + 8]
                            srcv = PS[6][:, 0:512].rearrange(
                                "p (G j t) -> p G j t", G=8, j=8, t=8)
                            vector.tensor_copy(dst, srcv)
                            inc(vector, "ve"); snap(PM + "o2", "ve")

                        bundles.append({"ta": _ta, "ae": _ae, "gm": _gm,
                                        "ts": _ts, "ar": _ar, "vn": _vn,
                                        "tv": _tv, "ao": _ao, "av": _av})
                        block.tensor(_ta)
                        if m >= 1:
                            block.tensor(bundles[m - 1]["ts"])
                        if m >= 2:
                            block.tensor(bundles[m - 2]["tv"])
                        block.scalar(_ae)
                        if m >= 1:
                            block.scalar(bundles[m - 1]["ar"])
                        block.gpsimd(_gm)
                        block.vector(_av)
                        if m >= 2:
                            block.vector(bundles[m - 2]["ao"])
                        if m >= 1:
                            block.vector(bundles[m - 1]["vn"])
                        if mb == 1 and half + 2 < NHALF:
                            dt_buf[half + 2] = emit_dt(half + 2)
                M = NHALF * NMB
                block.tensor(bundles[M - 1]["ts"])
                block.scalar(bundles[M - 1]["ar"])
                block.tensor(bundles[M - 2]["tv"])
                block.vector(bundles[M - 2]["ao"])
                block.vector(bundles[M - 1]["vn"])
                block.tensor(bundles[M - 1]["tv"])
                block.vector(bundles[M - 1]["ao"])
                snap(P + "h%do2last" % (NHALF - 1), "ve")

                for fc in range(NCH):
                    def _tw(tensor, fc=fc):
                        tensor.wait_ge(sem["ve"], at(P + "h%do2last" % (FT // HT - 1)))
                        if fc >= 2:
                            tensor.wait_ge(sem["ve"], at(P + "xpe%d" % (fc - 2)))
                        for jj in range(8):
                            tensor.matmul(PS[fc % 2][:, 0:TL],
                                          WPOOL[:, WO_O + jj * D + fc * 128:
                                                WO_O + jj * D + (fc + 1) * 128],
                                          BPOOL[:, BO2 + jj * TL:BO2 + (jj + 1) * TL],
                                          start=(jj == 0), stop=(jj == 7))
                        inc(tensor, "pe"); snap(P + "wo%d" % fc, "pe")
                    block.tensor(_tw)
                    def _vw(vector, fc=fc, j=j):
                        vector.wait_ge(sem["pe"], at(P + "wo%d" % fc))
                        if fc == 0 and j > 0:
                            vector.wait_ge(sem["qst"],
                                           at(tag + str(j - 1) + ":stored"))
                        vector.scalar_tensor_tensor(
                            FPOOL[:, FXP + fc * TL: FXP + (fc + 1) * TL],
                            PS[fc % 2][:, 0:TL], vecs["bo"][:, fc:fc + 1],
                            FPOOL[:, FLN1 + fc * TL: FLN1 + (fc + 1) * TL],
                            OP.add, OP.add)
                        inc(vector, "ve"); snap(P + "xpe%d" % fc, "ve")
                    block.vector(_vw)

                if DEBUG_TAPS and tag == "A" and j == 0:
                    def _tapxp(sync):
                        sync.wait_ge(sem["ve"], at(P + "xpe%d" % (NCH - 1)))
                        sync.dma_start(out=tap_xp, in_=FPOOL[:, FXP:FXP + 4096]
                                       ).then_inc(sem["qst"], 16); bump("qst", 16)
                    block.sync(_tapxp)
                def _a4(scalar):
                    for c in range(NCH):
                        scalar.wait_ge(sem["ve"], at(P + "xpe%d" % c))
                        scalar.activation(BPOOL[:, BSQ + c * TL:BSQ + (c + 1) * TL],
                                          FPOOL[:, FXP + c * TL:FXP + (c + 1) * TL],
                                          AF.Square)
                        inc(scalar, "ac"); snap(P + "sq2c%d" % c, "ac")
                    snap(P + "sq2", "ac")
                block.scalar(_a4)
                def _vxh(vector):
                    vector.wait_ge(sem["ve"], at(P + "xpe%d" % (NCH - 1)))
                    vector.scalar_tensor_tensor(
                        cap(FPOOL, FXH, NCH, TL), cap(FPOOL, FXP, NCH, TL), 0.5,
                        cap(FPOOL, FXB, NCH, TL), OP.mult, OP.add)
                    inc(vector, "ve"); snap(P + "xh", "ve")
                block.vector(_vxh)
                def _t1b(tensor):
                    xq = cap(BPOOL, BSQ, NCH, TL)
                    for c in range(NCH):
                        tensor.wait_ge(sem["ac"], at(P + "sq2c%d" % c))
                        tensor.matmul(PS[2][:, 0:TL], ones_b[:], xq[:, c],
                                      start=(c == 0), stop=(c == NCH - 1))
                    inc(tensor, "pe"); snap(P + "st2", "pe")
                block.tensor(_t1b)
                def _v5(vector):
                    vector.wait_ge(sem["pe"], at(P + "st2"))
                    vector.tensor_scalar(SPOOL[:, SS_T:SS_T + TL], PS[2][:, 0:TL],
                                         1.0 / D, None, OP.mult)
                    inc(vector, "ve"); snap(P + "m2", "ve")
                block.vector(_v5)
                def _a5(scalar):
                    scalar.wait_ge(sem["ve"], at(P + "m2"))
                    scalar.activation(SPOOL[:, SS_T:SS_T + TL],
                                      SPOOL[:, SS_T:SS_T + TL], AF.Ln, bias=cb[:, 1:2])
                    scalar.drain()
                    scalar.activation(SPOOL[:, SS_R2:SS_R2 + TL],
                                      SPOOL[:, SS_T:SS_T + TL], AF.Exp, scale=-0.5)
                    inc(scalar, "ac"); snap(P + "r2", "ac")
                block.scalar(_a5)
                def _v6(vector):
                    vector.wait_ge(sem["ac"], at(P + "r2"))
                    r2b = SPOOL[:, SS_R2:SS_R2 + TL].unsqueeze(1).broadcast_to([128, NCH, TL])
                    vector.tensor_tensor(cap(BPOOL, BSQ, NCH, TL),
                                         cap(FPOOL, FXP, NCH, TL), r2b, OP.mult)
                    inc(vector, "ve"); snap(P + "h2", "ve")
                block.vector(_v6)

                def _tg(tensor):
                    tensor.wait_ge(sem["ve"], at(P + "h2"))
                    h2 = cap(BPOOL, BSQ, NCH, TL)
                    for c in range(NCH):
                        tensor.matmul(PS[3][0:NE, 0:TL],
                                      WPOOL[:, WG_O + c * NE: WG_O + (c + 1) * NE],
                                      h2[:, c], start=(c == 0), stop=(c == NCH - 1))
                    inc(tensor, "pe"); snap(P + "gz", "pe")
                block.tensor(_tg)
                def _ag(scalar):
                    scalar.wait_ge(sem["pe"], at(P + "gz"))
                    scalar.activation(SMALLB[0:NE, 3072:3072 + TL],
                                      PS[3][0:NE, 0:TL], AF.Exp, bias=v_bg[:, 0:1])
                    inc(scalar, "ac"); snap(P + "eg", "ac")
                block.scalar(_ag)
                def _tgs(tensor):
                    tensor.wait_ge(sem["ac"], at(P + "eg"))
                    tensor.matmul(PS[3][:, 0:TL], ones_b[0:NE, :],
                                  SMALLB[0:NE, 3072:3072 + TL], start=True, stop=True)
                    inc(tensor, "pe"); snap(P + "gs", "pe")
                block.tensor(_tgs)
                def _agr(scalar):
                    scalar.wait_ge(sem["pe"], at(P + "gs"))
                    scalar.activation(SPOOL[:, SGRB:SGRB + TL], PS[3][:, 0:TL], AF.Ln)
                    scalar.drain()
                    scalar.activation(SPOOL[:, SGRB:SGRB + TL],
                                      SPOOL[:, SGRB:SGRB + TL], AF.Exp, scale=-1.0)
                    inc(scalar, "ac"); snap(P + "gr", "ac")
                block.scalar(_agr)
                def _vgn(vector):
                    vector.wait_ge(sem["ac"], at(P + "gr"))
                    vector.tensor_tensor(SMALLB[0:NE, 3584:3584 + TL],
                                         SMALLB[0:NE, 3072:3072 + TL],
                                         SPOOL[0:NE, SGRB:SGRB + TL], OP.mult)
                    inc(vector, "ve"); snap(P + "gn", "ve")
                block.vector(_vgn)
                for e in range(NE):
                    def _tgb(tensor, e=e):
                        tensor.wait_ge(sem["ve"], at(P + "gn"))
                        if e > 0:
                            tensor.wait_ge(sem["ac"], at(P + "gbe%d" % (e - 1)))
                        tensor.matmul(PS[4][:, 0:TL],
                                      esel_b[:, e * 128:(e + 1) * 128],
                                      SMALLB[0:NE, 3584:3584 + TL],
                                      start=True, stop=True)
                        inc(tensor, "pe"); snap(P + "gb%d" % e, "pe")
                    block.tensor(_tgb)
                    def _agb(scalar, e=e):
                        scalar.wait_ge(sem["pe"], at(P + "gb%d" % e))
                        if DEBUG_TAPS:
                            scalar.wait_ge(sem["qst"], tot.get("qst", 0))
                        scalar.activation(BPOOL[:, BGC + e * TL: BGC + (e + 1) * TL],
                                          PS[4][:, 0:TL], AF.Copy)
                        inc(scalar, "ac"); snap(P + "gbe%d" % e, "ac")
                    block.scalar(_agb)

                if j + 1 < NT:
                    frontend(j + 1, True, apply_now=False)

                steps = [(e, hh) for e in range(NE) for hh in range(2)]
                for si, (e, hh) in enumerate(steps):
                    buf = si % 2
                    def _swe(sync, e=e, hh=hh, si=si, buf=buf):
                        if si >= 2:
                            sync.wait_ge(sem["pe"], at(P + "ex%d" % (si - 2)))
                        else:
                            sync.wait_ge(sem["qst"], at("p_west%d" % (NE - 1)))
                        qs = "qwa" if buf == 0 else "qwb"
                        src = we_dram[e].rearrange("p (c f) -> p c f", c=NCH, f=D)[
                            :, :, hh * 512:(hh + 1) * 512]
                        sync.dma_start(out=WPOOL[:, WE_O + buf * 4096:
                                                 WE_O + buf * 4096 + 4096],
                                       in_=src).then_inc(sem[qs], 16); bump(qs, 16)
                        snap(P + "we%d" % si, qs)
                    block.sync(_swe)
                    if hh == 0:
                        def _vxe(vector, e=e):
                            vector.wait_ge(sem["ac"], at(P + "gbe%d" % e))
                            if DEBUG_TAPS:
                                vector.wait_ge(sem["qst"], tot.get("qst", 0))
                            if e >= 2:
                                vector.wait_ge(sem["pe"], at(P + "ex%d" % (2 * e - 3)))
                            gb = BPOOL[:, BGC + e * TL: BGC + (e + 1) * TL
                                       ].unsqueeze(1).broadcast_to([128, NCH, TL])
                            xeo = BXEA if e % 2 == 0 else BXEB
                            vector.tensor_tensor(cap(BPOOL, xeo, NCH, TL),
                                                 cap(BPOOL, BSQ, NCH, TL), gb, OP.mult)
                            inc(vector, "ve"); snap(P + "xe%d" % e, "ve")
                        block.vector(_vxe)
                    def _tex(tensor, e=e, hh=hh, si=si, buf=buf):
                        tensor.wait_ge(sem["ve"], at(P + "xe%d" % e))
                        tensor.wait_ge(sem["qwa" if buf == 0 else "qwb"],
                                       at(P + "we%d" % si))
                        if si == 0:
                            tensor.wait_ge(sem["ve"], at(P + "xh"))
                            tensor.wait_ge(sem["ac"], at(P + "gbe%d" % (NE - 1)))
                            if j + 1 < NT:
                                tensor.wait_ge(sem["ve"],
                                               at(tag + str(j + 1) + ":var"))
                        xeo = BXEA if e % 2 == 0 else BXEB
                        xe = cap(BPOOL, xeo, NCH, TL)
                        for fi in range(4):
                            fc = hh * 4 + fi
                            for c in range(NCH):
                                tensor.matmul(
                                    PS[fc][:, 0:TL],
                                    WPOOL[:, WE_O + buf * 4096 + c * 512 + fi * 128:
                                          WE_O + buf * 4096 + c * 512 + (fi + 1) * 128],
                                    xe[:, c],
                                    start=(e == 0 and c == 0),
                                    stop=(e == NE - 1 and c == NCH - 1))
                        inc(tensor, "pe"); snap(P + "ex%d" % si, "pe")
                    block.tensor(_tex)

                def _vout(vector):
                    vector.wait_ge(sem["pe"], at(P + "ex%d" % (2 * NE - 1)))
                    for fc in range(2):
                        vector.scalar_tensor_tensor(
                            FPOOL[:, FT1 + fc * TL: FT1 + (fc + 1) * TL],
                            PS[fc][:, 0:TL], 0.5,
                            FPOOL[:, FXH + fc * TL: FXH + (fc + 1) * TL],
                            OP.mult, OP.add)
                    inc(vector, "ve"); snap(P + "out01", "ve")
                    for fc in range(2, NCH):
                        vector.scalar_tensor_tensor(
                            FPOOL[:, FT1 + fc * TL: FT1 + (fc + 1) * TL],
                            PS[fc][:, 0:TL], 0.5,
                            FPOOL[:, FXH + fc * TL: FXH + (fc + 1) * TL],
                            OP.mult, OP.add)
                    inc(vector, "ve"); snap(P + "out", "ve")
                block.vector(_vout)
                def _sst(sync, t0=t0):
                    sync.wait_ge(sem["ve"], at(P + "out"))
                    for c in range(0, NCH, 2):
                        sync.dma_start(out=dst_dram[c * 128:(c + 1) * 128, t0:t0 + TL],
                                       in_=FPOOL[:, FT1 + c * TL: FT1 + (c + 1) * TL]
                                       ).then_inc(sem["qst"], 16); bump("qst", 16)
                block.sync(_sst)
                def _sst2(scalar, t0=t0):
                    scalar.wait_ge(sem["ve"], at(P + "out"))
                    for c in range(1, NCH, 2):
                        scalar.dma_start(out=dst_dram[c * 128:(c + 1) * 128, t0:t0 + TL],
                                         in_=FPOOL[:, FT1 + c * TL: FT1 + (c + 1) * TL]
                                         ).then_inc(sem["qst"], 16); bump("qst", 16)
                    snap(P + "stored", "qst")
                block.scalar(_sst2)
                if j + 1 < NT:
                    emit_lnapply(j + 1)


        def attn_phase(src_dram, dst_dram):
            TL, TLc = BT, BTc
            NBT = ntok // TL
            for wi, wname in enumerate(["W_k", "W_v", "W_r", "W_o"]):
                def _sw(sync, wi=wi, wname=wname):
                    if wi == 0:
                        sync.wait_ge(sem["pe"], tot.get("pe", 0))
                        sync.wait_ge(sem["ve"], tot.get("ve", 0))
                        sync.wait_ge(sem["qst"], tot.get("qst", 0))
                    else:
                        sync.wait_ge(sem["ve"], at("aw%d" % (wi - 1)))
                    for c in range(NCH):
                        sync.dma_start(out=FPOOL[:, c * D:(c + 1) * D],
                                       in_=dram["at_" + wname][c * 128:(c + 1) * 128, :]
                                       ).then_inc(sem["qld"], 16); bump("qld", 16)
                    snap("awl%d" % wi, "qld")
                block.sync(_sw)
                def _vw(vector, wi=wi):
                    vector.wait_ge(sem["qld"], at("awl%d" % wi))
                    for c in range(NCH):
                        vector.tensor_copy(WPOOL[:, wi * 8192 + c * D: wi * 8192 + (c + 1) * D],
                                           FPOOL[:, c * D:(c + 1) * D])
                    inc(vector, "ve"); snap("aw%d" % wi, "ve")
                block.vector(_vw)

            BITX = BPOOL[:, 12288:16384].bitcast(F32)
            BITMD = BPOOL[:, 16384:20480].bitcast(F32)
            BITX2 = BPOOL[:, 20480:24576].bitcast(F32)

            def axv(j):
                k = j % 3
                if k == 0:
                    return cap(FPOOL, FAX, NCH, TL)
                b = BITX if k == 1 else BITX2
                return b.rearrange("p (c t) -> p c t", c=NCH, t=TL)

            def bnd(base, c):
                return FPOOL[:, base + c * TLc: base + (c + 1) * TLc]
            pall = FPOOL[:, FAP:FAP + NCH * TLc].rearrange(
                "p (c t) -> p c t", c=NCH, t=TLc)
            qall = FPOOL[:, FAQ:FAQ + NCH * TLc].rearrange(
                "p (c t) -> p c t", c=NCH, t=TLc)
            lall = FPOOL[:, FALN:FALN + NCH * TLc].rearrange(
                "p (c t) -> p c t", c=NCH, t=TLc)
            mdv = BITMD.rearrange("p (c t) -> p c t", c=NCH, t=TL)

            def frontend1(j):
                P = "B%d:" % j
                t0 = j * TL

                def _ld(sync, t0=t0, j=j):
                    if j == 0:
                        sync.wait_ge(sem["ve"], tot.get("ve", 0))
                        sync.wait_ge(sem["qst"], tot.get("qst", 0))
                    elif j >= 3:
                        sync.wait_ge(sem["ve"], at("B%d:x2el" % (j - 3)))
                    dst = axv(j)
                    qs = ["qx", "qx1", "qx2"][j % 3]
                    for c in range(NCH):
                        sync.dma_start(out=dst[:, c],
                                       in_=src_dram[c * 128:(c + 1) * 128, t0:t0 + TL]
                                       ).then_inc(sem[qs], 16); bump(qs, 16)
                    snap(P + "x", qs)
                block.sync(_ld)

                def _ac(scalar, j=j):
                    scalar.wait_ge(sem[["qx", "qx1", "qx2"][j % 3]], at(P + "x"))
                    if j > 0:
                        scalar.wait_ge(sem["pe"], at("B%d:st" % (j - 1)))
                    xv = axv(j)
                    scalar.activation(cap(BPOOL, AXB2, NCH, TL), xv, AF.Identity)
                    scalar.activation(cap(BPOOL, ASQ2, NCH, TL), xv, AF.Square)
                    inc(scalar, "ac"); snap(P + "cast", "ac")
                block.scalar(_ac)

                def _t1(tensor, j=j):
                    tensor.wait_ge(sem["ac"], at(P + "cast"))
                    if j > 0:
                        tensor.wait_ge(sem["ve"], at("B%d:var" % (j - 1)))
                    xb = cap(BPOOL, AXB2, NCH, TL)
                    xq = cap(BPOOL, ASQ2, NCH, TL)
                    for c in range(NCH):
                        tensor.matmul(PS[0][:, 0:TL], ones_b[:], xb[:, c],
                                      start=(c == 0), stop=(c == NCH - 1))
                    for c in range(NCH):
                        tensor.matmul(PS[1][:, 0:TL], ones_b[:], xq[:, c],
                                      start=(c == 0), stop=(c == NCH - 1))
                    inc(tensor, "pe"); snap(P + "st", "pe")
                block.tensor(_t1)

                def _v2(vector):
                    vector.wait_ge(sem["pe"], at(P + "st"))
                    vector.tensor_scalar(SPOOL[:, AS_MU:AS_MU + TL], PS[0][:, 0:TL],
                                         1.0 / D, None, OP.mult)
                    vector.tensor_scalar(SPOOL[:, AS_T:AS_T + TL], PS[1][:, 0:TL],
                                         1.0 / D, None, OP.mult)
                    vector.drain()
                    vector.tensor_tensor(SPOOL[:, AS_SQ:AS_SQ + TL],
                                         SPOOL[:, AS_MU:AS_MU + TL],
                                         SPOOL[:, AS_MU:AS_MU + TL], OP.mult)
                    vector.drain()
                    vector.tensor_tensor(SPOOL[:, AS_SUM:AS_SUM + TL],
                                         SPOOL[:, AS_T:AS_T + TL],
                                         SPOOL[:, AS_SQ:AS_SQ + TL], OP.subtract)
                    inc(vector, "ve"); snap(P + "var", "ve")
                block.vector(_v2)

                def _a2(scalar, j=j):
                    scalar.wait_ge(sem["ve"], at(P + "var"))
                    if j > 0:
                        scalar.wait_ge(sem["ve"], at("B%d:ln2" % (j - 1)))
                    scalar.activation(SPOOL[:, AS_T:AS_T + TL],
                                      SPOOL[:, AS_SUM:AS_SUM + TL], AF.Ln, bias=cb[:, 0:1])
                    scalar.drain()
                    scalar.activation(SPOOL[:, AS_RSTD:AS_RSTD + TL],
                                      SPOOL[:, AS_T:AS_T + TL], AF.Exp, scale=-0.5)
                    inc(scalar, "ac"); snap(P + "rstd", "ac")
                block.scalar(_a2)

            def frontend2(j):
                P = "B%d:" % j
                seq_start = (j * TL) % tseq == 0

                def _v3(vector, seq_start=seq_start, j=j):
                    vector.wait_ge(sem["ac"], at(P + "rstd"))
                    if j > 0:
                        vector.wait_ge(sem["gp"], at("B%d:mix" % (j - 1)))
                    for c in range(NCH):
                        if seq_start:
                            vector.memset(bnd(FALN, c)[:, 0:1], 0.0)
                        else:
                            vector.tensor_copy(bnd(FALN, c)[:, 0:1],
                                               SPOOL[:, ALNC + c:ALNC + c + 1])
                    vector.drain()
                    mu_b = SPOOL[:, AS_MU:AS_MU + TL].unsqueeze(1).broadcast_to([128, NCH, TL])
                    rstd_b = SPOOL[:, AS_RSTD:AS_RSTD + TL].unsqueeze(1).broadcast_to([128, NCH, TL])
                    vector.tensor_tensor(lall[:, :, 1:TLc], axv(j), mu_b, OP.subtract)
                    vector.drain()
                    vector.tensor_tensor(lall[:, :, 1:TLc], lall[:, :, 1:TLc],
                                         rstd_b, OP.mult)
                    vector.drain()
                    for c in range(NCH):
                        vector.tensor_copy(SPOOL[:, ALNC + c:ALNC + c + 1],
                                           bnd(FALN, c)[:, TL:TL + 1])
                    inc(vector, "ve"); snap(P + "ln2", "ve")
                block.vector(_v3)
                def _g3(gpsimd, j=j):
                    gpsimd.wait_ge(sem["ve"], at(P + "ln2"))
                    if j > 0:
                        gpsimd.wait_ge(sem["pe"], at("B%d:kvpL" % (j - 1)))
                    for c in range(NCH):
                        cur = bnd(FALN, c)[:, 1:TLc]
                        prev = bnd(FALN, c)[:, 0:TL]
                        d_ = SPOOL[:, 5384:5384 + TL]
                        tmp = SPOOL[:, 5640:5640 + TL]
                        gpsimd.tensor_tensor(d_, cur, prev, OP.subtract)
                        gpsimd.drain()
                        for vec, bo_ in [(vecs["tmk"], AXK), (vecs["tmv"], AXV),
                                         (vecs["tmr"], AXR)]:
                            vb = vec[:, c:c + 1].broadcast_to([128, TL])
                            gpsimd.tensor_tensor(tmp, d_, vb, OP.mult)
                            gpsimd.drain()
                            gpsimd.tensor_tensor(
                                BPOOL[:, bo_ + c * TL: bo_ + (c + 1) * TL],
                                tmp, prev, OP.add)
                            gpsimd.drain()
                    gpsimd.sem_inc(sem["gp"], 1); bump("gp", 1)
                    snap(P + "mix", "gp")
                block.gpsimd(_g3)

            frontend1(0)
            frontend2(0)
            slot_last = {}

            def kvr_block(j, kinds):
                P = "B%d:" % j
                kb = (2, 3) if j % 2 == 0 else (4, 5)
                banks = [kb[0], kb[1], 6, 7]
                for wi in kinds:
                    bo_ = [AXK, AXV, AXR][wi]
                    for fc in range(NCH):
                        gidx = wi * NCH + fc
                        bank = banks[fc % 4]
                        off = (fc // 4) * 256
                        skey = (bank, off)
                        def _tk(tensor, wi=wi, bo_=bo_, fc=fc, bank=bank, off=off,
                                prev=slot_last.get(skey)):
                            tensor.wait_ge(sem["gp"], at(P + "mix"))
                            if prev is not None:
                                tensor.wait_ge(sem[prev[0]], at(prev[1]))
                            xm = cap(BPOOL, bo_, NCH, TL)
                            for c in range(NCH):
                                tensor.matmul(PS[bank][:, off:off + TL],
                                              WPOOL[:, wi * 8192 + c * D + fc * 128:
                                                    wi * 8192 + c * D + (fc + 1) * 128],
                                              xm[:, c], start=(c == 0), stop=(c == NCH - 1))
                            inc(tensor, "pe"); snap(P + "kvp%d" % gidx, "pe")
                        block.tensor(_tk)
                        fdst = [FAK, FAV, FAR][wi]
                        def _ak(scalar, wi=wi, fc=fc, fdst=fdst, gidx=gidx,
                                bank=bank, off=off, j=j):
                            scalar.wait_ge(sem["pe"], at(P + "kvp%d" % gidx))
                            if gidx == 0 and j > 0:
                                scalar.wait_ge(sem["ve"], at("B%d:y" % (j - 1)))
                            dst = FPOOL[:, fdst + fc * TL: fdst + (fc + 1) * TL]
                            scalar.activation(dst, PS[bank][:, off:off + TL], AF.Copy)
                            inc(scalar, "ac"); snap(P + "kve%d" % gidx, "ac")
                        block.scalar(_ak)
                        slot_last[skey] = ("ac", P + "kve%d" % gidx)
                snap(P + "kvpL", "pe")

            def wo_block(j):
                P = "B%d:" % j
                t0 = j * TL
                kb = (2, 3) if j % 2 == 0 else (4, 5)
                for fc in range(NCH):
                    bank = kb[fc % 2]
                    skey = (bank, "wo")
                    def _to(tensor, fc=fc, bank=bank, prevk=slot_last.get((bank, 0)),
                            prevk2=slot_last.get((bank, 256)), P=P):
                        tensor.wait_ge(sem["ve"], at(P + "y"))
                        if fc >= 2:
                            tensor.wait_ge(sem["ve"], at(P + "x2e%d" % (fc - 2)))
                        if prevk is not None:
                            tensor.wait_ge(sem[prevk[0]], at(prevk[1]))
                        if prevk2 is not None:
                            tensor.wait_ge(sem[prevk2[0]], at(prevk2[1]))
                        ym = cap(BPOOL, ABY, NCH, TL)
                        for c in range(NCH):
                            tensor.matmul(PS[bank][:, 0:TL],
                                          WPOOL[:, 3 * 8192 + c * D + fc * 128:
                                                3 * 8192 + c * D + (fc + 1) * 128],
                                          ym[:, c], start=(c == 0), stop=(c == NCH - 1))
                        inc(tensor, "pe"); snap(P + "yo%d" % fc, "pe")
                    block.tensor(_to)
                    def _vo(vector, fc=fc, j=j, bank=bank, P=P):
                        vector.wait_ge(sem["pe"], at(P + "yo%d" % fc))
                        xv = axv(j)
                        vector.scalar_tensor_tensor(
                            SPOOL[:, ASC2 + fc * TL: ASC2 + (fc + 1) * TL],
                            PS[bank][:, 0:TL], 0.0, xv[:, fc],
                            OP.add, OP.add)
                        inc(vector, "ve"); snap(P + "x2e%d" % fc, "ve")
                    block.vector(_vo)
                    slot_last[(bank, 0)] = ("ve", P + "x2e%d" % fc)
                snap(P + "x2el", "ve")
                def _sst(sync, t0=t0, P=P):
                    sync.wait_ge(sem["ve"], at(P + "x2el"))
                    for c in range(NCH):
                        sync.dma_start(out=dst_dram[c * 128:(c + 1) * 128, t0:t0 + TL],
                                       in_=SPOOL[:, ASC2 + c * TL: ASC2 + (c + 1) * TL]
                                       ).then_inc(sem["qst"], 16); bump("qst", 16)
                    snap(P + "stored", "qst")
                block.sync(_sst)

            for j in range(NBT):
                P = "B%d:" % j
                t0 = j * TL

                if j + 1 < NBT:
                    frontend1(j + 1)

                kvr_block(j, [0])
                if j > 0:
                    wo_block(j - 1)
                kvr_block(j, [1, 2])

                def _asig(scalar, P=P):
                    scalar.wait_ge(sem["ac"], at(P + "kve%d" % (3 * NCH - 1)))
                    scalar.activation(FPOOL[:, FAR:FAR + NCH * TL],
                                      FPOOL[:, FAR:FAR + NCH * TL], AF.Exp, scale=-1.0)
                    scalar.drain()
                    scalar.activation(FPOOL[:, FAR:FAR + NCH * TL],
                                      FPOOL[:, FAR:FAR + NCH * TL], AF.Ln, bias=cb[:, 2:3])
                    scalar.drain()
                    scalar.activation(FPOOL[:, FAR:FAR + NCH * TL],
                                      FPOOL[:, FAR:FAR + NCH * TL], AF.Exp, scale=-1.0)
                    inc(scalar, "ac"); snap(P + "sig", "ac")
                block.scalar(_asig)

                seq_start = (t0 % tseq) == 0
                def _vscan(vector, seq_start=seq_start, j=j, P=P):
                    vector.wait_ge(sem["ac"], at(P + "kve%d" % (2 * NCH - 1)))
                    if j > 0:
                        vector.wait_ge(sem["qst"], at("B%d:stored" % (j - 1)))
                    for c in range(NCH):
                        if seq_start:
                            vector.memset(bnd(FAP, c)[:, 0:1], -1e30)
                            vector.memset(bnd(FAQ, c)[:, 0:1], 0.0)
                        else:
                            vector.tensor_copy(bnd(FAP, c)[:, 0:1],
                                               bnd(FAP, c)[:, TL:TL + 1])
                            vector.tensor_copy(bnd(FAQ, c)[:, 0:1],
                                               bnd(FAQ, c)[:, TL:TL + 1])
                    vector.drain()
                    for c in range(NCH):
                        wb = vecs["w"][:, c:c + 1].broadcast_to([128, TL])
                        vector.tensor_tensor_scan(
                            bnd(FAP, c)[:, 1:TLc], wb,
                            FPOOL[:, FAK + c * TL: FAK + (c + 1) * TL],
                            bnd(FAP, c)[:, 0:1], OP.add, OP.max)
                    vector.drain()
                    for c in range(NCH):
                        vector.scalar_tensor_tensor(
                            mdv[:, c], FPOOL[:, FAK + c * TL: FAK + (c + 1) * TL],
                            vecs["u"][:, c:c + 1], pall[:, c, 0:TL],
                            OP.add, OP.subtract)
                    vector.drain()
                    vector.tensor_tensor(cap(SPOOL, ASC2, NCH, TL),
                                         cap(FPOOL, FAK, NCH, TL),
                                         pall[:, :, 1:TLc], OP.subtract)
                    inc(vector, "ve"); snap(P + "snarg", "ve")
                block.vector(_vscan)

                if j + 1 < NBT:
                    frontend2(j + 1)

                def _asn(scalar, P=P):
                    scalar.wait_ge(sem["ve"], at(P + "snarg"))
                    scalar.activation(SPOOL[:, ASNV:ASNV + NCH * TL],
                                      SPOOL[:, ASC2:ASC2 + NCH * TL], AF.Exp)
                    inc(scalar, "ac"); snap(P + "sn", "ac")
                block.scalar(_asn)
                def _vsnv(vector, P=P):
                    vector.wait_ge(sem["ac"], at(P + "sn"))
                    vector.tensor_tensor(cap(FPOOL, FAK, NCH, TL),
                                         cap(SPOOL, ASNV, NCH, TL),
                                         cap(FPOOL, FAV, NCH, TL), OP.mult)
                    vector.drain()
                    vector.tensor_tensor(cap(SPOOL, ASC2, NCH, TL),
                                         pall[:, :, 0:TL], pall[:, :, 1:TLc],
                                         OP.subtract)
                    inc(vector, "ve"); snap(P + "d1", "ve")
                block.vector(_vsnv)
                def _asp(scalar, P=P):
                    scalar.wait_ge(sem["ve"], at(P + "d1"))
                    for c in range(NCH):
                        scalar.activation(SPOOL[:, ASNV + c * TL: ASNV + (c + 1) * TL],
                                          SPOOL[:, ASC2 + c * TL: ASC2 + (c + 1) * TL],
                                          AF.Exp, bias=vecs["w"][:, c:c + 1])
                    inc(scalar, "ac"); snap(P + "sp", "ac")
                block.scalar(_asp)
                def _vq(vector, P=P):
                    vector.wait_ge(sem["ac"], at(P + "sp"))
                    for c in range(NCH):
                        vector.tensor_tensor_scan(
                            bnd(FAQ, c)[:, 1:TLc],
                            SPOOL[:, ASNV + c * TL: ASNV + (c + 1) * TL],
                            FPOOL[:, FAK + c * TL: FAK + (c + 1) * TL],
                            bnd(FAQ, c)[:, 0:1], OP.mult, OP.add)
                    vector.drain()
                    vector.tensor_scalar(cap(SPOOL, ASC2, NCH, TL), mdv,
                                         -1.0, 0.0, OP.mult, OP.min)
                    inc(vector, "ve"); snap(P + "qs", "ve")
                block.vector(_vq)
                def _aA(scalar, P=P):
                    scalar.wait_ge(sem["ve"], at(P + "qs"))
                    scalar.activation(FPOOL[:, FAK:FAK + NCH * TL],
                                      SPOOL[:, ASC2:ASC2 + NCH * TL], AF.Exp)
                    inc(scalar, "ac"); snap(P + "Araw", "ac")
                block.scalar(_aA)
                def _vA(vector, P=P):
                    vector.wait_ge(sem["ac"], at(P + "Araw"))
                    vector.tensor_scalar(cap(SPOOL, ASC2, NCH, TL),
                                         cap(FPOOL, FAK, NCH, TL),
                                         CL30, None, OP.max)
                    vector.drain()
                    vector.tensor_scalar(cap(FPOOL, FAK, NCH, TL), mdv,
                                         0.0, None, OP.min)
                    inc(vector, "ve"); snap(P + "Barg", "ve")
                block.vector(_vA)
                def _aB(scalar, P=P):
                    scalar.wait_ge(sem["ve"], at(P + "Barg"))
                    scalar.activation(SPOOL[:, ASNV:ASNV + NCH * TL],
                                      FPOOL[:, FAK:FAK + NCH * TL], AF.Exp)
                    inc(scalar, "ac"); snap(P + "Braw", "ac")
                block.scalar(_aB)
                def _vfin(vector, P=P):
                    vector.wait_ge(sem["ac"], at(P + "Braw"))
                    vector.tensor_scalar(cap(FPOOL, FAK, NCH, TL),
                                         cap(SPOOL, ASNV, NCH, TL), CL30, None, OP.max)
                    vector.drain()
                    vector.tensor_tensor(cap(SPOOL, ASNV, NCH, TL),
                                         cap(SPOOL, ASC2, NCH, TL),
                                         cap(FPOOL, FAK, NCH, TL), OP.add)
                    vector.drain()
                    vector.tensor_tensor(mdv, cap(FPOOL, FAK, NCH, TL),
                                         cap(FPOOL, FAV, NCH, TL), OP.mult)
                    vector.drain()
                    vector.tensor_tensor(cap(FPOOL, FAV, NCH, TL),
                                         cap(SPOOL, ASC2, NCH, TL),
                                         qall[:, :, 0:TL], OP.mult)
                    vector.drain()
                    vector.tensor_tensor(cap(SPOOL, ASC2, NCH, TL),
                                         cap(FPOOL, FAV, NCH, TL), mdv, OP.add)
                    inc(vector, "ve"); snap(P + "num", "ve")
                block.vector(_vfin)
                def _aden(scalar, P=P):
                    scalar.wait_ge(sem["ve"], at(P + "num"))
                    scalar.activation(FPOOL[:, FAK:FAK + NCH * TL],
                                      SPOOL[:, ASNV:ASNV + NCH * TL], AF.Ln,
                                      bias=cb[:, 3:4])
                    scalar.drain()
                    scalar.activation(SPOOL[:, ASNV:ASNV + NCH * TL],
                                      FPOOL[:, FAK:FAK + NCH * TL], AF.Exp, scale=-1.0)
                    inc(scalar, "ac"); snap(P + "rden", "ac")
                block.scalar(_aden)
                def _vy(vector, P=P):
                    vector.wait_ge(sem["ac"], at(P + "rden"))
                    vector.wait_ge(sem["ac"], at(P + "sig"))
                    vector.tensor_tensor(cap(FPOOL, FAK, NCH, TL),
                                         cap(SPOOL, ASC2, NCH, TL),
                                         cap(SPOOL, ASNV, NCH, TL), OP.mult)
                    vector.drain()
                    vector.tensor_tensor(cap(BPOOL, ABY, NCH, TL),
                                         cap(FPOOL, FAK, NCH, TL),
                                         cap(FPOOL, FAR, NCH, TL), OP.mult)
                    inc(vector, "ve"); snap(P + "y", "ve")
                block.vector(_vy)
            wo_block(NBT - 1)

        prep()
        ffn_phase(x_in, x1_dram, "A")
        attn_phase(x1_dram, x2_dram)
        def _reload_gate(sync):
            sync.wait_ge(sem["pe"], tot.get("pe", 0))
            sync.wait_ge(sem["ve"], tot.get("ve", 0))
        block.sync(_reload_gate)
        def reprep():
            def wprep2(dname, dst_off, ncols, scale, nrow, tag):
                def _sy(sync):
                    sync.wait_ge(sem["ve"], tot.get("ve", 0))
                    nch = (nrow + 127) // 128
                    for c in range(nch):
                        r0, r1 = c * 128, min((c + 1) * 128, nrow)
                        sync.dma_start(out=FPOOL[0:r1 - r0, c * ncols:(c + 1) * ncols],
                                       in_=dram[dname][r0:r1, :]
                                       ).then_inc(sem["qld"], 16); bump("qld", 16)
                    snap("r_%sld" % tag, "qld")
                block.sync(_sy)
                def _ve(vector):
                    vector.wait_ge(sem["qld"], at("r_%sld" % tag))
                    nch = (nrow + 127) // 128
                    for c in range(nch):
                        r0, r1 = c * 128, min((c + 1) * 128, nrow)
                        dst = WPOOL[0:r1 - r0, dst_off + c * ncols: dst_off + (c + 1) * ncols]
                        src = FPOOL[0:r1 - r0, c * ncols:(c + 1) * ncols]
                        if scale is None:
                            vector.tensor_copy(dst, src)
                        else:
                            vector.tensor_scalar(dst, src, scale[:, c:c + 1], None, OP.mult)
                    inc(vector, "ve"); snap("r_%sc" % tag, "ve")
                block.vector(_ve)
            wprep2("f_Wq", WQ_O, D, vecs["g1"], D, "wq")
            wprep2("f_Wo", WO_O, D, None, D, "wo")
            wprep2("f_Wdown", WD_O, LR, vecs["g1"], D, "wd")
            wprep2("f_Wup", WU_O, 2 * D, None, LR, "wu")
            wprep2("f_Wg", WG_O, NE, vecs["g2"], D, "wg")
        reprep()
        ffn_phase(x2_dram, out_fm, "C")

        def _fin(sync):
            sync.wait_ge(sem["qst"], tot.get("qst", 0))
        block.sync(_fin)

        with nc.Block() as RB:
            RB.sync(lambda e: _replay(e, block.q["sync"]))
            RB.scalar(lambda e: _replay(e, block.q["scalar"]))
            RB.vector(lambda e: _replay(e, block.q["vector"]))
            RB.tensor(lambda e: _replay(e, block.q["tensor"]))
            RB.gpsimd(lambda e: _replay(e, block.q["gpsimd"]))

    return nc


def _vec_layout(v):
    v = np.asarray(v, np.float32)
    n = v.shape[0]
    return np.ascontiguousarray(v.reshape(n // 128, 128).T)


def _make_consts(HT):
    ones = np.ones((128, 128), bfdt)
    id64 = np.concatenate([np.eye(64, dtype=np.float32)] * 2, axis=0).astype(bfdt)
    esel = np.zeros((NE, NE * 128), np.float32)
    for e in range(NE):
        esel[e, e * 128:(e + 1) * 128] = 1.0
    m1 = np.where(np.arange(128)[:, None] % 8 == np.arange(128)[None, :] % 8,
                  np.float32(1.0), np.float32(CL30))
    mask = np.concatenate([m1] * 4, axis=1)
    return ones, id64, esel.astype(bfdt), mask.astype(bfdt)


_NC_CACHE = {}


def kernel(**inputs):
    x = np.asarray(inputs["L0_attn_x"], np.float32)
    B, T, _ = x.shape
    ncore = 8
    nseq = B // ncore
    key = (nseq, T)
    if key not in _NC_CACHE:
        _NC_CACHE[key] = build_nc(nseq, T)
    nc = _NC_CACHE[key]

    HT = min(512, T) // 4
    ones, id64, esel, mask = _make_consts(HT)

    shared = {
        "at_W_k": np.ascontiguousarray(np.asarray(inputs["L0_attn_W_k"], np.float32).T),
        "at_W_v": np.ascontiguousarray(np.asarray(inputs["L0_attn_W_v"], np.float32).T),
        "at_W_r": np.ascontiguousarray(np.asarray(inputs["L0_attn_W_r"], np.float32).T),
        "at_W_o": np.ascontiguousarray(np.asarray(inputs["L0_attn_W_o"], np.float32).T),
        "at_tmk": _vec_layout(inputs["L0_attn_time_mix_k"]),
        "at_tmv": _vec_layout(inputs["L0_attn_time_mix_v"]),
        "at_tmr": _vec_layout(inputs["L0_attn_time_mix_r"]),
        "at_w": _vec_layout(inputs["L0_attn_time_decay"]),
        "at_u": _vec_layout(inputs["L0_attn_time_first"]),
        "f_g1": _vec_layout(inputs["L0_ffn_gamma1"]),
        "f_g2": _vec_layout(inputs["L0_ffn_gamma2"]),
        "f_bq": _vec_layout(inputs["L0_ffn_bq"]),
        "f_bo": _vec_layout(inputs["L0_ffn_bo"]),
        "f_Wq": np.asarray(inputs["L0_ffn_Wq"], np.float32),
        "f_Wdown": np.asarray(inputs["L0_ffn_Wdown"], np.float32),
        "f_Wup": np.asarray(inputs["L0_ffn_Wup"], np.float32),
        "f_Wo": np.asarray(inputs["L0_ffn_Wo"], np.float32),
        "f_Wg": np.asarray(inputs["L0_ffn_Wg"], np.float32),
        "f_We": np.ascontiguousarray(
            np.asarray(inputs["L0_ffn_We"], np.float32).reshape(NE * D, D)),
        "f_bdown": np.asarray(inputs["L0_ffn_bdown"], np.float32).reshape(LR, 1),
        "f_bup": _vec_layout(inputs["L0_ffn_bup"]),
        "f_bg": np.asarray(inputs["L0_ffn_bg"], np.float32).reshape(NE, 1),
        "c_ones": ones, "c_id64": id64, "c_esel": esel, "c_mask": mask,
    }
    in_maps = []
    for i in range(ncore):
        xm = np.ascontiguousarray(
            x[i * nseq:(i + 1) * nseq].reshape(nseq * T, D).T)
        m = dict(shared)
        m["x_fm"] = xm
        in_maps.append(m)

    res = run_bass_kernel_spmd(nc, in_maps, core_ids=list(range(ncore)))
    outs = []
    for i in range(ncore):
        o = np.asarray(res.results[i]["out_fm"])
        outs.append(o.T.reshape(nseq, T, D))
    return np.concatenate(outs, axis=0).astype(np.float32)


# version 20
# speedup vs baseline: 1.0622x; 1.0125x over previous
import numpy as np
import ml_dtypes
from contextlib import ExitStack

import concourse.bass as bass
import concourse.mybir as mybir
from concourse.alu_op_type import AluOpType
from concourse.bass_utils import run_bass_kernel_spmd

F32 = mybir.dt.float32
BF16 = mybir.dt.bfloat16
AF = mybir.ActivationFunctionType
OP = AluOpType
bfdt = ml_dtypes.bfloat16

D = 1024
NCH = 8
H, HD, LR, NE = 16, 64, 64, 8
SUPPV = 15.5
CL30 = float(np.exp(np.float32(-30.0)))
EPS_LN = 1e-5
EPS_RMS = 1e-6
EPS_WKV = 1e-9


DEBUG_TAPS = False


class _FakeInst:
    __slots__ = ("ti",)

    def __init__(self):
        self.ti = None

    def then_inc(self, sem, v):
        self.ti = (sem, v)
        return self


class _EngRecorder:
    def __init__(self, lst):
        self._lst = lst

    def __getattr__(self, name):
        lst = self._lst

        def call(*a, **k):
            fi = _FakeInst()
            lst.append((name, a, k, fi))
            return fi
        return call


class _BlockRecorder:

    def __init__(self):
        self.q = {e: [] for e in ("sync", "scalar", "vector", "tensor", "gpsimd")}

    def _mk(self, eng):
        def add(fn):
            fn(_EngRecorder(self.q[eng]))
        return add

    def __getattr__(self, name):
        if name in ("sync", "scalar", "vector", "tensor", "gpsimd"):
            return self._mk(name)
        raise AttributeError(name)


def _replay(engine, lst):
    for name, a, k, fi in lst:
        r = getattr(engine, name)(*a, **k)
        if fi.ti is not None:
            r.then_inc(*fi.ti)


def build_nc(nseq, tseq):
    ntok = nseq * tseq
    FT = min(512, tseq)
    BT = min(256, tseq)
    BTc = BT + 1
    HT = FT // 4
    NG = HT // 8
    NMB = NG // 8
    assert NG % 8 == 0

    nc = bass.Bass("TRN2", target_bir_lowering=False, debug=False)

    dram = {}
    def din(name, shape, dt=F32):
        dram[name] = nc.dram_tensor(name, shape, dt, kind="ExternalInput").ap()
        return dram[name]

    x_in = din("x_fm", [D, ntok])
    for w in ["W_k", "W_v", "W_r", "W_o"]:
        din("at_" + w, [D, D])
    for v in ["tmk", "tmv", "tmr", "w", "u"]:
        din("at_" + v, [128, NCH])
    for v in ["g1", "g2", "bq", "bo"]:
        din("f_" + v, [128, NCH])
    din("f_Wq", [D, D]); din("f_Wdown", [D, LR]); din("f_Wup", [LR, 2 * D])
    din("f_Wo", [D, D]); din("f_Wg", [D, NE]); din("f_We", [NE * D, D])
    din("f_bdown", [LR, 1]); din("f_bup", [128, 2 * NCH]); din("f_bg", [NE, 1])
    din("c_ones", [128, 128], BF16)
    din("c_id64", [128, 64], BF16)
    din("c_esel", [NE, NE * 128], BF16)
    din("c_mask", [128, 512], BF16)
    out_fm = nc.dram_tensor("out_fm", [D, ntok], F32, kind="ExternalOutput").ap()
    x1_dram = nc.dram_tensor("x1_scr", [D, ntok], F32, kind="Internal").ap()
    tap_o2 = (nc.dram_tensor("tap_o2", [128, 4096], BF16, kind="ExternalOutput").ap()
              if DEBUG_TAPS else None)
    tap_xp = (nc.dram_tensor("tap_xp", [128, 4096], F32, kind="ExternalOutput").ap()
              if DEBUG_TAPS else None)
    tap_hq = (nc.dram_tensor("tap_hq", [128, 4096], BF16, kind="ExternalOutput").ap()
              if DEBUG_TAPS else None)
    tap_kv = (nc.dram_tensor("tap_kv", [128, 8192], BF16, kind="ExternalOutput").ap()
              if DEBUG_TAPS else None)
    tap_sm = (nc.dram_tensor("tap_sm", [128, 4096], BF16, kind="ExternalOutput").ap()
              if DEBUG_TAPS else None)
    tap_qt = (nc.dram_tensor("tap_qt", [128, 3 * 2048], BF16, kind="ExternalOutput").ap()
              if DEBUG_TAPS else None)
    x2_dram = nc.dram_tensor("x2_scr", [D, ntok], F32, kind="Internal").ap()
    we_dram = nc.dram_tensor("we_scr", [NE, 128, NCH * D], BF16, kind="Internal").ap()

    cnt = {}
    tot = {}
    def bump(s, by):
        tot[s] = tot.get(s, 0) + by
        return tot[s]
    def snap(name, s):
        cnt[name] = tot.get(s, 0)
    def at(name):
        return cnt.get(name, 0)

    ctx = ExitStack()
    with ctx:
        sem = {}
        for s in ["qld", "qsc", "qsv", "qst", "qx", "qwa", "qwb", "pe", "ve", "ac", "gp",
                  "qsc1", "qsv1", "qsd0", "qsd1", "qtt0", "qtt1", "qx1", "qx2"]:
            sem[s] = ctx.enter_context(nc.semaphore(s))

        def inc(eng, s):
            eng.drain()
            eng.sem_inc(sem[s], 1)
            bump(s, 1)

        vecs = {}
        for name in ["tmk", "tmv", "tmr", "w", "u", "g1", "g2", "bq", "bo"]:
            vecs[name] = ctx.enter_context(nc.sbuf_tensor("v_" + name, [128, NCH], F32))
        v_bdown = ctx.enter_context(nc.sbuf_tensor("v_bdown", [LR, 1], F32))
        v_bup = ctx.enter_context(nc.sbuf_tensor("v_bup", [128, 2 * NCH], F32))
        v_bg = ctx.enter_context(nc.sbuf_tensor("v_bg", [NE, 1], F32))
        ones_b = ctx.enter_context(nc.sbuf_tensor("ones_b", [128, 128], BF16))
        cb = ctx.enter_context(nc.sbuf_tensor("cbias", [128, 4], F32))
        id64_b = ctx.enter_context(nc.sbuf_tensor("id64_b", [128, 64], BF16))
        mask_b = ctx.enter_context(nc.sbuf_tensor("mask_b", [128, 512], BF16))
        esel_b = ctx.enter_context(nc.sbuf_tensor("esel_b", [NE, NE * 128], BF16))

        WPOOL = ctx.enter_context(nc.sbuf_tensor("wpool", [128, 32768], BF16))
        SPOOL = ctx.enter_context(nc.sbuf_tensor("spool", [128, 6144], F32))
        FPOOL = ctx.enter_context(nc.sbuf_tensor("fpool", [128, 14464], F32))
        BPOOL = ctx.enter_context(nc.sbuf_tensor("bpool", [128, 24576], BF16))
        PS = [ctx.enter_context(nc.psum_tensor("ps%d" % i, [128, 512], F32))
              for i in range(8)]

        block = _BlockRecorder()

        def cap(pool, base, nch, width):
            return pool[:, base:base + nch * width].rearrange(
                "p (c t) -> p c t", c=nch, t=width)

        WQ_O, WO_O, WD_O, WU_O, WG_O, WE_O = 0, 8192, 16384, 16896, 18944, 19456

        BXB, BSQ, BKV, BQF = 0, 4096, 8192, 16384
        BO2 = BQF
        BXEA = BKV
        BXEB = BKV + 4096
        BQT, BKVT = 20480, 22528
        BGC = BQF
        SMALLB = ctx.enter_context(nc.sbuf_tensor("smallb", [128, 4096], BF16))

        AXK, AXV, AXR, ABY, AXB2, ASQ2 = 0, 2048, 4096, 6144, 8192, 10240

        FXB, FT1, FLN1 = 0, 4096, 8192
        FXP = FT1
        FXH = FLN1
        FAX, FALN, FAP, FAQ = 0, 2048, 4104, 6160
        FAK, FAV, FAR, FSC2 = 8216, 10264, 12312, 14360

        SS_SUM, SS_SQ, SS_MU, SS_RSTD, SS_C1, SS_T, SS_R2 = (
            0, 512, 1024, 1536, 2048, 2560, 3072)
        SLS, SRS, SGRB = 3584, 4608, 5632
        AS_SUM, AS_SQ, AS_MU, AS_RSTD, AS_T = 0, 256, 512, 768, 1024
        ASNV = 1280
        ASC2 = 3328
        ALNC = 5376

        def prep():
            def _vc(vector):
                vector.memset(cb[:, 0:1], EPS_LN)
                vector.memset(cb[:, 1:2], EPS_RMS)
                vector.memset(cb[:, 2:3], 1.0)
                vector.memset(cb[:, 3:4], EPS_WKV)
                inc(vector, "ve"); snap("p_cb", "ve")
            block.vector(_vc)
            def _sy(sync):
                for name in ["tmk", "tmv", "tmr", "w", "u"]:
                    sync.dma_start(out=vecs[name][:], in_=dram["at_" + name]
                                   ).then_inc(sem["qld"], 16); bump("qld", 16)
                for name in ["g1", "g2", "bq", "bo"]:
                    sync.dma_start(out=vecs[name][:], in_=dram["f_" + name]
                                   ).then_inc(sem["qld"], 16); bump("qld", 16)
                sync.dma_start(out=v_bdown[:], in_=dram["f_bdown"]).then_inc(sem["qld"], 16); bump("qld", 16)
                sync.dma_start(out=v_bup[:], in_=dram["f_bup"]).then_inc(sem["qld"], 16); bump("qld", 16)
                sync.dma_start(out=v_bg[:], in_=dram["f_bg"]).then_inc(sem["qld"], 16); bump("qld", 16)
                sync.dma_start(out=ones_b[:], in_=dram["c_ones"]).then_inc(sem["qld"], 16); bump("qld", 16)
                sync.dma_start(out=id64_b[:], in_=dram["c_id64"]).then_inc(sem["qld"], 16); bump("qld", 16)
                sync.dma_start(out=mask_b[:], in_=dram["c_mask"]).then_inc(sem["qld"], 16); bump("qld", 16)
                sync.dma_start(out=esel_b[:], in_=dram["c_esel"]).then_inc(sem["qld"], 16); bump("qld", 16)
                snap("p_vecs", "qld")
            block.sync(_sy)

            for e in range(NE):
                def _sy(sync, e=e):
                    if e > 0:
                        sync.wait_ge(sem["ve"], at("p_wecast%d" % (e - 1)))
                    for c in range(NCH):
                        sync.dma_start(out=FPOOL[:, c * D:(c + 1) * D],
                                       in_=dram["f_We"][e * D + c * 128:e * D + (c + 1) * 128, :]
                                       ).then_inc(sem["qld"], 16); bump("qld", 16)
                    snap("p_weld%d" % e, "qld")
                block.sync(_sy)
                def _ve(vector, e=e):
                    vector.wait_ge(sem["qld"], at("p_weld%d" % e))
                    if e > 0:
                        vector.wait_ge(sem["qst"], at("p_west%d" % (e - 1)))
                    for c in range(NCH):
                        vector.tensor_scalar(BPOOL[:, c * D:(c + 1) * D],
                                             FPOOL[:, c * D:(c + 1) * D],
                                             vecs["g2"][:, c:c + 1], None, OP.mult)
                    inc(vector, "ve"); snap("p_wecast%d" % e, "ve")
                block.vector(_ve)
                def _sy2(sync, e=e):
                    sync.wait_ge(sem["ve"], at("p_wecast%d" % e))
                    sync.dma_start(out=we_dram[e], in_=BPOOL[:, 0:NCH * D]
                                   ).then_inc(sem["qst"], 16); bump("qst", 16)
                    snap("p_west%d" % e, "qst")
                block.sync(_sy2)

            def wprep(dname, dst_off, ncols, scale, nrow, tag):
                def _sy(sync):
                    sync.wait_ge(sem["ve"], tot.get("ve", 0))
                    nch = (nrow + 127) // 128
                    for c in range(nch):
                        r0, r1 = c * 128, min((c + 1) * 128, nrow)
                        sync.dma_start(out=FPOOL[0:r1 - r0, c * ncols:(c + 1) * ncols],
                                       in_=dram[dname][r0:r1, :]
                                       ).then_inc(sem["qld"], 16); bump("qld", 16)
                    snap("p_%sld" % tag, "qld")
                block.sync(_sy)
                def _ve(vector):
                    vector.wait_ge(sem["qld"], at("p_%sld" % tag))
                    nch = (nrow + 127) // 128
                    for c in range(nch):
                        r0, r1 = c * 128, min((c + 1) * 128, nrow)
                        dst = WPOOL[0:r1 - r0, dst_off + c * ncols: dst_off + (c + 1) * ncols]
                        src = FPOOL[0:r1 - r0, c * ncols:(c + 1) * ncols]
                        if scale is None:
                            vector.tensor_copy(dst, src)
                        else:
                            vector.tensor_scalar(dst, src, scale[:, c:c + 1], None, OP.mult)
                    inc(vector, "ve"); snap("p_%sc" % tag, "ve")
                block.vector(_ve)

            wprep("f_Wq", WQ_O, D, vecs["g1"], D, "wq")
            wprep("f_Wo", WO_O, D, None, D, "wo")
            wprep("f_Wdown", WD_O, LR, vecs["g1"], D, "wd")
            wprep("f_Wup", WU_O, 2 * D, None, LR, "wu")
            wprep("f_Wg", WG_O, NE, vecs["g2"], D, "wg")

        dtctr = [0]

        def ffn_phase(src_dram, dst_dram, tag):
            TL = FT
            NT = ntok // TL
            mlist = []

            def frontend(j, hoisted, apply_now=True):
                P = tag + str(j) + ":"
                t0 = j * TL
                XSQR = BQT if hoisted else BSQ

                def _ld(sync, t0=t0, j=j, tag=tag):
                    if j == 0:
                        sync.wait_ge(sem["ve"], tot.get("ve", 0))
                        sync.wait_ge(sem["qst"], tot.get("qst", 0))
                    else:
                        sync.wait_ge(sem["ve"], at(tag + str(j - 1) + ":xh"))
                    qs = "qx" if j % 2 == 0 else "qx1"
                    for c in range(NCH):
                        sync.dma_start(out=FPOOL[:, FXB + c * TL: FXB + (c + 1) * TL],
                                       in_=src_dram[c * 128:(c + 1) * 128, t0:t0 + TL]
                                       ).then_inc(sem[qs], 16); bump(qs, 16)
                    snap(P + "x", qs)
                block.sync(_ld)

                def _v1(vector, j=j):
                    vector.wait_ge(sem["qx" if j % 2 == 0 else "qx1"], at(P + "x"))
                    if hoisted:
                        vector.wait_ge(sem["pe"], at("g_lastvsm0"))
                        vector.wait_ge(sem["pe"], at("g_lastvsm1"))
                    vector.tensor_copy(BPOOL[:, BXB:BXB + NCH * TL],
                                       FPOOL[:, FXB:FXB + NCH * TL])
                    inc(vector, "ve"); snap(P + "xb", "ve")
                block.vector(_v1)
                def _a1(scalar, j=j, XSQR=XSQR):
                    scalar.wait_ge(sem["qx" if j % 2 == 0 else "qx1"], at(P + "x"))
                    if hoisted:
                        scalar.wait_ge(sem["pe"], at("g_lastvsm0"))
                        scalar.wait_ge(sem["pe"], at("g_lastvsm1"))
                    scalar.activation(BPOOL[:, XSQR:XSQR + NCH * TL],
                                      FPOOL[:, FXB:FXB + NCH * TL], AF.Square)
                    inc(scalar, "ac"); snap(P + "xsq", "ac")
                block.scalar(_a1)
                def _t1(tensor, j=j, tag=tag, XSQR=XSQR):
                    tensor.wait_ge(sem["ve"], at(P + "xb"))
                    tensor.wait_ge(sem["ac"], at(P + "xsq"))
                    if j > 0 and not hoisted:
                        tensor.wait_ge(sem["ve"], at(tag + str(j - 1) + ":out01"))
                    if hoisted:
                        tensor.wait_ge(sem["ve"], at(tag + str(j - 1) + ":xpe%d" % (NCH - 1)))
                    xb = cap(BPOOL, BXB, NCH, TL)
                    xq = cap(BPOOL, XSQR, NCH, TL)
                    for c in range(NCH):
                        tensor.matmul(PS[0][:, 0:TL], ones_b[:], xb[:, c],
                                      start=(c == 0), stop=(c == NCH - 1))
                    for c in range(NCH):
                        tensor.matmul(PS[1][:, 0:TL], ones_b[:], xq[:, c],
                                      start=(c == 0), stop=(c == NCH - 1))
                    inc(tensor, "pe"); snap(P + "st", "pe")
                block.tensor(_t1)
                def _v2(vector):
                    vector.wait_ge(sem["pe"], at(P + "st"))
                    vector.tensor_scalar(SPOOL[:, SS_MU:SS_MU + TL], PS[0][:, 0:TL],
                                         1.0 / D, None, OP.mult)
                    vector.tensor_scalar(SPOOL[:, SS_T:SS_T + TL], PS[1][:, 0:TL],
                                         1.0 / D, None, OP.mult)
                    vector.drain()
                    vector.tensor_tensor(SPOOL[:, SS_SQ:SS_SQ + TL],
                                         SPOOL[:, SS_MU:SS_MU + TL],
                                         SPOOL[:, SS_MU:SS_MU + TL], OP.mult)
                    vector.drain()
                    vector.tensor_tensor(SPOOL[:, SS_SUM:SS_SUM + TL],
                                         SPOOL[:, SS_T:SS_T + TL],
                                         SPOOL[:, SS_SQ:SS_SQ + TL], OP.subtract)
                    inc(vector, "ve"); snap(P + "var", "ve")
                block.vector(_v2)
                def _a2(scalar):
                    scalar.wait_ge(sem["ve"], at(P + "var"))
                    scalar.activation(SPOOL[:, SS_T:SS_T + TL],
                                      SPOOL[:, SS_SUM:SS_SUM + TL], AF.Ln, bias=cb[:, 0:1])
                    scalar.drain()
                    scalar.activation(SPOOL[:, SS_RSTD:SS_RSTD + TL],
                                      SPOOL[:, SS_T:SS_T + TL], AF.Exp, scale=-0.5)
                    inc(scalar, "ac"); snap(P + "rstd", "ac")
                block.scalar(_a2)
                def _v3(vector):
                    vector.wait_ge(sem["ac"], at(P + "rstd"))
                    vector.tensor_tensor(SPOOL[:, SS_T:SS_T + TL],
                                         SPOOL[:, SS_SUM:SS_SUM + TL],
                                         SPOOL[:, SS_RSTD:SS_RSTD + TL], OP.mult)
                    vector.drain()
                    vector.tensor_tensor(SPOOL[:, SS_T:SS_T + TL],
                                         SPOOL[:, SS_T:SS_T + TL],
                                         SPOOL[:, SS_RSTD:SS_RSTD + TL], OP.mult)
                    inc(vector, "ve"); snap(P + "tt", "ve")
                block.vector(_v3)
                def _a3(scalar):
                    scalar.wait_ge(sem["ve"], at(P + "tt"))
                    scalar.activation(SPOOL[:, SS_T:SS_T + TL],
                                      SPOOL[:, SS_T:SS_T + TL], AF.Ln, bias=cb[:, 1:2])
                    scalar.drain()
                    scalar.activation(SPOOL[:, SS_C1:SS_C1 + TL],
                                      SPOOL[:, SS_T:SS_T + TL], AF.Exp, scale=-0.5)
                    inc(scalar, "ac"); snap(P + "r1", "ac")
                block.scalar(_a3)
                def _v4a(vector):
                    vector.wait_ge(sem["ac"], at(P + "r1"))
                    vector.tensor_tensor(SPOOL[:, SS_C1:SS_C1 + TL],
                                         SPOOL[:, SS_C1:SS_C1 + TL],
                                         SPOOL[:, SS_RSTD:SS_RSTD + TL], OP.mult)
                    inc(vector, "ve"); snap(P + "c1", "ve")
                block.vector(_v4a)
                if not apply_now:
                    return
                emit_lnapply(j)

            def emit_lnapply(j):
                P = tag + str(j) + ":"
                def _v4(vector, j=j, tag=tag):
                    vector.wait_ge(sem["ve"], at(P + "c1"))
                    for c in range(NCH):
                        vector.tensor_tensor(PS[c][:, 0:TL],
                                             FPOOL[:, FXB + c * TL:FXB + (c + 1) * TL],
                                             SPOOL[:, SS_MU:SS_MU + TL], OP.subtract)
                    vector.drain()
                    for c in range(NCH):
                        vector.tensor_tensor(BPOOL[:, BXB + c * TL:BXB + (c + 1) * TL],
                                             PS[c][:, 0:TL],
                                             SPOOL[:, SS_C1:SS_C1 + TL], OP.mult)
                    inc(vector, "ve"); snap(P + "hq", "ve")
                    for c in range(2):
                        vector.tensor_tensor(FPOOL[:, FLN1 + c * TL:FLN1 + (c + 1) * TL],
                                             PS[c][:, 0:TL],
                                             SPOOL[:, SS_RSTD:SS_RSTD + TL], OP.mult)
                    inc(vector, "ve"); snap(P + "ln1a", "ve")
                    for c in range(2, NCH):
                        vector.tensor_tensor(FPOOL[:, FLN1 + c * TL:FLN1 + (c + 1) * TL],
                                             PS[c][:, 0:TL],
                                             SPOOL[:, SS_RSTD:SS_RSTD + TL], OP.mult)
                    inc(vector, "ve"); snap(P + "ln1v", "ve")
                block.vector(_v4)

            frontend(0, False)
            for j in range(NT):
                t0 = j * TL
                P = tag + str(j) + ":"
                QROT = [0, 1, 5, 6, 7, 0, 1, 5]
                for hp in range(NCH):
                    def _tq(tensor, hp=hp):
                        tensor.wait_ge(sem["ve"],
                                       at(P + ("ln1a" if hp < 2 else "ln1v")))
                        if hp >= 5:
                            tensor.wait_ge(sem["ac"], at(P + "qe%d" % (hp - 5)))
                        hq = cap(BPOOL, BXB, NCH, TL)
                        for c in range(NCH):
                            tensor.matmul(PS[QROT[hp]][:, 0:TL],
                                          WPOOL[:, WQ_O + c * D + hp * 128:
                                                WQ_O + c * D + (hp + 1) * 128],
                                          hq[:, c], start=(c == 0), stop=(c == NCH - 1))
                        inc(tensor, "pe"); snap(P + "q%d" % hp, "pe")
                    block.tensor(_tq)
                    def _aq(scalar, hp=hp):
                        scalar.wait_ge(sem["pe"], at(P + "q%d" % hp))
                        scalar.activation(BPOOL[:, BSQ + hp * TL: BSQ + (hp + 1) * TL],
                                          PS[QROT[hp]][:, 0:TL], AF.Identity,
                                          bias=vecs["bq"][:, hp:hp + 1])
                        inc(scalar, "ac"); snap(P + "qe%d" % hp, "ac")
                    block.scalar(_aq)

                def _td(tensor):
                    tensor.wait_ge(sem["ve"], at(P + "ln1v"))
                    hq = cap(BPOOL, BXB, NCH, TL)
                    for c in range(NCH):
                        tensor.matmul(PS[2][0:LR, 0:TL],
                                      WPOOL[:, WD_O + c * LR: WD_O + (c + 1) * LR],
                                      hq[:, c], start=(c == 0), stop=(c == NCH - 1))
                    inc(tensor, "pe"); snap(P + "dv", "pe")
                block.tensor(_td)
                def _ad(scalar):
                    scalar.wait_ge(sem["pe"], at(P + "dv"))
                    scalar.activation(SMALLB[0:LR, 2560:2560 + TL],
                                      PS[2][0:LR, 0:TL], AF.Identity,
                                      bias=v_bdown[:, 0:1])
                    inc(scalar, "ac"); snap(P + "dve", "ac")
                block.scalar(_ad)

                def _dst0(par):
                    return BPOOL[0:64, BQT:BQT + H * HT].rearrange(
                        "p (G q j t) -> p q j G t", G=NG, q=2, j=8, t=8)[:, par]

                def _qs0(par):
                    return [BPOOL[par * 64:par * 64 + 64,
                                  BSQ + jj * TL:BSQ + jj * TL + HT]
                            for jj in range(8)]

                def _q0(sync, P=P):
                    sync.wait_ge(sem["pe"], at("g_lastvsm0"))
                    if DEBUG_TAPS:
                        sync.wait_ge(sem["qst"], tot.get("qst", 0))
                    for jj in range(8):
                        sync.wait_ge(sem["ac"], at(P + "qe%d" % jj))
                        sync.dma_start(out=_dst0(0)[:, jj], in_=_qs0(0)[jj]
                                       ).then_inc(sem["qsc"], 16)
                        bump("qsc", 16)
                block.sync(_q0)
                snap(P + "h0dt_scalar", "qsc")

                def _q1(gpsimd, P=P):
                    gpsimd.wait_ge(sem["pe"], at("g_lastvsm0"))
                    if DEBUG_TAPS:
                        gpsimd.wait_ge(sem["qst"], tot.get("qst", 0))
                    for jj in range(8):
                        gpsimd.wait_ge(sem["ac"], at(P + "qe%d" % jj))
                        gpsimd.dma_start(out=_dst0(1)[:, jj], in_=_qs0(1)[jj]
                                         ).then_inc(sem["qsd0"], 16)
                        bump("qsd0", 16)
                block.gpsimd(_q1)
                snap(P + "h0dt_gpsimd", "qsd0")

                for g in range(2 * NCH):
                    def _tk(tensor, g=g):
                        tensor.wait_ge(sem["ac"], at(P + "dve"))
                        if g >= 6:
                            tensor.wait_ge(sem["ac"], at(P + "kve%d" % (g - 6)))
                        tensor.matmul(PS[2 + g % 6][:, 0:TL],
                                      WPOOL[0:LR, WU_O + g * 128: WU_O + (g + 1) * 128],
                                      SMALLB[0:LR, 2560:2560 + TL],
                                      start=True, stop=True)
                        inc(tensor, "pe"); snap(P + "kv%d" % g, "pe")
                    block.tensor(_tk)
                    def _ak(scalar, g=g):
                        scalar.wait_ge(sem["pe"], at(P + "kv%d" % g))
                        scalar.activation(BPOOL[:, BKV + g * TL: BKV + (g + 1) * TL],
                                          PS[2 + g % 6][:, 0:TL], AF.Identity,
                                          bias=v_bup[:, g:g + 1])
                        inc(scalar, "ac"); snap(P + "kve%d" % g, "ac")
                        if g % 2 == 1:
                            if g == 1:
                                scalar.wait_ge(sem["pe"], at("g_lastvsm0"))
                                if DEBUG_TAPS:
                                    scalar.wait_ge(sem["qst"], tot.get("qst", 0))
                            dvx = BPOOL[0:128, BKVT:BKVT + H * HT].rearrange(
                                "p (G q j t) -> p q j G t", G=NG, q=2, j=8, t=8)
                            scalar.dma_start(out=dvx[:, g % 2, g // 2],
                                             in_=BPOOL[0:128, BKV + g * TL:
                                                       BKV + g * TL + HT]
                                             ).then_inc(sem["qsv"], 16)
                            bump("qsv", 16)
                    block.scalar(_ak)

                def _kvd(sync, P=P):
                    sync.wait_ge(sem["pe"], at("g_lastvsm0"))
                    if DEBUG_TAPS:
                        sync.wait_ge(sem["qst"], tot.get("qst", 0))
                    dv0 = BPOOL[0:128, BKVT:BKVT + H * HT].rearrange(
                        "p (G q j t) -> p q j G t", G=NG, q=2, j=8, t=8)
                    for g in range(0, 2 * NCH, 2):
                        sync.wait_ge(sem["ac"], at(P + "kve%d" % g))
                        sync.dma_start(out=dv0[:, g % 2, g // 2],
                                       in_=BPOOL[0:128, BKV + g * TL:
                                                 BKV + g * TL + HT]
                                       ).then_inc(sem["qsv"], 16)
                        bump("qsv", 16)
                block.sync(_kvd)
                snap(P + "h0dt_sync", "qsv")

                if DEBUG_TAPS and tag == "A" and j == 0:
                    def _tapkv(sync):
                        sync.wait_ge(sem["ac"], at(P + "kve%d" % (2 * NCH - 1)))
                        sync.dma_start(out=tap_kv, in_=BPOOL[:, BKV:BKV + 8192]
                                       ).then_inc(sem["qst"], 16); bump("qst", 16)
                    block.sync(_tapkv)
                QTB = [BQT, BXB]
                KVB = [BKVT, BXB + 2048]
                rs_names = {"scalar": ["qsc", "qsc1"], "gpsimd": ["qsd0", "qsd1"],
                            "sync": ["qsv", "qsv1"]}

                def emit_dt(s):
                    ht0 = s * HT
                    buf = dtctr[0] % 2
                    dtctr[0] += 1
                    PH = P + "h%d" % s
                    qt_o, kv_o = QTB[buf], KVB[buf]

                    def dstap(base, par):
                        return BPOOL[0:64, base:base + H * HT].rearrange(
                            "p (G q j t) -> p q j G t", G=NG, q=2, j=8, t=8)[:, par]

                    def qsrc(par):
                        return [BPOOL[par * 64:par * 64 + 64,
                                      BSQ + jj * TL + ht0:BSQ + jj * TL + ht0 + HT]
                                for jj in range(8)]

                    def kvsrc(par):
                        return [BPOOL[0:128, BKV + (2 * jj + par) * TL + ht0:
                                      BKV + (2 * jj + par) * TL + ht0 + HT]
                                for jj in range(8)]

                    def emit_ring(eng_name, dst, srcs, issue=None):
                        sname = rs_names[eng_name][buf]
                        def _r(eng, dst=dst, srcs=srcs, sname=sname):
                            eng.wait_ge(sem["ac"], at(P + "qe%d" % (NCH - 1)))
                            if buf == 1:
                                eng.wait_ge(sem["ac"], at(P + "kve%d" % (2 * NCH - 1)))
                            eng.wait_ge(sem["pe"], at("g_lastvsm%d" % buf))
                            if DEBUG_TAPS:
                                eng.wait_ge(sem["qst"], tot.get("qst", 0))
                            for jj in range(8):
                                eng.dma_start(out=dst[:, jj], in_=srcs[jj]
                                              ).then_inc(sem[sname], 16)
                                bump(sname, 16)
                        getattr(block, issue or eng_name)(_r)
                        snap(PH + "dt_" + eng_name, sname)

                    if s != 0:
                        emit_ring("scalar", dstap(qt_o, 0), qsrc(0), issue="sync")
                        emit_ring("gpsimd", dstap(qt_o, 1), qsrc(1))
                        sname = rs_names["sync"][buf]

                        def _kv(sync, kv_o=kv_o, buf=buf, sname=sname):
                            sync.wait_ge(sem["ac"], at(P + "kve%d" % (2 * NCH - 1)))
                            sync.wait_ge(sem["pe"], at("g_lastvsm%d" % buf))
                            dv = BPOOL[0:128, kv_o:kv_o + H * HT].rearrange(
                                "p (G q j t) -> p q j G t", G=NG, q=2, j=8, t=8)
                            ss = kvsrc(0)
                            for jj in range(8):
                                sync.dma_start(out=dv[:, 0, jj], in_=ss[jj]
                                               ).then_inc(sem[sname], 16)
                                bump(sname, 16)
                        block.sync(_kv)

                        def _kv2(scalar, kv_o=kv_o, buf=buf, sname=sname):
                            scalar.wait_ge(sem["ac"], at(P + "kve%d" % (2 * NCH - 1)))
                            scalar.wait_ge(sem["pe"], at("g_lastvsm%d" % buf))
                            dv = BPOOL[0:128, kv_o:kv_o + H * HT].rearrange(
                                "p (G q j t) -> p q j G t", G=NG, q=2, j=8, t=8)
                            ss = kvsrc(1)
                            for jj in range(8):
                                scalar.dma_start(out=dv[:, 1, jj], in_=ss[jj]
                                                 ).then_inc(sem[sname], 16)
                                bump(sname, 16)
                        block.scalar(_kv2)
                        snap(PH + "dt_sync", sname)
                    return buf

                dt_buf = {}
                dt_buf[0] = emit_dt(0)
                if FT // HT > 1:
                    dt_buf[1] = emit_dt(1)
                EMB = FPOOL[:, 12288:12800].bitcast(BF16)
                bundles = []
                NHALF = FT // HT
                for half in range(NHALF):
                    PH = P + "h%d" % half
                    for mb in range(NMB):
                        m = half * NMB + mb
                        g0 = mb * 8
                        PM = PH + "m%d" % mb
                        mlist.append(PM)
                        prev1 = mlist[-2] if len(mlist) >= 2 else None
                        prev2 = mlist[-3] if len(mlist) >= 3 else None
                        p = mb % 2
                        E2 = 0 if p == 0 else 3072
                        VSo = 2048 if p == 0 else 2560
                        QB = (PS[2], PS[3]) if p == 0 else (PS[0], PS[1])
                        em_full = SMALLB[:, 1024:2048] if p == 0 else EMB
                        srs_full = (SPOOL[:, SRS:SRS + 1024] if p == 0
                                    else FPOOL[:, 12800:13824])
                        buf = dt_buf[half]

                        def _ta(tensor, g0=g0, mb=mb, PM=PM, PH=PH, buf=buf,
                                QB=QB, prev1=prev1, prev2=prev2):
                            if mb == 0:
                                for en in ("scalar", "gpsimd", "sync"):
                                    tensor.wait_ge(sem[rs_names[en][buf]],
                                                   at(PH + "dt_" + en))
                            if prev2:
                                tensor.wait_ge(sem["ac"], at(prev2 + "e2"))
                            qt_o, kv_o = QTB[buf], KVB[buf]
                            for gi in range(8):
                                g = g0 + gi
                                bank = QB[0] if gi < 4 else QB[1]
                                tensor.matmul(bank[:, (gi % 4) * 128:(gi % 4 + 1) * 128],
                                              BPOOL[0:64, kv_o + g * 128:kv_o + (g + 1) * 128],
                                              BPOOL[0:64, qt_o + g * 128:qt_o + (g + 1) * 128],
                                              start=True, stop=True)
                            inc(tensor, "pe"); snap(PM + "att", "pe")
                            if prev1:
                                tensor.wait_ge(sem["ve"], at(prev1 + "vse"))
                            for gi in range(8):
                                g = g0 + gi
                                tensor.matmul(PS[4][:, gi * 64:(gi + 1) * 64],
                                              BPOOL[64:128, kv_o + g * 128:kv_o + (g + 1) * 128],
                                              id64_b[64:128, :],
                                              start=True, stop=True)
                            inc(tensor, "pe"); snap(PM + "vsm", "pe")
                            snap("g_lastvsm%d" % buf, "pe")

                        def _ae(scalar, PM=PM, QB=QB, E2=E2, prev2=prev2):
                            scalar.wait_ge(sem["pe"], at(PM + "att"))
                            if prev2:
                                scalar.wait_ge(sem["pe"], at(prev2 + "av"))
                            scalar.activation(SMALLB[:, E2:E2 + 512], QB[0][:, 0:512],
                                              AF.Exp, scale=0.125)
                            scalar.activation(SMALLB[:, E2 + 512:E2 + 1024],
                                              QB[1][:, 0:512], AF.Exp, scale=0.125)
                            inc(scalar, "ac"); snap(PM + "e2", "ac")

                        def _av(vector, PM=PM, VSo=VSo, prev2=prev2):
                            vector.wait_ge(sem["pe"], at(PM + "vsm"))
                            if prev2:
                                vector.wait_ge(sem["pe"], at(prev2 + "av"))
                            vector.tensor_copy(SMALLB[:, VSo:VSo + 512],
                                               PS[4][:, 0:512])
                            inc(vector, "ve"); snap(PM + "vse", "ve")

                        def _gm(gpsimd, PM=PM, E2=E2, em_full=em_full, prev2=prev2):
                            gpsimd.wait_ge(sem["ac"], at(PM + "e2"))
                            if prev2:
                                gpsimd.wait_ge(sem["pe"], at(prev2 + "s"))
                                gpsimd.wait_ge(sem["ve"], at(prev2 + "e2n"))
                            mb_ = mask_b[:, :].rearrange("p (a t) -> p a t", a=4, t=128
                                                         ).unsqueeze(1).broadcast_to(
                                                             [128, 2, 4, 128])
                            e2v = SMALLB[:, E2:E2 + 1024].rearrange(
                                "p (b a t) -> p b a t", b=2, a=4, t=128)
                            emv = em_full.rearrange(
                                "p (b a t) -> p b a t", b=2, a=4, t=128)
                            gpsimd.tensor_tensor(emv, e2v, mb_, OP.mult)
                            gpsimd.drain()
                            gpsimd.sem_inc(sem["gp"], 1)
                            bump("gp", 1)
                            snap(PM + "em", "gp")

                        def _ts(tensor, PM=PM, em_full=em_full, prev1=prev1):
                            tensor.wait_ge(sem["gp"], at(PM + "em"))
                            if prev1:
                                tensor.wait_ge(sem["ac"], at(prev1 + "rs"))
                            tensor.matmul(PS[5][:, 0:512], ones_b[:],
                                          em_full[:, 0:512], start=True, stop=True)
                            tensor.matmul(PS[7][:, 0:512], ones_b[:],
                                          em_full[:, 512:1024], start=True, stop=True)
                            inc(tensor, "pe"); snap(PM + "s", "pe")

                        def _ar(scalar, PM=PM, srs_full=srs_full, prev2=prev2):
                            scalar.wait_ge(sem["pe"], at(PM + "s"))
                            if prev2:
                                scalar.wait_ge(sem["ve"], at(prev2 + "e2n"))
                            scalar.activation(SPOOL[:, SLS:SLS + 512], PS[5][:, 0:512], AF.Ln)
                            scalar.activation(SPOOL[:, SLS + 512:SLS + 1024],
                                              PS[7][:, 0:512], AF.Ln)
                            scalar.drain()
                            scalar.activation(srs_full,
                                              SPOOL[:, SLS:SLS + 1024], AF.Exp, scale=-1.0)
                            inc(scalar, "ac"); snap(PM + "rs", "ac")

                        def _vn(vector, PM=PM, E2=E2, em_full=em_full,
                                srs_full=srs_full):
                            vector.wait_ge(sem["ac"], at(PM + "rs"))
                            vector.wait_ge(sem["gp"], at(PM + "em"))
                            vector.tensor_tensor(SMALLB[:, E2:E2 + 1024],
                                                 em_full, srs_full, OP.mult)
                            inc(vector, "ve"); snap(PM + "e2n", "ve")

                        def _tv(tensor, g0=g0, PM=PM, E2=E2, VSo=VSo, prev1=prev1):
                            tensor.wait_ge(sem["ve"], at(PM + "e2n"))
                            if prev1:
                                tensor.wait_ge(sem["ve"], at(prev1 + "o2"))
                            for gi in range(8):
                                rhs_e = SMALLB[:, E2 + gi * 128:E2 + gi * 128 + 64]
                                rhs_o = SMALLB[:, E2 + gi * 128 + 64:E2 + (gi + 1) * 128]
                                vs = SMALLB[:, VSo + gi * 64:VSo + (gi + 1) * 64]
                                tensor.matmul(PS[6][0:64, gi * 64:(gi + 1) * 64],
                                              vs, rhs_e, start=True, stop=True)
                                tensor.matmul(PS[6][64:128, gi * 64:(gi + 1) * 64],
                                              vs, rhs_o, start=True, stop=True,
                                              tile_position=(0, 64))
                            inc(tensor, "pe"); snap(PM + "av", "pe")

                        def _ao(vector, g0=g0, half=half, PM=PM):
                            vector.wait_ge(sem["pe"], at(PM + "av"))
                            g_glob = half * NG + g0
                            dst = BPOOL[:, BO2:BO2 + 8 * TL].rearrange(
                                "p (j G t) -> p G j t", j=8, G=TL // 8, t=8)[
                                :, g_glob:g_glob + 8]
                            srcv = PS[6][:, 0:512].rearrange(
                                "p (G j t) -> p G j t", G=8, j=8, t=8)
                            vector.tensor_copy(dst, srcv)
                            inc(vector, "ve"); snap(PM + "o2", "ve")

                        bundles.append({"ta": _ta, "ae": _ae, "gm": _gm,
                                        "ts": _ts, "ar": _ar, "vn": _vn,
                                        "tv": _tv, "ao": _ao, "av": _av})
                        block.tensor(_ta)
                        if m >= 1:
                            block.tensor(bundles[m - 1]["ts"])
                        if m >= 2:
                            block.tensor(bundles[m - 2]["tv"])
                        block.scalar(_ae)
                        if m >= 1:
                            block.scalar(bundles[m - 1]["ar"])
                        block.gpsimd(_gm)
                        block.vector(_av)
                        if m >= 2:
                            block.vector(bundles[m - 2]["ao"])
                        if m >= 1:
                            block.vector(bundles[m - 1]["vn"])
                        if mb == 1 and half + 2 < NHALF:
                            dt_buf[half + 2] = emit_dt(half + 2)
                M = NHALF * NMB
                block.tensor(bundles[M - 1]["ts"])
                block.scalar(bundles[M - 1]["ar"])
                block.tensor(bundles[M - 2]["tv"])
                block.vector(bundles[M - 2]["ao"])
                block.vector(bundles[M - 1]["vn"])
                block.tensor(bundles[M - 1]["tv"])
                block.vector(bundles[M - 1]["ao"])
                snap(P + "h%do2last" % (NHALF - 1), "ve")

                for fc in range(NCH):
                    def _tw(tensor, fc=fc):
                        tensor.wait_ge(sem["ve"], at(P + "h%do2last" % (FT // HT - 1)))
                        if fc >= 2:
                            tensor.wait_ge(sem["ve"], at(P + "xpe%d" % (fc - 2)))
                        for jj in range(8):
                            tensor.matmul(PS[fc % 2][:, 0:TL],
                                          WPOOL[:, WO_O + jj * D + fc * 128:
                                                WO_O + jj * D + (fc + 1) * 128],
                                          BPOOL[:, BO2 + jj * TL:BO2 + (jj + 1) * TL],
                                          start=(jj == 0), stop=(jj == 7))
                        inc(tensor, "pe"); snap(P + "wo%d" % fc, "pe")
                    block.tensor(_tw)
                    def _vw(vector, fc=fc, j=j):
                        vector.wait_ge(sem["pe"], at(P + "wo%d" % fc))
                        if fc == 0 and j > 0:
                            vector.wait_ge(sem["qst"],
                                           at(tag + str(j - 1) + ":stored"))
                        vector.scalar_tensor_tensor(
                            FPOOL[:, FXP + fc * TL: FXP + (fc + 1) * TL],
                            PS[fc % 2][:, 0:TL], vecs["bo"][:, fc:fc + 1],
                            FPOOL[:, FLN1 + fc * TL: FLN1 + (fc + 1) * TL],
                            OP.add, OP.add)
                        inc(vector, "ve"); snap(P + "xpe%d" % fc, "ve")
                    block.vector(_vw)

                if DEBUG_TAPS and tag == "A" and j == 0:
                    def _tapxp(sync):
                        sync.wait_ge(sem["ve"], at(P + "xpe%d" % (NCH - 1)))
                        sync.dma_start(out=tap_xp, in_=FPOOL[:, FXP:FXP + 4096]
                                       ).then_inc(sem["qst"], 16); bump("qst", 16)
                    block.sync(_tapxp)
                def _a4(scalar):
                    for c in range(NCH):
                        scalar.wait_ge(sem["ve"], at(P + "xpe%d" % c))
                        scalar.activation(BPOOL[:, BSQ + c * TL:BSQ + (c + 1) * TL],
                                          FPOOL[:, FXP + c * TL:FXP + (c + 1) * TL],
                                          AF.Square)
                        inc(scalar, "ac"); snap(P + "sq2c%d" % c, "ac")
                    snap(P + "sq2", "ac")
                block.scalar(_a4)
                def _vxh(vector):
                    vector.wait_ge(sem["ve"], at(P + "xpe%d" % (NCH - 1)))
                    vector.scalar_tensor_tensor(
                        cap(FPOOL, FXH, NCH, TL), cap(FPOOL, FXP, NCH, TL), 0.5,
                        cap(FPOOL, FXB, NCH, TL), OP.mult, OP.add)
                    inc(vector, "ve"); snap(P + "xh", "ve")
                block.vector(_vxh)
                def _t1b(tensor):
                    xq = cap(BPOOL, BSQ, NCH, TL)
                    for c in range(NCH):
                        tensor.wait_ge(sem["ac"], at(P + "sq2c%d" % c))
                        tensor.matmul(PS[2][:, 0:TL], ones_b[:], xq[:, c],
                                      start=(c == 0), stop=(c == NCH - 1))
                    inc(tensor, "pe"); snap(P + "st2", "pe")
                block.tensor(_t1b)
                def _v5(vector):
                    vector.wait_ge(sem["pe"], at(P + "st2"))
                    vector.tensor_scalar(SPOOL[:, SS_T:SS_T + TL], PS[2][:, 0:TL],
                                         1.0 / D, None, OP.mult)
                    inc(vector, "ve"); snap(P + "m2", "ve")
                block.vector(_v5)
                def _a5(scalar):
                    scalar.wait_ge(sem["ve"], at(P + "m2"))
                    scalar.activation(SPOOL[:, SS_T:SS_T + TL],
                                      SPOOL[:, SS_T:SS_T + TL], AF.Ln, bias=cb[:, 1:2])
                    scalar.drain()
                    scalar.activation(SPOOL[:, SS_R2:SS_R2 + TL],
                                      SPOOL[:, SS_T:SS_T + TL], AF.Exp, scale=-0.5)
                    inc(scalar, "ac"); snap(P + "r2", "ac")
                block.scalar(_a5)
                def _v6(vector):
                    vector.wait_ge(sem["ac"], at(P + "r2"))
                    r2b = SPOOL[:, SS_R2:SS_R2 + TL].unsqueeze(1).broadcast_to([128, NCH, TL])
                    vector.tensor_tensor(cap(BPOOL, BSQ, NCH, TL),
                                         cap(FPOOL, FXP, NCH, TL), r2b, OP.mult)
                    inc(vector, "ve"); snap(P + "h2", "ve")
                block.vector(_v6)

                def _tg(tensor):
                    tensor.wait_ge(sem["ve"], at(P + "h2"))
                    h2 = cap(BPOOL, BSQ, NCH, TL)
                    for c in range(NCH):
                        tensor.matmul(PS[3][0:NE, 0:TL],
                                      WPOOL[:, WG_O + c * NE: WG_O + (c + 1) * NE],
                                      h2[:, c], start=(c == 0), stop=(c == NCH - 1))
                    inc(tensor, "pe"); snap(P + "gz", "pe")
                block.tensor(_tg)
                def _ag(scalar):
                    scalar.wait_ge(sem["pe"], at(P + "gz"))
                    scalar.activation(SMALLB[0:NE, 3072:3072 + TL],
                                      PS[3][0:NE, 0:TL], AF.Exp, bias=v_bg[:, 0:1])
                    inc(scalar, "ac"); snap(P + "eg", "ac")
                block.scalar(_ag)
                def _tgs(tensor):
                    tensor.wait_ge(sem["ac"], at(P + "eg"))
                    tensor.matmul(PS[3][:, 0:TL], ones_b[0:NE, :],
                                  SMALLB[0:NE, 3072:3072 + TL], start=True, stop=True)
                    inc(tensor, "pe"); snap(P + "gs", "pe")
                block.tensor(_tgs)
                def _agr(scalar):
                    scalar.wait_ge(sem["pe"], at(P + "gs"))
                    scalar.activation(SPOOL[:, SGRB:SGRB + TL], PS[3][:, 0:TL], AF.Ln)
                    scalar.drain()
                    scalar.activation(SPOOL[:, SGRB:SGRB + TL],
                                      SPOOL[:, SGRB:SGRB + TL], AF.Exp, scale=-1.0)
                    inc(scalar, "ac"); snap(P + "gr", "ac")
                block.scalar(_agr)
                def _vgn(vector):
                    vector.wait_ge(sem["ac"], at(P + "gr"))
                    vector.tensor_tensor(SMALLB[0:NE, 3584:3584 + TL],
                                         SMALLB[0:NE, 3072:3072 + TL],
                                         SPOOL[0:NE, SGRB:SGRB + TL], OP.mult)
                    inc(vector, "ve"); snap(P + "gn", "ve")
                block.vector(_vgn)
                for e in range(NE):
                    def _tgb(tensor, e=e):
                        tensor.wait_ge(sem["ve"], at(P + "gn"))
                        if e > 0:
                            tensor.wait_ge(sem["ac"], at(P + "gbe%d" % (e - 1)))
                        tensor.matmul(PS[4][:, 0:TL],
                                      esel_b[:, e * 128:(e + 1) * 128],
                                      SMALLB[0:NE, 3584:3584 + TL],
                                      start=True, stop=True)
                        inc(tensor, "pe"); snap(P + "gb%d" % e, "pe")
                    block.tensor(_tgb)
                    def _agb(scalar, e=e):
                        scalar.wait_ge(sem["pe"], at(P + "gb%d" % e))
                        if DEBUG_TAPS:
                            scalar.wait_ge(sem["qst"], tot.get("qst", 0))
                        scalar.activation(BPOOL[:, BGC + e * TL: BGC + (e + 1) * TL],
                                          PS[4][:, 0:TL], AF.Copy)
                        inc(scalar, "ac"); snap(P + "gbe%d" % e, "ac")
                    block.scalar(_agb)

                if j + 1 < NT:
                    frontend(j + 1, True, apply_now=False)

                steps = [(e, hh) for e in range(NE) for hh in range(2)]
                for si, (e, hh) in enumerate(steps):
                    buf = si % 2
                    def _swe(sync, e=e, hh=hh, si=si, buf=buf):
                        if si >= 2:
                            sync.wait_ge(sem["pe"], at(P + "ex%d" % (si - 2)))
                        else:
                            sync.wait_ge(sem["qst"], at("p_west%d" % (NE - 1)))
                        qs = "qwa" if buf == 0 else "qwb"
                        src = we_dram[e].rearrange("p (c f) -> p c f", c=NCH, f=D)[
                            :, :, hh * 512:(hh + 1) * 512]
                        sync.dma_start(out=WPOOL[:, WE_O + buf * 4096:
                                                 WE_O + buf * 4096 + 4096],
                                       in_=src).then_inc(sem[qs], 16); bump(qs, 16)
                        snap(P + "we%d" % si, qs)
                    block.sync(_swe)
                    if hh == 0:
                        def _vxe(vector, e=e):
                            vector.wait_ge(sem["ac"], at(P + "gbe%d" % e))
                            if DEBUG_TAPS:
                                vector.wait_ge(sem["qst"], tot.get("qst", 0))
                            if e >= 2:
                                vector.wait_ge(sem["pe"], at(P + "ex%d" % (2 * e - 3)))
                            gb = BPOOL[:, BGC + e * TL: BGC + (e + 1) * TL
                                       ].unsqueeze(1).broadcast_to([128, NCH, TL])
                            xeo = BXEA if e % 2 == 0 else BXEB
                            vector.tensor_tensor(cap(BPOOL, xeo, NCH, TL),
                                                 cap(BPOOL, BSQ, NCH, TL), gb, OP.mult)
                            inc(vector, "ve"); snap(P + "xe%d" % e, "ve")
                        block.vector(_vxe)
                    def _tex(tensor, e=e, hh=hh, si=si, buf=buf):
                        tensor.wait_ge(sem["ve"], at(P + "xe%d" % e))
                        tensor.wait_ge(sem["qwa" if buf == 0 else "qwb"],
                                       at(P + "we%d" % si))
                        if si == 0:
                            tensor.wait_ge(sem["ve"], at(P + "xh"))
                            tensor.wait_ge(sem["ac"], at(P + "gbe%d" % (NE - 1)))
                            if j + 1 < NT:
                                tensor.wait_ge(sem["ve"],
                                               at(tag + str(j + 1) + ":var"))
                        xeo = BXEA if e % 2 == 0 else BXEB
                        xe = cap(BPOOL, xeo, NCH, TL)
                        for fi in range(4):
                            fc = hh * 4 + fi
                            for c in range(NCH):
                                tensor.matmul(
                                    PS[fc][:, 0:TL],
                                    WPOOL[:, WE_O + buf * 4096 + c * 512 + fi * 128:
                                          WE_O + buf * 4096 + c * 512 + (fi + 1) * 128],
                                    xe[:, c],
                                    start=(e == 0 and c == 0),
                                    stop=(e == NE - 1 and c == NCH - 1))
                        inc(tensor, "pe"); snap(P + "ex%d" % si, "pe")
                    block.tensor(_tex)

                def _vout(vector):
                    vector.wait_ge(sem["pe"], at(P + "ex%d" % (2 * NE - 1)))
                    for fc in range(2):
                        vector.scalar_tensor_tensor(
                            FPOOL[:, FT1 + fc * TL: FT1 + (fc + 1) * TL],
                            PS[fc][:, 0:TL], 0.5,
                            FPOOL[:, FXH + fc * TL: FXH + (fc + 1) * TL],
                            OP.mult, OP.add)
                    inc(vector, "ve"); snap(P + "out01", "ve")
                    for fc in range(2, NCH):
                        vector.scalar_tensor_tensor(
                            FPOOL[:, FT1 + fc * TL: FT1 + (fc + 1) * TL],
                            PS[fc][:, 0:TL], 0.5,
                            FPOOL[:, FXH + fc * TL: FXH + (fc + 1) * TL],
                            OP.mult, OP.add)
                    inc(vector, "ve"); snap(P + "out", "ve")
                block.vector(_vout)
                def _sst(sync, t0=t0):
                    sync.wait_ge(sem["ve"], at(P + "out"))
                    for c in range(0, NCH, 2):
                        sync.dma_start(out=dst_dram[c * 128:(c + 1) * 128, t0:t0 + TL],
                                       in_=FPOOL[:, FT1 + c * TL: FT1 + (c + 1) * TL]
                                       ).then_inc(sem["qst"], 16); bump("qst", 16)
                block.sync(_sst)
                def _sst2(scalar, t0=t0):
                    scalar.wait_ge(sem["ve"], at(P + "out"))
                    for c in range(1, NCH, 2):
                        scalar.dma_start(out=dst_dram[c * 128:(c + 1) * 128, t0:t0 + TL],
                                         in_=FPOOL[:, FT1 + c * TL: FT1 + (c + 1) * TL]
                                         ).then_inc(sem["qst"], 16); bump("qst", 16)
                    snap(P + "stored", "qst")
                block.scalar(_sst2)
                if j + 1 < NT:
                    emit_lnapply(j + 1)


        def attn_phase(src_dram, dst_dram):
            TL, TLc = BT, BTc
            NBT = ntok // TL
            for wi, wname in enumerate(["W_k", "W_v", "W_r", "W_o"]):
                def _sw(sync, wi=wi, wname=wname):
                    if wi == 0:
                        sync.wait_ge(sem["pe"], tot.get("pe", 0))
                        sync.wait_ge(sem["ve"], tot.get("ve", 0))
                        sync.wait_ge(sem["qst"], tot.get("qst", 0))
                    else:
                        sync.wait_ge(sem["ve"], at("aw%d" % (wi - 1)))
                    for c in range(NCH):
                        sync.dma_start(out=FPOOL[:, c * D:(c + 1) * D],
                                       in_=dram["at_" + wname][c * 128:(c + 1) * 128, :]
                                       ).then_inc(sem["qld"], 16); bump("qld", 16)
                    snap("awl%d" % wi, "qld")
                block.sync(_sw)
                def _vw(vector, wi=wi):
                    vector.wait_ge(sem["qld"], at("awl%d" % wi))
                    for c in range(NCH):
                        vector.tensor_copy(WPOOL[:, wi * 8192 + c * D: wi * 8192 + (c + 1) * D],
                                           FPOOL[:, c * D:(c + 1) * D])
                    inc(vector, "ve"); snap("aw%d" % wi, "ve")
                block.vector(_vw)

            BITX = BPOOL[:, 12288:16384].bitcast(F32)
            BITMD = BPOOL[:, 16384:20480].bitcast(F32)
            BITX2 = BPOOL[:, 20480:24576].bitcast(F32)

            def axv(j):
                k = j % 3
                if k == 0:
                    return cap(FPOOL, FAX, NCH, TL)
                b = BITX if k == 1 else BITX2
                return b.rearrange("p (c t) -> p c t", c=NCH, t=TL)

            def bnd(base, c):
                return FPOOL[:, base + c * TLc: base + (c + 1) * TLc]
            pall = FPOOL[:, FAP:FAP + NCH * TLc].rearrange(
                "p (c t) -> p c t", c=NCH, t=TLc)
            qall = FPOOL[:, FAQ:FAQ + NCH * TLc].rearrange(
                "p (c t) -> p c t", c=NCH, t=TLc)
            lall = FPOOL[:, FALN:FALN + NCH * TLc].rearrange(
                "p (c t) -> p c t", c=NCH, t=TLc)
            mdv = BITMD.rearrange("p (c t) -> p c t", c=NCH, t=TL)

            def frontend1(j):
                P = "B%d:" % j
                t0 = j * TL

                def _ld(sync, t0=t0, j=j):
                    if j == 0:
                        sync.wait_ge(sem["ve"], tot.get("ve", 0))
                        sync.wait_ge(sem["qst"], tot.get("qst", 0))
                    elif j >= 3:
                        sync.wait_ge(sem["ve"], at("B%d:x2el" % (j - 3)))
                    dst = axv(j)
                    qs = ["qx", "qx1", "qx2"][j % 3]
                    for c in range(NCH):
                        sync.dma_start(out=dst[:, c],
                                       in_=src_dram[c * 128:(c + 1) * 128, t0:t0 + TL]
                                       ).then_inc(sem[qs], 16); bump(qs, 16)
                    snap(P + "x", qs)
                block.sync(_ld)

                def _ac(scalar, j=j):
                    scalar.wait_ge(sem[["qx", "qx1", "qx2"][j % 3]], at(P + "x"))
                    if j > 0:
                        scalar.wait_ge(sem["pe"], at("B%d:st" % (j - 1)))
                    xv = axv(j)
                    scalar.activation(cap(BPOOL, AXB2, NCH, TL), xv, AF.Identity)
                    scalar.activation(cap(BPOOL, ASQ2, NCH, TL), xv, AF.Square)
                    inc(scalar, "ac"); snap(P + "cast", "ac")
                block.scalar(_ac)

                def _t1(tensor, j=j):
                    tensor.wait_ge(sem["ac"], at(P + "cast"))
                    if j > 0:
                        tensor.wait_ge(sem["ve"], at("B%d:var" % (j - 1)))
                    xb = cap(BPOOL, AXB2, NCH, TL)
                    xq = cap(BPOOL, ASQ2, NCH, TL)
                    for c in range(NCH):
                        tensor.matmul(PS[0][:, 0:TL], ones_b[:], xb[:, c],
                                      start=(c == 0), stop=(c == NCH - 1))
                    for c in range(NCH):
                        tensor.matmul(PS[1][:, 0:TL], ones_b[:], xq[:, c],
                                      start=(c == 0), stop=(c == NCH - 1))
                    inc(tensor, "pe"); snap(P + "st", "pe")
                block.tensor(_t1)

                def _v2(vector):
                    vector.wait_ge(sem["pe"], at(P + "st"))
                    vector.tensor_scalar(SPOOL[:, AS_MU:AS_MU + TL], PS[0][:, 0:TL],
                                         1.0 / D, None, OP.mult)
                    vector.tensor_scalar(SPOOL[:, AS_T:AS_T + TL], PS[1][:, 0:TL],
                                         1.0 / D, None, OP.mult)
                    vector.drain()
                    vector.tensor_tensor(SPOOL[:, AS_SQ:AS_SQ + TL],
                                         SPOOL[:, AS_MU:AS_MU + TL],
                                         SPOOL[:, AS_MU:AS_MU + TL], OP.mult)
                    vector.drain()
                    vector.tensor_tensor(SPOOL[:, AS_SUM:AS_SUM + TL],
                                         SPOOL[:, AS_T:AS_T + TL],
                                         SPOOL[:, AS_SQ:AS_SQ + TL], OP.subtract)
                    inc(vector, "ve"); snap(P + "var", "ve")
                block.vector(_v2)

                def _a2(scalar, j=j):
                    scalar.wait_ge(sem["ve"], at(P + "var"))
                    if j > 0:
                        scalar.wait_ge(sem["ve"], at("B%d:ln2" % (j - 1)))
                    scalar.activation(SPOOL[:, AS_T:AS_T + TL],
                                      SPOOL[:, AS_SUM:AS_SUM + TL], AF.Ln, bias=cb[:, 0:1])
                    scalar.drain()
                    scalar.activation(SPOOL[:, AS_RSTD:AS_RSTD + TL],
                                      SPOOL[:, AS_T:AS_T + TL], AF.Exp, scale=-0.5)
                    inc(scalar, "ac"); snap(P + "rstd", "ac")
                block.scalar(_a2)

            def frontend2(j):
                P = "B%d:" % j
                seq_start = (j * TL) % tseq == 0

                def _v3(vector, seq_start=seq_start, j=j):
                    vector.wait_ge(sem["ac"], at(P + "rstd"))
                    if j > 0:
                        vector.wait_ge(sem["gp"], at("B%d:mix" % (j - 1)))
                    for c in range(NCH):
                        if seq_start:
                            vector.memset(bnd(FALN, c)[:, 0:1], 0.0)
                        else:
                            vector.tensor_copy(bnd(FALN, c)[:, 0:1],
                                               SPOOL[:, ALNC + c:ALNC + c + 1])
                    vector.drain()
                    mu_b = SPOOL[:, AS_MU:AS_MU + TL].unsqueeze(1).broadcast_to([128, NCH, TL])
                    rstd_b = SPOOL[:, AS_RSTD:AS_RSTD + TL].unsqueeze(1).broadcast_to([128, NCH, TL])
                    vector.tensor_tensor(lall[:, :, 1:TLc], axv(j), mu_b, OP.subtract)
                    vector.drain()
                    vector.tensor_tensor(lall[:, :, 1:TLc], lall[:, :, 1:TLc],
                                         rstd_b, OP.mult)
                    vector.drain()
                    for c in range(NCH):
                        vector.tensor_copy(SPOOL[:, ALNC + c:ALNC + c + 1],
                                           bnd(FALN, c)[:, TL:TL + 1])
                    inc(vector, "ve"); snap(P + "ln2", "ve")
                block.vector(_v3)
                def _g3(gpsimd, j=j):
                    gpsimd.wait_ge(sem["ve"], at(P + "ln2"))
                    if j > 0:
                        gpsimd.wait_ge(sem["pe"], at("B%d:kvpL" % (j - 1)))
                    for c in range(NCH):
                        cur = bnd(FALN, c)[:, 1:TLc]
                        prev = bnd(FALN, c)[:, 0:TL]
                        d_ = SPOOL[:, 5384:5384 + TL]
                        tmp = SPOOL[:, 5640:5640 + TL]
                        gpsimd.tensor_tensor(d_, cur, prev, OP.subtract)
                        gpsimd.drain()
                        for vec, bo_ in [(vecs["tmk"], AXK), (vecs["tmv"], AXV),
                                         (vecs["tmr"], AXR)]:
                            vb = vec[:, c:c + 1].broadcast_to([128, TL])
                            gpsimd.tensor_tensor(tmp, d_, vb, OP.mult)
                            gpsimd.drain()
                            gpsimd.tensor_tensor(
                                BPOOL[:, bo_ + c * TL: bo_ + (c + 1) * TL],
                                tmp, prev, OP.add)
                            gpsimd.drain()
                    gpsimd.sem_inc(sem["gp"], 1); bump("gp", 1)
                    snap(P + "mix", "gp")
                block.gpsimd(_g3)

            frontend1(0)
            frontend2(0)
            slot_last = {}

            def kvr_block(j, kinds):
                P = "B%d:" % j
                kb = (2, 3) if j % 2 == 0 else (4, 5)
                banks = [kb[0], kb[1], 6, 7]
                for wi in kinds:
                    bo_ = [AXK, AXV, AXR][wi]
                    for fc in range(NCH):
                        gidx = wi * NCH + fc
                        bank = banks[fc % 4]
                        off = (fc // 4) * 256
                        skey = (bank, off)
                        def _tk(tensor, wi=wi, bo_=bo_, fc=fc, bank=bank, off=off,
                                prev=slot_last.get(skey)):
                            tensor.wait_ge(sem["gp"], at(P + "mix"))
                            if prev is not None:
                                tensor.wait_ge(sem[prev[0]], at(prev[1]))
                            xm = cap(BPOOL, bo_, NCH, TL)
                            for c in range(NCH):
                                tensor.matmul(PS[bank][:, off:off + TL],
                                              WPOOL[:, wi * 8192 + c * D + fc * 128:
                                                    wi * 8192 + c * D + (fc + 1) * 128],
                                              xm[:, c], start=(c == 0), stop=(c == NCH - 1))
                            inc(tensor, "pe"); snap(P + "kvp%d" % gidx, "pe")
                        block.tensor(_tk)
                        fdst = [FAK, FAV, FAR][wi]
                        def _ak(scalar, wi=wi, fc=fc, fdst=fdst, gidx=gidx,
                                bank=bank, off=off, j=j):
                            scalar.wait_ge(sem["pe"], at(P + "kvp%d" % gidx))
                            if gidx == 0 and j > 0:
                                scalar.wait_ge(sem["ve"], at("B%d:y" % (j - 1)))
                            dst = FPOOL[:, fdst + fc * TL: fdst + (fc + 1) * TL]
                            scalar.activation(dst, PS[bank][:, off:off + TL], AF.Copy)
                            inc(scalar, "ac"); snap(P + "kve%d" % gidx, "ac")
                        block.scalar(_ak)
                        slot_last[skey] = ("ac", P + "kve%d" % gidx)
                snap(P + "kvpL", "pe")

            def wo_block(j):
                P = "B%d:" % j
                t0 = j * TL
                kb = (2, 3) if j % 2 == 0 else (4, 5)
                for fc in range(NCH):
                    bank = kb[fc % 2]
                    skey = (bank, "wo")
                    def _to(tensor, fc=fc, bank=bank, prevk=slot_last.get((bank, 0)),
                            prevk2=slot_last.get((bank, 256)), P=P):
                        tensor.wait_ge(sem["ve"], at(P + "y"))
                        if fc >= 2:
                            tensor.wait_ge(sem["ve"], at(P + "x2e%d" % (fc - 2)))
                        if prevk is not None:
                            tensor.wait_ge(sem[prevk[0]], at(prevk[1]))
                        if prevk2 is not None:
                            tensor.wait_ge(sem[prevk2[0]], at(prevk2[1]))
                        ym = cap(BPOOL, ABY, NCH, TL)
                        for c in range(NCH):
                            tensor.matmul(PS[bank][:, 0:TL],
                                          WPOOL[:, 3 * 8192 + c * D + fc * 128:
                                                3 * 8192 + c * D + (fc + 1) * 128],
                                          ym[:, c], start=(c == 0), stop=(c == NCH - 1))
                        inc(tensor, "pe"); snap(P + "yo%d" % fc, "pe")
                    block.tensor(_to)
                    def _vo(vector, fc=fc, j=j, bank=bank, P=P):
                        vector.wait_ge(sem["pe"], at(P + "yo%d" % fc))
                        xv = axv(j)
                        vector.scalar_tensor_tensor(
                            SPOOL[:, ASC2 + fc * TL: ASC2 + (fc + 1) * TL],
                            PS[bank][:, 0:TL], 0.0, xv[:, fc],
                            OP.add, OP.add)
                        inc(vector, "ve"); snap(P + "x2e%d" % fc, "ve")
                    block.vector(_vo)
                    slot_last[(bank, 0)] = ("ve", P + "x2e%d" % fc)
                snap(P + "x2el", "ve")
                def _sst(sync, t0=t0, P=P):
                    sync.wait_ge(sem["ve"], at(P + "x2el"))
                    for c in range(NCH):
                        sync.dma_start(out=dst_dram[c * 128:(c + 1) * 128, t0:t0 + TL],
                                       in_=SPOOL[:, ASC2 + c * TL: ASC2 + (c + 1) * TL]
                                       ).then_inc(sem["qst"], 16); bump("qst", 16)
                    snap(P + "stored", "qst")
                block.sync(_sst)

            for j in range(NBT):
                P = "B%d:" % j
                t0 = j * TL

                if j + 1 < NBT:
                    frontend1(j + 1)

                kvr_block(j, [0])
                if j > 0:
                    wo_block(j - 1)
                kvr_block(j, [1, 2])

                def _asig(scalar, P=P):
                    scalar.wait_ge(sem["ac"], at(P + "kve%d" % (3 * NCH - 1)))
                    scalar.activation(FPOOL[:, FAR:FAR + NCH * TL],
                                      FPOOL[:, FAR:FAR + NCH * TL], AF.Exp, scale=-1.0)
                    scalar.drain()
                    scalar.activation(FPOOL[:, FAR:FAR + NCH * TL],
                                      FPOOL[:, FAR:FAR + NCH * TL], AF.Ln, bias=cb[:, 2:3])
                    scalar.drain()
                    scalar.activation(FPOOL[:, FAR:FAR + NCH * TL],
                                      FPOOL[:, FAR:FAR + NCH * TL], AF.Exp, scale=-1.0)
                    inc(scalar, "ac"); snap(P + "sig", "ac")
                block.scalar(_asig)

                seq_start = (t0 % tseq) == 0
                def _vscan(vector, seq_start=seq_start, j=j, P=P):
                    vector.wait_ge(sem["ac"], at(P + "kve%d" % (2 * NCH - 1)))
                    if j > 0:
                        vector.wait_ge(sem["qst"], at("B%d:stored" % (j - 1)))
                    for c in range(NCH):
                        if seq_start:
                            vector.memset(bnd(FAP, c)[:, 0:1], -1e30)
                            vector.memset(bnd(FAQ, c)[:, 0:1], 0.0)
                        else:
                            vector.tensor_copy(bnd(FAP, c)[:, 0:1],
                                               bnd(FAP, c)[:, TL:TL + 1])
                            vector.tensor_copy(bnd(FAQ, c)[:, 0:1],
                                               bnd(FAQ, c)[:, TL:TL + 1])
                    vector.drain()
                    for c in range(NCH):
                        wb = vecs["w"][:, c:c + 1].broadcast_to([128, TL])
                        vector.tensor_tensor_scan(
                            bnd(FAP, c)[:, 1:TLc], wb,
                            FPOOL[:, FAK + c * TL: FAK + (c + 1) * TL],
                            bnd(FAP, c)[:, 0:1], OP.add, OP.max)
                    vector.drain()
                    for c in range(NCH):
                        vector.scalar_tensor_tensor(
                            mdv[:, c], FPOOL[:, FAK + c * TL: FAK + (c + 1) * TL],
                            vecs["u"][:, c:c + 1], pall[:, c, 0:TL],
                            OP.add, OP.subtract)
                    vector.drain()
                    vector.tensor_tensor(cap(SPOOL, ASC2, NCH, TL),
                                         cap(FPOOL, FAK, NCH, TL),
                                         pall[:, :, 1:TLc], OP.subtract)
                    inc(vector, "ve"); snap(P + "snarg", "ve")
                block.vector(_vscan)

                if j + 1 < NBT:
                    frontend2(j + 1)

                def _asn(scalar, P=P):
                    scalar.wait_ge(sem["ve"], at(P + "snarg"))
                    scalar.activation(SPOOL[:, ASNV:ASNV + NCH * TL],
                                      SPOOL[:, ASC2:ASC2 + NCH * TL], AF.Exp)
                    inc(scalar, "ac"); snap(P + "sn", "ac")
                block.scalar(_asn)
                def _vsnv(vector, P=P):
                    vector.wait_ge(sem["ac"], at(P + "sn"))
                    vector.tensor_tensor(cap(FPOOL, FAK, NCH, TL),
                                         cap(SPOOL, ASNV, NCH, TL),
                                         cap(FPOOL, FAV, NCH, TL), OP.mult)
                    vector.drain()
                    vector.tensor_tensor(cap(SPOOL, ASC2, NCH, TL),
                                         pall[:, :, 0:TL], pall[:, :, 1:TLc],
                                         OP.subtract)
                    inc(vector, "ve"); snap(P + "d1", "ve")
                block.vector(_vsnv)
                def _asp(scalar, P=P):
                    scalar.wait_ge(sem["ve"], at(P + "d1"))
                    for c in range(NCH):
                        scalar.activation(SPOOL[:, ASNV + c * TL: ASNV + (c + 1) * TL],
                                          SPOOL[:, ASC2 + c * TL: ASC2 + (c + 1) * TL],
                                          AF.Exp, bias=vecs["w"][:, c:c + 1])
                    inc(scalar, "ac"); snap(P + "sp", "ac")
                block.scalar(_asp)
                def _vq(vector, P=P):
                    vector.wait_ge(sem["ac"], at(P + "sp"))
                    for c in range(NCH):
                        vector.tensor_tensor_scan(
                            bnd(FAQ, c)[:, 1:TLc],
                            SPOOL[:, ASNV + c * TL: ASNV + (c + 1) * TL],
                            FPOOL[:, FAK + c * TL: FAK + (c + 1) * TL],
                            bnd(FAQ, c)[:, 0:1], OP.mult, OP.add)
                    vector.drain()
                    vector.tensor_scalar(cap(SPOOL, ASC2, NCH, TL), mdv,
                                         -1.0, 0.0, OP.mult, OP.min)
                    inc(vector, "ve"); snap(P + "qs", "ve")
                block.vector(_vq)
                def _aA(scalar, P=P):
                    scalar.wait_ge(sem["ve"], at(P + "qs"))
                    scalar.activation(FPOOL[:, FAK:FAK + NCH * TL],
                                      SPOOL[:, ASC2:ASC2 + NCH * TL], AF.Exp)
                    inc(scalar, "ac"); snap(P + "Araw", "ac")
                block.scalar(_aA)
                def _vA(vector, P=P):
                    vector.wait_ge(sem["ac"], at(P + "Araw"))
                    vector.tensor_scalar(cap(SPOOL, ASC2, NCH, TL),
                                         cap(FPOOL, FAK, NCH, TL),
                                         CL30, None, OP.max)
                    vector.drain()
                    vector.tensor_scalar(cap(FPOOL, FAK, NCH, TL), mdv,
                                         0.0, None, OP.min)
                    inc(vector, "ve"); snap(P + "Barg", "ve")
                block.vector(_vA)
                def _aB(scalar, P=P):
                    scalar.wait_ge(sem["ve"], at(P + "Barg"))
                    scalar.activation(SPOOL[:, ASNV:ASNV + NCH * TL],
                                      FPOOL[:, FAK:FAK + NCH * TL], AF.Exp)
                    inc(scalar, "ac"); snap(P + "Braw", "ac")
                block.scalar(_aB)
                def _vfin(vector, P=P):
                    vector.wait_ge(sem["ac"], at(P + "Braw"))
                    vector.tensor_scalar(cap(FPOOL, FAK, NCH, TL),
                                         cap(SPOOL, ASNV, NCH, TL), CL30, None, OP.max)
                    vector.drain()
                    vector.tensor_tensor(cap(SPOOL, ASNV, NCH, TL),
                                         cap(SPOOL, ASC2, NCH, TL),
                                         cap(FPOOL, FAK, NCH, TL), OP.add)
                    vector.drain()
                    vector.tensor_tensor(mdv, cap(FPOOL, FAK, NCH, TL),
                                         cap(FPOOL, FAV, NCH, TL), OP.mult)
                    vector.drain()
                    vector.tensor_tensor(cap(FPOOL, FAV, NCH, TL),
                                         cap(SPOOL, ASC2, NCH, TL),
                                         qall[:, :, 0:TL], OP.mult)
                    vector.drain()
                    vector.tensor_tensor(cap(SPOOL, ASC2, NCH, TL),
                                         cap(FPOOL, FAV, NCH, TL), mdv, OP.add)
                    inc(vector, "ve"); snap(P + "num", "ve")
                block.vector(_vfin)
                def _aden(scalar, P=P):
                    scalar.wait_ge(sem["ve"], at(P + "num"))
                    scalar.activation(FPOOL[:, FAK:FAK + NCH * TL],
                                      SPOOL[:, ASNV:ASNV + NCH * TL], AF.Ln,
                                      bias=cb[:, 3:4])
                    scalar.drain()
                    scalar.activation(SPOOL[:, ASNV:ASNV + NCH * TL],
                                      FPOOL[:, FAK:FAK + NCH * TL], AF.Exp, scale=-1.0)
                    inc(scalar, "ac"); snap(P + "rden", "ac")
                block.scalar(_aden)
                def _vy(vector, P=P):
                    vector.wait_ge(sem["ac"], at(P + "rden"))
                    vector.wait_ge(sem["ac"], at(P + "sig"))
                    vector.tensor_tensor(cap(FPOOL, FAK, NCH, TL),
                                         cap(SPOOL, ASC2, NCH, TL),
                                         cap(SPOOL, ASNV, NCH, TL), OP.mult)
                    vector.drain()
                    vector.tensor_tensor(cap(BPOOL, ABY, NCH, TL),
                                         cap(FPOOL, FAK, NCH, TL),
                                         cap(FPOOL, FAR, NCH, TL), OP.mult)
                    inc(vector, "ve"); snap(P + "y", "ve")
                block.vector(_vy)
            wo_block(NBT - 1)

        prep()
        ffn_phase(x_in, x1_dram, "A")
        attn_phase(x1_dram, x2_dram)
        def _reload_gate(sync):
            sync.wait_ge(sem["pe"], tot.get("pe", 0))
            sync.wait_ge(sem["ve"], tot.get("ve", 0))
        block.sync(_reload_gate)
        def reprep():
            def wprep2(dname, dst_off, ncols, scale, nrow, tag):
                def _sy(sync):
                    sync.wait_ge(sem["ve"], tot.get("ve", 0))
                    nch = (nrow + 127) // 128
                    for c in range(nch):
                        r0, r1 = c * 128, min((c + 1) * 128, nrow)
                        sync.dma_start(out=FPOOL[0:r1 - r0, c * ncols:(c + 1) * ncols],
                                       in_=dram[dname][r0:r1, :]
                                       ).then_inc(sem["qld"], 16); bump("qld", 16)
                    snap("r_%sld" % tag, "qld")
                block.sync(_sy)
                def _ve(vector):
                    vector.wait_ge(sem["qld"], at("r_%sld" % tag))
                    nch = (nrow + 127) // 128
                    for c in range(nch):
                        r0, r1 = c * 128, min((c + 1) * 128, nrow)
                        dst = WPOOL[0:r1 - r0, dst_off + c * ncols: dst_off + (c + 1) * ncols]
                        src = FPOOL[0:r1 - r0, c * ncols:(c + 1) * ncols]
                        if scale is None:
                            vector.tensor_copy(dst, src)
                        else:
                            vector.tensor_scalar(dst, src, scale[:, c:c + 1], None, OP.mult)
                    inc(vector, "ve"); snap("r_%sc" % tag, "ve")
                block.vector(_ve)
            wprep2("f_Wq", WQ_O, D, vecs["g1"], D, "wq")
            wprep2("f_Wo", WO_O, D, None, D, "wo")
            wprep2("f_Wdown", WD_O, LR, vecs["g1"], D, "wd")
            wprep2("f_Wup", WU_O, 2 * D, None, LR, "wu")
            wprep2("f_Wg", WG_O, NE, vecs["g2"], D, "wg")
        reprep()
        ffn_phase(x2_dram, out_fm, "C")

        def _fin(sync):
            sync.wait_ge(sem["qst"], tot.get("qst", 0))
        block.sync(_fin)

        with nc.Block() as RB:
            RB.sync(lambda e: _replay(e, block.q["sync"]))
            RB.scalar(lambda e: _replay(e, block.q["scalar"]))
            RB.vector(lambda e: _replay(e, block.q["vector"]))
            RB.tensor(lambda e: _replay(e, block.q["tensor"]))
            RB.gpsimd(lambda e: _replay(e, block.q["gpsimd"]))

    return nc


def _vec_layout(v):
    v = np.asarray(v, np.float32)
    n = v.shape[0]
    return np.ascontiguousarray(v.reshape(n // 128, 128).T)


def _make_consts(HT):
    ones = np.ones((128, 128), bfdt)
    id64 = np.concatenate([np.eye(64, dtype=np.float32)] * 2, axis=0).astype(bfdt)
    esel = np.zeros((NE, NE * 128), np.float32)
    for e in range(NE):
        esel[e, e * 128:(e + 1) * 128] = 1.0
    m1 = np.where(np.arange(128)[:, None] % 8 == np.arange(128)[None, :] % 8,
                  np.float32(1.0), np.float32(CL30))
    mask = np.concatenate([m1] * 4, axis=1)
    return ones, id64, esel.astype(bfdt), mask.astype(bfdt)


_NC_CACHE = {}


def kernel(**inputs):
    x = np.asarray(inputs["L0_attn_x"], np.float32)
    B, T, _ = x.shape
    ncore = 8
    nseq = B // ncore
    key = (nseq, T)
    if key not in _NC_CACHE:
        _NC_CACHE[key] = build_nc(nseq, T)
    nc = _NC_CACHE[key]

    HT = min(512, T) // 4
    ones, id64, esel, mask = _make_consts(HT)

    shared = {
        "at_W_k": np.ascontiguousarray(np.asarray(inputs["L0_attn_W_k"], np.float32).T),
        "at_W_v": np.ascontiguousarray(np.asarray(inputs["L0_attn_W_v"], np.float32).T),
        "at_W_r": np.ascontiguousarray(np.asarray(inputs["L0_attn_W_r"], np.float32).T),
        "at_W_o": np.ascontiguousarray(np.asarray(inputs["L0_attn_W_o"], np.float32).T),
        "at_tmk": _vec_layout(inputs["L0_attn_time_mix_k"]),
        "at_tmv": _vec_layout(inputs["L0_attn_time_mix_v"]),
        "at_tmr": _vec_layout(inputs["L0_attn_time_mix_r"]),
        "at_w": _vec_layout(inputs["L0_attn_time_decay"]),
        "at_u": _vec_layout(inputs["L0_attn_time_first"]),
        "f_g1": _vec_layout(inputs["L0_ffn_gamma1"]),
        "f_g2": _vec_layout(inputs["L0_ffn_gamma2"]),
        "f_bq": _vec_layout(inputs["L0_ffn_bq"]),
        "f_bo": _vec_layout(inputs["L0_ffn_bo"]),
        "f_Wq": np.asarray(inputs["L0_ffn_Wq"], np.float32),
        "f_Wdown": np.asarray(inputs["L0_ffn_Wdown"], np.float32),
        "f_Wup": np.asarray(inputs["L0_ffn_Wup"], np.float32),
        "f_Wo": np.asarray(inputs["L0_ffn_Wo"], np.float32),
        "f_Wg": np.asarray(inputs["L0_ffn_Wg"], np.float32),
        "f_We": np.ascontiguousarray(
            np.asarray(inputs["L0_ffn_We"], np.float32).reshape(NE * D, D)),
        "f_bdown": np.asarray(inputs["L0_ffn_bdown"], np.float32).reshape(LR, 1),
        "f_bup": _vec_layout(inputs["L0_ffn_bup"]),
        "f_bg": np.asarray(inputs["L0_ffn_bg"], np.float32).reshape(NE, 1),
        "c_ones": ones, "c_id64": id64, "c_esel": esel, "c_mask": mask,
    }
    in_maps = []
    for i in range(ncore):
        xm = np.ascontiguousarray(
            x[i * nseq:(i + 1) * nseq].reshape(nseq * T, D).T)
        m = dict(shared)
        m["x_fm"] = xm
        in_maps.append(m)

    res = run_bass_kernel_spmd(nc, in_maps, core_ids=list(range(ncore)))
    outs = []
    for i in range(ncore):
        o = np.asarray(res.results[i]["out_fm"])
        outs.append(o.T.reshape(nseq, T, D))
    return np.concatenate(outs, axis=0).astype(np.float32)
